# version 02
# baseline (speedup 1.0000x reference)
import os
import numpy as np
import ml_dtypes
from contextlib import ExitStack
import concourse.bass as bass
import concourse.mybir as mybir
from concourse.bass_utils import run_bass_kernel_spmd

F32 = mybir.dt.float32
BF16 = mybir.dt.bfloat16
I32 = mybir.dt.int32
AF = mybir.ActivationFunctionType
ALU = mybir.AluOpType
AX = mybir.AxisListType

S, D, NB, TB = 8192, 2048, 16, 512
NT = S // 128
NH, NKV, HD = 16, 4, 128
HID = 5632
NHC = HID // 128
ND = D // 128
EPS_LN, EPS_RMS = 1e-5, 1e-6
NSLOT = 24
NHQ = NHC // 4
INV_SQRT_HD = float(1.0 / np.sqrt(128.0))

_PH = int(os.environ.get("KPHASES", "5"))

TBL_KD = 0
TBL_XO = 2


def build_nc():
    nc = bass.Bass("TRN2", target_bir_lowering=False)

    x_full = nc.dram_tensor("x_full", [S, D], F32, kind="ExternalInput").ap()
    x_own = nc.dram_tensor("x_own", [2 * TB, D], F32, kind="ExternalInput").ap()
    Wq_in = nc.dram_tensor("Wq_in", [D, D], F32, kind="ExternalInput").ap()
    Wk_in = nc.dram_tensor("Wk_in", [D, 512], F32, kind="ExternalInput").ap()
    Wv_in = nc.dram_tensor("Wv_in", [D, 512], F32, kind="ExternalInput").ap()
    Wo_in = nc.dram_tensor("Wo_in", [D, D], F32, kind="ExternalInput").ap()
    gq_col = nc.dram_tensor("gq_col", [HD, 1], F32, kind="ExternalInput").ap()
    gk_col = nc.dram_tensor("gk_col", [HD, 1], F32, kind="ExternalInput").ap()
    wg_i = nc.dram_tensor("wg_i", [HID, D], I32, kind="ExternalInput").ap()
    wu_i = nc.dram_tensor("wu_i", [HID, D], I32, kind="ExternalInput").ap()
    wd_i = nc.dram_tensor("wd_i", [D, HID], I32, kind="ExternalInput").ap()
    sg_in = nc.dram_tensor("sg_in", [128, NHC], F32, kind="ExternalInput").ap()
    su_in = nc.dram_tensor("su_in", [128, NHC], F32, kind="ExternalInput").ap()
    sd_in = nc.dram_tensor("sd_in", [128, ND], F32, kind="ExternalInput").ap()
    masks_in = nc.dram_tensor("masks_in", [128, 4 * TB], BF16, kind="ExternalInput").ap()
    negsel_in = nc.dram_tensor("negsel_in", [16, 16 * 128], BF16, kind="ExternalInput").ap()
    identb_in = nc.dram_tensor("identb_in", [128, 128], BF16, kind="ExternalInput").ap()
    usel_in = nc.dram_tensor("usel_in", [33, NH * NSLOT], BF16, kind="ExternalInput").ap()
    tbl_in = nc.dram_tensor("tbl_in", [1, 16], I32, kind="ExternalInput").ap()
    out_own = nc.dram_tensor("out_own", [2 * TB, D], F32, kind="ExternalOutput").ap()

    xn_s = nc.dram_tensor("xn_s", [S, D], BF16)
    xn_own = nc.dram_tensor("xn_own", [2 * TB, D], BF16)
    q_s = nc.dram_tensor("q_s", [2 * TB, D], BF16)
    k_s = nc.dram_tensor("k_s", [S, NKV * HD], BF16)
    v_row_s = nc.dram_tensor("v_row_s", [S, NKV * HD], BF16)
    ku_s = nc.dram_tensor("ku_s", [NSLOT * TB, NKV * HD], BF16)
    vu_s = nc.dram_tensor("vu_s", [NSLOT * TB, NKV * HD], BF16)
    wg_s = nc.dram_tensor("wg_s", [HID, D], BF16)
    wu_s = nc.dram_tensor("wu_s", [HID, D], BF16)
    wd_s = nc.dram_tensor("wd_s", [D, HID], BF16)

    with ExitStack() as top:
        Et = top.enter_context
        TT = Et(nc.semaphore("tt"))
        AA = Et(nc.semaphore("aa"))
        VV = Et(nc.semaphore("vv"))
        GV = Et(nc.semaphore("gv"))
        GXm = Et(nc.semaphore("gxm"))
        GKm = Et(nc.semaphore("gkm"))
        LA = Et(nc.semaphore("la"))
        LXs = [Et(nc.semaphore(f"lx{i}")) for i in range(2)]
        LC = Et(nc.semaphore("lc"))
        LW = Et(nc.semaphore("lw"))
        SW = Et(nc.semaphore("sw"))
        SX = Et(nc.semaphore("sx"))
        LT = [Et(nc.semaphore(f"lt{i}")) for i in range(2)]
        SKs = Et(nc.semaphore("sk"))
        GXo = Et(nc.semaphore("gxo"))
        LE = Et(nc.semaphore("le"))
        LQ = Et(nc.semaphore("lq"))
        SQ = [Et(nc.semaphore(f"sq{i}")) for i in range(2)]
        LU = Et(nc.semaphore("lu"))
        GU = Et(nc.semaphore("gu2"))
        LQd = Et(nc.semaphore("lqd"))
        LKd = Et(nc.semaphore("lkd"))
        LG = [Et(nc.semaphore(f"lg{i}")) for i in range(2)]
        LX2 = Et(nc.semaphore("lx2"))
        LS = [Et(nc.semaphore(f"ls{i}")) for i in range(2)]
        LDt = [Et(nc.semaphore(f"ldt{i}")) for i in range(2)]
        LXO = [Et(nc.semaphore(f"lxo{i}")) for i in range(2)]
        GO = [Et(nc.semaphore(f"go{i}")) for i in range(2)]
        GS = [Et(nc.semaphore(f"gs{i}")) for i in range(2)]
        PV = Et(nc.semaphore("pv"))
        PA = Et(nc.semaphore("pa"))
        pv_n = [0]
        pa_n = [0]

        def vping(eng, inst):
            pv_n[0] += 1
            inst.then_inc(PV, 1)
            eng.wait_ge(PV, pv_n[0])

        def aping(eng, inst):
            pa_n[0] += 1
            inst.then_inc(PA, 1)
            eng.wait_ge(PA, pa_n[0])

        s_gq = Et(nc.sbuf_tensor("s_gq", [HD, 1], F32))
        s_gk = Et(nc.sbuf_tensor("s_gk", [HD, 1], F32))
        s_G = Et(nc.sbuf_tensor("s_G", [HD, 1], F32))
        s_sg = Et(nc.sbuf_tensor("s_sg", [128, NHC], F32))
        s_su = Et(nc.sbuf_tensor("s_su", [128, NHC], F32))
        s_sd = Et(nc.sbuf_tensor("s_sd", [128, ND], F32))
        s_tbl = Et(nc.sbuf_tensor("s_tbl", [1, 16], I32))
        s_onesb = Et(nc.sbuf_tensor("s_onesb", [128, 1], BF16))
        s_onesrf = Et(nc.sbuf_tensor("s_onesrf", [1, 128], F32))
        s_epsln = Et(nc.sbuf_tensor("s_epsln", [128, 1], F32))
        s_epsrms = Et(nc.sbuf_tensor("s_epsrms", [128, 1], F32))
        s_attnT = Et(nc.sbuf_tensor("s_attnT", [128, NH * 1024], BF16))
        af2 = ExitStack()
        Ea = af2.enter_context
        s_masks = Ea(nc.sbuf_tensor("s_masks", [128, 4 * TB], BF16))
        s_negsel = Ea(nc.sbuf_tensor("s_negsel", [16, 16 * 128], BF16))
        s_identb = Ea(nc.sbuf_tensor("s_identb", [128, 128], BF16))
        s_usel = Ea(nc.sbuf_tensor("s_usel", [33, NH * NSLOT], BF16))

        with nc.Block() as blk:
            @blk.sync
            def _(sy):
                for dst, srcp in [
                    (s_masks[:, :], masks_in), (s_negsel[:, :], negsel_in),
                    (s_identb[:, :], identb_in), (s_usel[:, :], usel_in),
                    (s_gq[:, :], gq_col), (s_gk[:, :], gk_col),
                    (s_sg[:, :], sg_in), (s_su[:, :], su_in), (s_sd[:, :], sd_in),
                    (s_tbl[:, :], tbl_in),
                ]:
                    sy.dma_start(dst, srcp).then_inc(LA, 16)

            @blk.gpsimd
            def _(g):
                g.memset(s_onesb[:, :], 1.0)
                g.memset(s_onesrf[:, :], 1.0)
                g.memset(s_epsln[:, :], EPS_LN)
                g.memset(s_epsrms[:, :], EPS_RMS)
                g.nop().then_inc(GV, 1)

            @blk.vector
            def _(v):
                v.wait_ge(LA, 160)
                vping(v, v.tensor_mul(s_G[:, :], s_gq[:, :], s_gk[:, :]))
                v.tensor_scalar_mul(s_G[:, :], s_G[:, :], INV_SQRT_HD).then_inc(VV, 1)
        LA_T = 160
        vv0 = 1
        gv0 = 1

        with ExitStack() as bd_scope:
            Eb = bd_scope.enter_context
            s_Wk = [Eb(nc.sbuf_tensor(f"s_Wk{d}", [128, 512], BF16)) for d in range(ND)]
            s_Wv = [Eb(nc.sbuf_tensor(f"s_Wv{d}", [128, 512], BF16)) for d in range(ND)]

            c_tiles = []
            for hc in range(NHC):
                c_tiles.append((wg_i[hc * 128:(hc + 1) * 128, :], s_sg[:, hc:hc + 1],
                                wg_s.ap()[hc * 128:(hc + 1) * 128, :], D))
            for hc in range(NHC):
                c_tiles.append((wu_i[hc * 128:(hc + 1) * 128, :], s_su[:, hc:hc + 1],
                                wu_s.ap()[hc * 128:(hc + 1) * 128, :], D))
            for dc in range(ND):
                c_tiles.append((wd_i[dc * 128:(dc + 1) * 128, :], s_sd[:, dc:dc + 1],
                                wd_s.ap()[dc * 128:(dc + 1) * 128, :], HID))
            NC_T = len(c_tiles)

            nci = []
            k = 0
            for i in range(NT):
                take = min(2, NC_T - k)
                nci.append(take)
                k += take
            csum = [0]
            for i in range(NT):
                csum.append(csum[-1] + nci[i])
            assert csum[-1] == NC_T

            vvB, gvB = vv0, gv0

            bdl = ExitStack()
            Ep = bdl.enter_context
            s_x = [Ep(nc.sbuf_tensor(f"s_x{k}", [128, D], F32)) for k in range(2)]
            s_xn = [Ep(nc.sbuf_tensor(f"s_xn{k}", [128, D], BF16)) for k in range(2)]
            s_sq = Ep(nc.sbuf_tensor("s_sq", [128, D], BF16))
            s_ssq = [Ep(nc.sbuf_tensor(f"s_ssq{k}", [128, 1], F32)) for k in range(2)]
            s_rsum = Ep(nc.sbuf_tensor("s_rsum", [128, 1], F32))
            s_negmean = [Ep(nc.sbuf_tensor(f"s_nm{k}", [128, 1], F32)) for k in range(2)]
            s_m2 = Ep(nc.sbuf_tensor("s_m2", [128, 1], F32))
            s_var = [Ep(nc.sbuf_tensor(f"s_var{k}", [128, 1], F32)) for k in range(2)]
            s_tmp1 = Ep(nc.sbuf_tensor("s_tmp1", [128, 1], F32))
            s_rstd = [Ep(nc.sbuf_tensor(f"s_rstd{k}", [128, 1], F32)) for k in range(2)]
            s_wint = Ep(nc.sbuf_tensor("s_wint0", [128, HID], I32))
            s_wbf = [Ep(nc.sbuf_tensor(f"s_wbf{k}", [128, HID], BF16)) for k in range(2)]
            s_xnT = [[Ep(nc.sbuf_tensor(f"s_xnT{p_}_{d}", [128, TB], BF16))
                      for d in range(ND)] for p_ in range(2)]
            s_kv = [Ep(nc.sbuf_tensor(f"s_kv{k}", [128, 1024], BF16)) for k in range(2)]
            s_ksq = [Ep(nc.sbuf_tensor(f"s_ksq{k}", [128, 512], F32)) for k in range(2)]
            s_kssq = Ep(nc.sbuf_tensor("s_kssq", [128, NKV], F32))
            s_krstd = Ep(nc.sbuf_tensor("s_krstd", [128, NKV], F32))

            with nc.Block() as blk:
                @blk.sync
                def _(sy):
                    sy.wait_ge(LA, LA_T)
                    for i in range(NT):
                        if i >= 2:
                            sy.wait_ge(GXm, i - 1)
                        sy.dma_start(s_x[i % 2][:, :],
                                     x_full[i * 128:(i + 1) * 128, :]).then_inc(LXs[i % 2], 16)
                        for j in range(nci[i]):
                            ci = csum[i] + j
                            if ci >= 1:
                                sy.wait_ge(SW, 16 * ci)
                            srcp, scol, dst, width = c_tiles[ci]
                            sy.dma_start(s_wint[:, 0:width], srcp).then_inc(LC, 16)

                @blk.scalar
                def _(sc):
                    sc.wait_ge(GV, gv0)
                    for i in range(NT):
                        sc.wait_ge(LXs[i % 2], 16 * (i // 2 + 1))
                        sc.activation(s_sq[:, :], s_x[i % 2][:, :], AF.Square,
                                      accum_out=s_ssq[i % 2][:, :1]).then_inc(AA, 1)
                        sc.wait_ge(VV, vvB + 2 * i + 1)
                        aping(sc, sc.activation(s_tmp1[:, :1], s_var[i % 2][:, :1], AF.Ln,
                                                bias=s_epsln[:, :1]))
                        sc.activation(s_rstd[i % 2][:, :1], s_tmp1[:, :1], AF.Exp,
                                      scale=-0.5).then_inc(AA, 1)

                @blk.vector
                def _(v):
                    for i in range(NT):
                        v.wait_ge(LXs[i % 2], 16 * (i // 2 + 1))
                        vping(v, v.reduce_sum(s_rsum[:, :1], s_x[i % 2][:, :], AX.X))
                        vping(v, v.tensor_scalar_mul(s_negmean[i % 2][:, :1],
                                                     s_rsum[:, :1], -1.0 / D))
                        v.wait_ge(AA, 2 * i + 1)
                        vping(v, v.tensor_mul(s_m2[:, :1], s_negmean[i % 2][:, :1],
                                              s_negmean[i % 2][:, :1]))
                        v.scalar_tensor_tensor(s_var[i % 2][:, :1], s_ssq[i % 2][:, :1],
                                               1.0 / D, s_m2[:, :1],
                                               ALU.mult, ALU.subtract).then_inc(VV, 1)
                        v.wait_ge(AA, 2 * i + 2)
                        if i >= 2:
                            v.wait_ge(GXm, i - 1)
                        v.tensor_scalar(s_xn[i % 2][:, :], s_x[i % 2][:, :],
                                        s_negmean[i % 2][:, :1], s_rstd[i % 2][:, :1],
                                        ALU.add, ALU.mult).then_inc(VV, 1)

                @blk.gpsimd
                def _(g):
                    g.wait_ge(LA, LA_T)
                    for d in range(ND):
                        g.dma_start(s_Wk[d][:, :], Wk_in[d * 128:(d + 1) * 128, :]).then_inc(LW, 16)
                    for d in range(ND):
                        g.dma_start(s_Wv[d][:, :], Wv_in[d * 128:(d + 1) * 128, :]).then_inc(LW, 16)
                    for i in range(NT):
                        for j in range(nci[i]):
                            ci = csum[i] + j
                            srcp, scol, dst, width = c_tiles[ci]
                            g.wait_ge(LC, 16 * (ci + 1))
                            if ci >= 2:
                                g.wait_ge(SW, 16 * ci)
                            g.tensor_scalar(s_wbf[ci % 2][:, 0:width], s_wint[:, 0:width],
                                            128.0, scol, ALU.subtract, ALU.mult).then_inc(GV, 1)
                            g.dma_start(dst, s_wbf[ci % 2][:, 0:width]).then_inc(SW, 16)
                        g.wait_ge(VV, vvB + 2 * i + 2)
                        g.dma_start(xn_s.ap()[i * 128:(i + 1) * 128, :],
                                    s_xn[i % 2][:, :]).then_inc(SX, 16)
                        g.wait_ge(SX, 16 * (i + 1))
                        g.nop().then_inc(GXm, 1)

            vv_B = vvB + 2 * NT
            aa_B = 2 * NT
            gv_B = gvB + NC_T
            SW_T = 16 * NC_T
            SX_T = 16 * NT
            GXm_T = NT
            LW_T = 16 * 32
            SK_T = 0
            GKm_T = 0
            GXo_T = 0
            SQ_T = [0, 0]
            tt_E, aa_E, vv_E = 0, aa_B, vv_B

            if _PH >= 2:
                with ExitStack() as de_scope:
                    ttD, aaD, vvD = 0, aa_B, vv_B
                    NTD = int(os.environ.get("KNTD", str(NT)))
                    NBD = NTD // 4
                    KXO = os.environ.get("KXO", "1") != "0"
                    KDS = int(os.environ.get("KDS", "4"))
                    KDN = int(os.environ.get("KDN", "0"))
                    with ExitStack() as ph, nc.Block() as blk:
                        Ep2 = ph.enter_context
                        ps_kv = [Ep2(nc.psum_tensor(f"ps_kv{k}", [128, 1024], F32))
                                 for k in range(2)]

                        @blk.gpsimd
                        def _(g):
                            for idx in (range(NTD) if KDS >= 4 else []):
                                g.wait_ge(VV, vvD + 2 * idx + 2)
                                g.dma_start(k_s.ap()[idx * 128:(idx + 1) * 128, :],
                                            s_kv[idx % 2][:, 0:512]).then_inc(SKs, 16)
                                g.dma_start(v_row_s.ap()[idx * 128:(idx + 1) * 128, :],
                                            s_kv[idx % 2][:, 512:1024]).then_inc(SKs, 16)
                                g.wait_ge(SKs, 16 * 2 * (idx + 1))
                                g.nop().then_inc(GKm, 1)
                            g.wait_ge(GXm, GXm_T)
                            rxo = g.alloc_register("rxo")
                            for ch in (range(2) if KXO else []):
                                g.reg_load(rxo, s_tbl[:1, TBL_XO + ch:TBL_XO + ch + 1])
                                g.dma_start(
                                    bass.AP(xn_own, ch * TB * D, [[1, TB * D]]),
                                    bass.AP(xn_s, rxo, [[1, TB * D]])).then_inc(GXo, 16)

                        @blk.sync
                        def _(sy):
                            for c in range(NBD):
                                sy.wait_ge(GXm, 4 * c + 4)
                                if c >= 2 and KDS >= 2:
                                    sy.wait_ge(TT, ttD + 4 * (c - 1))
                                for d in range(ND):
                                    sy.dma_start_transpose(
                                        s_xnT[c % 2][d][:, :],
                                        xn_s.ap()[c * TB:(c + 1) * TB, d * 128:(d + 1) * 128]
                                    ).then_inc(LT[c % 2], 16)

                        @blk.tensor
                        def _(t):
                            if KDS >= 2:
                                t.wait_ge(LW, LW_T)
                            for c in (range(NBD) if KDS >= 2 else []):
                                t.wait_ge(LT[c % 2], 16 * ND * (c // 2 + 1))
                                for tt in range(4):
                                    idx = 4 * c + tt
                                    if idx >= 2 and KDS >= 4:
                                        t.wait_ge(GKm, idx - 1)
                                    for d in range(ND):
                                        lhs = s_xnT[c % 2][d][:, tt * 128:(tt + 1) * 128]
                                        t.matmul(ps_kv[idx % 2][:, 0:512], lhs, s_Wk[d][:, :],
                                                 start=(d == 0), stop=(d == ND - 1))
                                        mmv = t.matmul(ps_kv[idx % 2][:, 512:1024], lhs,
                                                       s_Wv[d][:, :],
                                                       start=(d == 0), stop=(d == ND - 1))
                                    mmv.then_inc(TT, 1)

                        @blk.scalar
                        def _(sc):
                            for idx in (range(NTD) if KDS >= 3 else []):
                                sc.wait_ge(TT, ttD + idx + 1)
                                if idx >= 2:
                                    sc.wait_ge(VV, vvD + 2 * (idx - 2) + 1)
                                sc.activation(s_ksq[idx % 2][:, :], ps_kv[idx % 2][:, 0:512],
                                              AF.Square).then_inc(AA, 1)
                                sc.wait_ge(VV, vvD + 2 * idx + 1)
                                if KDN & 4:
                                    aping(sc, sc.activation(s_krstd[:, 0:NKV],
                                                            s_kssq[:, 0:NKV], AF.Square))
                                    sc.activation(s_krstd[:, 0:NKV], s_krstd[:, 0:NKV],
                                                  AF.Square).then_inc(AA, 1)
                                else:
                                    aping(sc, sc.activation(s_krstd[:, 0:NKV], s_kssq[:, 0:NKV],
                                                            AF.Ln, scale=1.0 / HD,
                                                            bias=s_epsrms[:, :1]))
                                    sc.activation(s_krstd[:, 0:NKV], s_krstd[:, 0:NKV], AF.Exp,
                                                  scale=-0.5).then_inc(AA, 1)

                        @blk.vector
                        def _(v):
                            for idx in (range(NTD) if KDS >= 3 else []):
                                v.wait_ge(AA, aaD + 2 * idx + 1)
                                if KDN & 2:
                                    v.reduce_sum(s_kssq[:, 0:1], s_ksq[idx % 2][:, :],
                                                 AX.X).then_inc(VV, 1)
                                else:
                                    v.tensor_reduce(
                                        s_kssq[:, 0:NKV],
                                        s_ksq[idx % 2][:, :].rearrange("p (g n) -> p g n",
                                                                       g=NKV),
                                        AX.X, ALU.add).then_inc(VV, 1)
                                v.wait_ge(AA, aaD + 2 * idx + 2)
                                if idx >= 2 and KDS >= 4:
                                    v.wait_ge(GKm, idx - 1)
                                if KDN & 1:
                                    v.tensor_copy(s_kv[idx % 2][:, 0:512],
                                                  ps_kv[idx % 2][:, 0:512])
                                else:
                                    v.tensor_tensor(
                                        s_kv[idx % 2][:, 0:512].rearrange("p (g n) -> p g n",
                                                                          g=NKV),
                                        ps_kv[idx % 2][:, 0:512].rearrange("p (g n) -> p g n",
                                                                           g=NKV),
                                        s_krstd[:, 0:NKV].unsqueeze(2).broadcast_to(
                                            [128, NKV, 128]),
                                        ALU.mult)
                                v.tensor_copy(s_kv[idx % 2][:, 512:1024],
                                              ps_kv[idx % 2][:, 512:1024]).then_inc(VV, 1)

                    tt_D = ttD + (NTD if KDS >= 2 else 0)
                    aa_D = aaD + (2 * NTD if KDS >= 3 else 0)
                    vv_D = vvD + (2 * NTD if KDS >= 3 else 0)
                    SK_T = 16 * 2 * NTD if KDS >= 4 else 0
                    GKm_T = NTD if KDS >= 4 else 0
                    GXo_T = 16 * 2 if KXO else 0
                    bdl.close()
                    tt_E, aa_E, vv_E = tt_D, aa_D, vv_D

                    if _PH >= 3:
                        ttE, aaE, vvE = tt_D, aa_D, vv_D
                        with ExitStack() as ph, nc.Block() as blk:
                            Ep3 = ph.enter_context
                            s_Wq = [Ep3(nc.sbuf_tensor(f"s_Wq{d}", [128, D], BF16))
                                    for d in range(ND)]
                            s_xnTe = [Ep3(nc.sbuf_tensor(f"s_xnTe{d}", [128, TB], BF16))
                                      for d in range(ND)]
                            s_q = [Ep3(nc.sbuf_tensor(f"s_q{k}", [128, D], BF16))
                                   for k in range(2)]
                            s_qsq = Ep3(nc.sbuf_tensor("s_qsq", [128, D], F32))
                            s_qssq = Ep3(nc.sbuf_tensor("s_qssq", [128, NH], F32))
                            s_qrstd = Ep3(nc.sbuf_tensor("s_qrstd", [128, NH], F32))
                            ps_q = Ep3(nc.psum_tensor("ps_q", [128, D], F32))

                            @blk.sync
                            def _(sy):
                                sy.wait_ge(GXo, GXo_T)
                                for ch in range(2):
                                    if ch == 1:
                                        sy.wait_ge(TT, ttE + 4)
                                    for d in range(ND):
                                        sy.dma_start_transpose(
                                            s_xnTe[d][:, :],
                                            xn_own.ap()[ch * TB:(ch + 1) * TB,
                                                        d * 128:(d + 1) * 128]
                                        ).then_inc(LE, 16)

                            @blk.tensor
                            def _(t):
                                t.wait_ge(LQ, 16 * ND)
                                for ch in range(2):
                                    t.wait_ge(LE, 16 * ND * (ch + 1))
                                    for tt in range(4):
                                        tile = 4 * ch + tt
                                        if tile >= 1:
                                            t.wait_ge(SQ[(tile - 1) % 2],
                                                      16 * ((tile - 1) // 2 + 1))
                                        for d in range(ND):
                                            lhs = s_xnTe[d][:, tt * 128:(tt + 1) * 128]
                                            for qc in range(4):
                                                mmq = t.matmul(
                                                    ps_q[:, qc * 512:(qc + 1) * 512],
                                                    lhs,
                                                    s_Wq[d][:, qc * 512:(qc + 1) * 512],
                                                    start=(d == 0), stop=(d == ND - 1))
                                        mmq.then_inc(TT, 1)

                            @blk.scalar
                            def _(sc):
                                for tile in range(8):
                                    sc.wait_ge(TT, ttE + tile + 1)
                                    sc.activation(s_qsq[:, :], ps_q[:, :],
                                                  AF.Square).then_inc(AA, 1)
                                    sc.wait_ge(VV, vvE + 2 * tile + 1)
                                    aping(sc, sc.activation(s_qrstd[:, 0:NH], s_qssq[:, 0:NH],
                                                            AF.Ln, scale=1.0 / HD,
                                                            bias=s_epsrms[:, :1]))
                                    sc.activation(s_qrstd[:, 0:NH], s_qrstd[:, 0:NH], AF.Exp,
                                                  scale=-0.5).then_inc(AA, 1)

                            @blk.vector
                            def _(v):
                                for tile in range(8):
                                    v.wait_ge(AA, aaE + 2 * tile + 1)
                                    v.tensor_reduce(
                                        s_qssq[:, 0:NH],
                                        s_qsq[:, :].rearrange("p (g n) -> p g n", g=NH),
                                        AX.X, ALU.add).then_inc(VV, 1)
                                    v.wait_ge(AA, aaE + 2 * tile + 2)
                                    if tile >= 2:
                                        v.wait_ge(SQ[tile % 2], 16 * (tile // 2))
                                    v.tensor_tensor(
                                        s_q[tile % 2][:, :].rearrange("p (g n) -> p g n",
                                                                      g=NH),
                                        ps_q[:, :].rearrange("p (g n) -> p g n", g=NH),
                                        s_qrstd[:, 0:NH].unsqueeze(2).broadcast_to(
                                            [128, NH, 128]),
                                        ALU.mult).then_inc(VV, 1)

                            @blk.gpsimd
                            def _(g):
                                for d in range(ND):
                                    g.dma_start(s_Wq[d][:, :],
                                                Wq_in[d * 128:(d + 1) * 128, :]
                                                ).then_inc(LQ, 16)
                                for tile in range(8):
                                    g.wait_ge(VV, vvE + 2 * tile + 2)
                                    g.dma_start(q_s.ap()[tile * 128:(tile + 1) * 128, :],
                                                s_q[tile % 2][:, :]).then_inc(SQ[tile % 2], 16)

                        tt_E = ttE + 8
                        aa_E = aaE + 16
                        vv_E = vvE + 16
                        SQ_T = [16 * 4, 16 * 4]
            else:
                bdl.close()

        vv_F_end = vv_E
        tt_F_end = tt_E
        aa_F_end = aa_E
        if _PH >= 4:
            with ExitStack() as fs, nc.Block() as blk:
                Ef = fs.enter_context
                s_qTd = Ef(nc.sbuf_tensor("s_qTd", [128, NH * 1024], BF16))
                s_Kg = [Ef(nc.sbuf_tensor(f"s_Kg{p_}", [128, NSLOT * TB], BF16))
                        for p_ in range(2)]
                s_Vg = [Ef(nc.sbuf_tensor(f"s_Vg{p_}", [128, NSLOT * TB], BF16))
                        for p_ in range(2)]
                s_kdiag = Ef(nc.sbuf_tensor("s_kdiag", [128, 2 * NKV * TB], BF16))
                s_PT = [Ef(nc.sbuf_tensor(f"s_PT{j}", [128, TB], BF16)) for j in range(4)]
                s_PTm = [Ef(nc.sbuf_tensor(f"s_PTm{j}", [128, TB], BF16)) for j in range(4)]
                s_recip = Ef(nc.sbuf_tensor("s_recip", [1, 1024], F32))
                s_recipb = Ef(nc.sbuf_tensor("s_recipb", [128, 1024], F32))
                s_mbpad = Ef(nc.sbuf_tensor("s_mbpad", [128, 128], BF16))
                s_nbpad = Ef(nc.sbuf_tensor("s_nbpad", [128, 128], BF16))
                s_mcol = Ef(nc.sbuf_tensor("s_mcol", [128, 1], F32))
                s_rmax = Ef(nc.sbuf_tensor("s_rmax", [128, 1], F32))
                s_bmax = Ef(nc.sbuf_tensor("s_bmax", [16, 1], BF16))
                s_nbT = Ef(nc.sbuf_tensor("s_nbT", [33, 128], BF16))
                s_nbu = Ef(nc.sbuf_tensor("s_nbu", [128, NH * NSLOT], F32))
                ps_scores = Ef(nc.psum_tensor("ps_scores", [128, 1024], F32))
                ps_attn = Ef(nc.psum_tensor("ps_attn", [128, 1024], F32))
                ps_sums = Ef(nc.psum_tensor("ps_sums", [1, 1024], F32))
                ps_trb = Ef(nc.psum_tensor("ps_trb", [128, 128], BF16))
                ps_misc = Ef(nc.psum_tensor("ps_misc", [128, 512], F32))

                diag_lin = [(J, h, sq) for J in range(2) for h in range(NH)
                            for sq in range(4)]
                vv = vv_E
                vv += 1; vv_kd = vv
                vv_dmax = {}
                vv_mcol = {}
                for J in range(2):
                    for h in range(NH):
                        for sq in range(4):
                            vv += 1; vv_dmax[(J, h, sq)] = vv
                        vv += 1; vv_mcol[(J, h)] = vv
                vv += 1; vv_bmax0 = vv
                vv += 1; vv_bmax1 = vv
                vv += 1; vv_nbevac = vv
                vv += 1; vv_nbT = vv
                vv += 1; vv_nbu = vv
                vv += 1; vv_kg = {0: vv}
                vv += 1; vv_kg[1] = vv
                ghs = [(g, hh) for g in range(NKV) for hh in range(4)]
                vv_mask = {}
                vv_recip = {}
                vv_epi = {}
                for gi, (g, hh) in enumerate(ghs):
                    for p_ in (0, 8):
                        for j in range(4):
                            vv += 1; vv_mask[(g, hh, p_, j)] = vv
                    vv += 1; vv_recip[(g, hh)] = vv
                    vv += 1; vv_epi[(g, hh)] = vv
                    if hh == 0 and 2 <= g + 1 <= 3:
                        vv += 1; vv_kg[g + 1] = vv
                vv_F_end = vv

                tt = tt_E
                tt_diag = {}
                for J in range(2):
                    for h in range(NH):
                        for sq in range(4):
                            tt += 1; tt_diag[(J, h, sq)] = tt
                tt += 1; tt_tr1 = tt
                tt += 1; tt_sel0 = tt
                tt += 1; tt_sel1 = tt
                tt += 1; tt_tr2 = tt
                tt += 1; tt_usel = tt
                tt_score = {}
                tt_avdone = {}
                tt_repl = {}
                for g, hh in ghs:
                    for p_ in range(NSLOT):
                        for j in range(4):
                            tt += 1; tt_score[(g, hh, p_, j)] = tt
                        tt += 1; tt_avdone[(g, hh, p_)] = tt
                    tt += 1; tt_repl[(g, hh)] = tt
                tt_F_end = tt

                aa = aa_E
                aa_exp = {}
                for g, hh in ghs:
                    for p_ in range(NSLOT):
                        for j in range(4):
                            aa += 1; aa_exp[(g, hh, p_, j)] = aa
                aa_F_end = aa

                @blk.sync
                def _(sy):
                    sy.wait_ge(GKm, GKm_T)
                    rkd = sy.alloc_register("rkd")
                    sy.reg_load(rkd, s_tbl[:1, TBL_KD + 0:TBL_KD + 1])
                    sy.dma_start(bass.AP(ku_s, 0, [[1, TB * 512]]),
                                 bass.AP(k_s, rkd, [[1, TB * 512]])).then_inc(LU, 16)
                    sy.dma_start(bass.AP(ku_s, TB * 512, [[1, 7 * TB * 512]]),
                                 bass.AP(k_s, 0, [[1, 7 * TB * 512]])).then_inc(LU, 16)
                    sy.reg_load(rkd, s_tbl[:1, TBL_KD + 1:TBL_KD + 2])
                    sy.dma_start(bass.AP(ku_s, 8 * TB * 512, [[1, TB * 512]]),
                                 bass.AP(k_s, rkd, [[1, TB * 512]])).then_inc(LU, 16)
                    sy.dma_start(bass.AP(ku_s, 9 * TB * 512, [[1, 15 * TB * 512]]),
                                 bass.AP(k_s, 0, [[1, 15 * TB * 512]])).then_inc(LU, 16)
                    sy.wait_ge(SQ[0], SQ_T[0])
                    sy.wait_ge(SQ[1], SQ_T[1])
                    for h in range(NH):
                        for J in range(2):
                            sy.dma_start_transpose(
                                s_qTd[:, (h * 2 + J) * TB:(h * 2 + J + 1) * TB],
                                q_s.ap()[J * TB:(J + 1) * TB, h * 128:(h + 1) * 128]
                            ).then_inc(LQd, 16)
                    sy.wait_ge(LU, 16 * 4)
                    for J in range(2):
                        for g in range(NKV):
                            sy.dma_start_transpose(
                                s_kdiag[:, (J * NKV + g) * TB:(J * NKV + g + 1) * TB],
                                ku_s.ap()[J * 8 * TB:J * 8 * TB + TB, g * 128:(g + 1) * 128]
                            ).then_inc(LKd, 16)
                    sy.wait_ge(GU, 16 * 4)
                    for g in range(NKV):
                        if g >= 2:
                            sy.wait_ge(TT, tt_avdone[(g - 2, 3, NSLOT - 1)])
                        sy.dma_start_transpose(
                            s_Kg[g % 2][:, :],
                            ku_s.ap()[:, g * 128:(g + 1) * 128]).then_inc(LG[g % 2], 16)
                        for p_ in range(NSLOT):
                            srcp = vu_s.ap()[p_ * TB:(p_ + 1) * TB,
                                             g * 128:(g + 1) * 128].rearrange(
                                "(j p) q -> p j q", p=128)
                            dst = s_Vg[g % 2][:, p_ * TB:(p_ + 1) * TB].rearrange(
                                "p (j q) -> p j q", q=128)
                            sy.dma_start(dst, srcp).then_inc(LG[g % 2], 16)

                @blk.gpsimd
                def _(g_):
                    g_.wait_ge(GKm, GKm_T)
                    rvd = g_.alloc_register("rvd")
                    g_.reg_load(rvd, s_tbl[:1, TBL_KD + 0:TBL_KD + 1])
                    g_.dma_start(bass.AP(vu_s, 0, [[1, TB * 512]]),
                                 bass.AP(v_row_s, rvd, [[1, TB * 512]])).then_inc(GU, 16)
                    g_.dma_start(bass.AP(vu_s, TB * 512, [[1, 7 * TB * 512]]),
                                 bass.AP(v_row_s, 0, [[1, 7 * TB * 512]])).then_inc(GU, 16)
                    g_.reg_load(rvd, s_tbl[:1, TBL_KD + 1:TBL_KD + 2])
                    g_.dma_start(bass.AP(vu_s, 8 * TB * 512, [[1, TB * 512]]),
                                 bass.AP(v_row_s, rvd, [[1, TB * 512]])).then_inc(GU, 16)
                    g_.dma_start(bass.AP(vu_s, 9 * TB * 512, [[1, 15 * TB * 512]]),
                                 bass.AP(v_row_s, 0, [[1, 15 * TB * 512]])).then_inc(GU, 16)

                @blk.tensor
                def _(t):
                    t.wait_ge(GV, gv0)
                    t.wait_ge(LQd, 16 * 32)
                    t.wait_ge(LKd, 16 * 8)
                    t.wait_ge(VV, vv_kd)
                    n = 0
                    for J in range(2):
                        for h in range(NH):
                            g = h // 4
                            for sq in range(4):
                                par = n % 2
                                if n >= 2:
                                    t.wait_ge(VV, vv_dmax[diag_lin[n - 2]])
                                lhs = s_qTd[:, (h * 2 + J) * TB + sq * 128:
                                            (h * 2 + J) * TB + (sq + 1) * 128]
                                rhs = s_kdiag[:, (J * NKV + g) * TB:(J * NKV + g + 1) * TB]
                                t.matmul(ps_scores[:, par * 512:(par + 1) * 512], lhs, rhs,
                                         start=True, stop=True).then_inc(TT, 1)
                                n += 1
                    t.wait_ge(VV, vv_mcol[(1, NH - 1)])
                    t.transpose(ps_trb[:, :], s_mbpad[:, :],
                                s_identb[:, :]).then_inc(TT, 1)
                    t.wait_ge(VV, vv_bmax0)
                    for h in range(NH):
                        mms = t.matmul(ps_misc[:, h:h + 1],
                                       s_negsel[:, h * 128:(h + 1) * 128],
                                       s_bmax[:, :1], start=True, stop=True)
                    mms.then_inc(TT, 1)
                    t.wait_ge(VV, vv_bmax1)
                    for h in range(NH):
                        mms = t.matmul(ps_misc[:, 16 + h:17 + h],
                                       s_negsel[:, h * 128:(h + 1) * 128],
                                       s_bmax[:, :1], start=True, stop=True)
                    mms.then_inc(TT, 1)
                    t.wait_ge(VV, vv_nbevac)
                    t.transpose(ps_trb[:, :], s_nbpad[:, :],
                                s_identb[:, :]).then_inc(TT, 1)
                    t.wait_ge(VV, vv_nbT)
                    t.matmul(ps_misc[:, 32:32 + NH * NSLOT], s_nbT[0:33, :], s_usel[:, :],
                             start=True, stop=True).then_inc(TT, 1)
                    for gi, (g, hh) in enumerate(ghs):
                        h = 4 * g + hh
                        t.wait_ge(LG[g % 2], 16 * 25 * (g // 2 + 1))
                        t.wait_ge(VV, vv_kg[g])
                        if gi >= 1:
                            t.wait_ge(VV, vv_epi[ghs[gi - 1]])
                        for p_ in range(NSLOT):
                            seg = 0 if p_ < 8 else 1
                            half = seg * 512
                            for j in range(4):
                                tl = p_ * 4 + j
                                par = tl % 2
                                if tl >= 2:
                                    pu, pj = divmod(tl - 2, 4)
                                    t.wait_ge(AA, aa_exp[(g, hh, pu, pj)])
                                elif gi >= 1:
                                    pg, ph_ = ghs[gi - 1]
                                    t.wait_ge(AA, aa_exp[(pg, ph_, NSLOT - 1, 2 + j)])
                                else:
                                    t.wait_ge(VV, vv_dmax[diag_lin[126 + j]])
                                t.matmul(ps_scores[:, par * 512:(par + 1) * 512],
                                         s_Kg[g % 2][:, p_ * TB + j * 128:
                                                     p_ * TB + (j + 1) * 128],
                                         s_qTd[:, (h * 2 + seg) * TB:
                                               (h * 2 + seg + 1) * TB],
                                         start=True, stop=True).then_inc(TT, 1)
                            psrc = s_PTm if p_ in (0, 8) else s_PT
                            for j in range(4):
                                if p_ in (0, 8):
                                    t.wait_ge(VV, vv_mask[(g, hh, p_, j)])
                                else:
                                    t.wait_ge(AA, aa_exp[(g, hh, p_, j)])
                                t.matmul(ps_sums[:1, half:half + 512],
                                         s_onesb[:, :1], psrc[j][:, :],
                                         start=(p_ in (0, 8) and j == 0), stop=(j == 3),
                                         skip_group_check=True)
                            for j in range(4):
                                mma = t.matmul(ps_attn[:, half:half + 512],
                                               s_Vg[g % 2][:, p_ * TB + j * 128:
                                                           p_ * TB + (j + 1) * 128],
                                               psrc[j][:, :],
                                               start=(p_ in (0, 8) and j == 0),
                                               stop=(j == 3),
                                               skip_group_check=True)
                            mma.then_inc(TT, 1)
                        t.wait_ge(VV, vv_recip[(g, hh)])
                        t.wait_ge(AA, aa_exp[(g, hh, NSLOT - 1, 3)])
                        t.matmul(ps_scores[:, 0:512], s_onesrf[:1, :], s_recip[:1, 0:512],
                                 start=True, stop=True)
                        t.matmul(ps_scores[:, 512:1024], s_onesrf[:1, :],
                                 s_recip[:1, 512:1024],
                                 start=True, stop=True).then_inc(TT, 1)

                @blk.scalar
                def _(sc):
                    sc.wait_ge(VV, vv_nbu)
                    for gi, (g, hh) in enumerate(ghs):
                        h = 4 * g + hh
                        for p_ in range(NSLOT):
                            if p_ >= 1:
                                sc.wait_ge(TT, tt_avdone[(g, hh, p_ - 1)])
                            elif gi >= 1:
                                pg, ph_ = ghs[gi - 1]
                                sc.wait_ge(TT, tt_avdone[(pg, ph_, NSLOT - 1)])
                            for j in range(4):
                                par = (p_ * 4 + j) % 2
                                sc.wait_ge(TT, tt_score[(g, hh, p_, j)])
                                sc.activation(s_PT[j][:, :],
                                              ps_scores[:, par * 512:(par + 1) * 512],
                                              AF.Exp,
                                              bias=s_nbu[:, h * NSLOT + p_:
                                                         h * NSLOT + p_ + 1]).then_inc(AA, 1)

                @blk.vector
                def _(v):
                    v.wait_ge(LKd, 16 * 8)
                    v.memset(s_mbpad[:, :], 0.0)
                    v.memset(s_nbpad[:, :], 0.0)
                    v.memset(s_nbpad[:, 32:33], 1e30)
                    v.tensor_scalar_mul(s_kdiag[:, :], s_kdiag[:, :],
                                        s_G[:, :1]).then_inc(VV, 1)
                    for J in range(2):
                        for h in range(NH):
                            for sq in range(4):
                                v.wait_ge(TT, tt_diag[(J, h, sq)])
                                par = (((J * NH + h) * 4) + sq) % 2
                                if sq == 0:
                                    vping(v, v.reduce_max(
                                        s_mcol[:, :1],
                                        ps_scores[:, par * 512:(par + 1) * 512], AX.X))
                                    v.engine_nop().then_inc(VV, 1)
                                else:
                                    vping(v, v.reduce_max(
                                        s_rmax[:, :1],
                                        ps_scores[:, par * 512:(par + 1) * 512], AX.X))
                                    v.engine_nop().then_inc(VV, 1)
                                    vping(v, v.tensor_max(s_mcol[:, :1], s_mcol[:, :1],
                                                          s_rmax[:, :1]))
                            v.tensor_copy(s_mbpad[:, J * 32 + h:J * 32 + h + 1],
                                          s_mcol[:, :1]).then_inc(VV, 1)
                    v.wait_ge(TT, tt_tr1)
                    vping(v, v.reduce_max(s_bmax[:, :1], ps_trb[0:16, :], AX.X))
                    v.engine_nop().then_inc(VV, 1)
                    v.wait_ge(TT, tt_sel0)
                    vping(v, v.reduce_max(s_bmax[:, :1], ps_trb[32:48, :], AX.X))
                    v.engine_nop().then_inc(VV, 1)
                    v.wait_ge(TT, tt_sel1)
                    v.tensor_copy(s_nbpad[:, 0:32], ps_misc[:, 0:32]).then_inc(VV, 1)
                    v.wait_ge(TT, tt_tr2)
                    v.tensor_copy(s_nbT[0:33, :], ps_trb[0:33, :]).then_inc(VV, 1)
                    v.wait_ge(TT, tt_usel)
                    v.tensor_copy(s_nbu[:, :],
                                  ps_misc[:, 32:32 + NH * NSLOT]).then_inc(VV, 1)
                    v.wait_ge(LG[0], 16 * 25)
                    v.tensor_scalar_mul(s_Kg[0][:, :], s_Kg[0][:, :],
                                        s_G[:, :1]).then_inc(VV, 1)
                    v.wait_ge(LG[1], 16 * 25)
                    v.tensor_scalar_mul(s_Kg[1][:, :], s_Kg[1][:, :],
                                        s_G[:, :1]).then_inc(VV, 1)
                    for gi, (g, hh) in enumerate(ghs):
                        h = 4 * g + hh
                        for p_ in (0, 8):
                            for j in range(4):
                                v.wait_ge(AA, aa_exp[(g, hh, p_, j)])
                                if j == 0:
                                    if p_ == 8:
                                        v.wait_ge(TT, tt_avdone[(g, hh, 0)])
                                    elif gi >= 1:
                                        pg, ph_ = ghs[gi - 1]
                                        v.wait_ge(TT, tt_avdone[(pg, ph_, NSLOT - 1)])
                                v.tensor_mul(s_PTm[j][:, :], s_PT[j][:, :],
                                             s_masks[:, j * TB:(j + 1) * TB]).then_inc(VV, 1)
                        v.wait_ge(TT, tt_avdone[(g, hh, NSLOT - 1)])
                        if gi >= 1:
                            v.wait_ge(TT, tt_repl[ghs[gi - 1]])
                        vping(v, v.reciprocal(s_recip[:1, :], ps_sums[:1, :]))
                        v.engine_nop().then_inc(VV, 1)
                        v.wait_ge(TT, tt_repl[(g, hh)])
                        vping(v, v.tensor_copy(s_recipb[:, :], ps_scores[:, :]))
                        v.tensor_mul(s_attnT[:, h * 1024:(h + 1) * 1024], ps_attn[:, :],
                                     s_recipb[:, :])
                        vping(v, v.tensor_copy(s_rmax[:, :1],
                                               s_recipb[:, 0:1]))
                        v.engine_nop().then_inc(VV, 1)
                        if hh == 0 and 2 <= g + 1 <= 3:
                            v.wait_ge(LG[(g + 1) % 2], 16 * 25 * ((g + 1) // 2 + 1))
                            v.tensor_scalar_mul(s_Kg[(g + 1) % 2][:, :],
                                                s_Kg[(g + 1) % 2][:, :],
                                                s_G[:, :1]).then_inc(VV, 1)

        if _PH >= 5:
            af2.close()
            tt = tt_F_end
            tt_gu = {}
            tt_b = {}
            for ch in range(2):
                for hc in range(NHC):
                    tt += 1
                    tt_gu[(ch, hc)] = tt
                for dc in range(4):
                    for t_ in range(4):
                        tt += 1
                        tt_b[(ch, dc, t_)] = tt
            aa_silu = {}
            aa = aa_F_end
            for ch in range(2):
                for hc in range(NHC):
                    aa += 1
                    aa_silu[(ch, hc)] = aa
            vv = vv_F_end
            vv_gu = {}
            vv_add = {}
            for ch in range(2):
                for hc in range(NHC):
                    vv += 1
                    vv_gu[(ch, hc)] = vv
                for dc in range(4):
                    for t_ in range(4):
                        vv += 1
                        vv_add[(ch, dc, t_)] = vv
            ldcis = [(ch, dc) for ch in range(2) for dc in range(4)]

            with ExitStack() as gout:
                Eg = gout.enter_context
                s_gu = [Eg(nc.sbuf_tensor(f"s_gu{hc}", [128, TB], BF16))
                        for hc in range(NHC)]
                s_Wo = [[Eg(nc.sbuf_tensor(f"s_Wo{p_}_{a}", [128, 512], BF16))
                         for a in range(ND)] for p_ in range(2)]

                for ch in range(2):
                    with ExitStack() as asc, nc.Block() as blk:
                        Ea_ = asc.enter_context
                        s_xnTg = [Ea_(nc.sbuf_tensor(f"s_xnTg{ch}_{d}", [128, TB], BF16))
                                  for d in range(ND)]
                        s_Wgs = [[Ea_(nc.sbuf_tensor(f"s_Wgs{ch}_{p_}_{d}", [128, 512],
                                                     BF16))
                                  for d in range(ND)] for p_ in range(2)]
                        s_Wus = [[Ea_(nc.sbuf_tensor(f"s_Wus{ch}_{p_}_{d}", [128, 512],
                                                     BF16))
                                  for d in range(ND)] for p_ in range(2)]
                        s_silu = [Ea_(nc.sbuf_tensor(f"s_silu{ch}_{k}", [128, TB], BF16))
                                  for k in range(2)]
                        ps_g = [Ea_(nc.psum_tensor(f"ps_g{ch}_{k}", [128, TB], F32))
                                for k in range(2)]
                        ps_u = [Ea_(nc.psum_tensor(f"ps_u{ch}_{k}", [128, TB], F32))
                                for k in range(2)]

                        @blk.sync
                        def _(sy, ch=ch, s_xnTg=s_xnTg, s_Wgs=s_Wgs, s_Wus=s_Wus):
                            if ch == 0:
                                sy.wait_ge(VV, vv_F_end)
                                sy.wait_ge(SW, SW_T)
                            if ch == 1:
                                sy.wait_ge(TT, tt_b[(0, 3, 3)])
                                sy.wait_ge(VV, vv_add[(0, 3, 3)])
                                sy.wait_ge(GS[0], 16 * 8)
                                sy.wait_ge(GS[1], 16 * 8)
                            for d in range(ND):
                                sy.dma_start_transpose(
                                    s_xnTg[d][:, :],
                                    xn_own.ap()[ch * TB:(ch + 1) * TB,
                                                d * 128:(d + 1) * 128]
                                ).then_inc(LX2, 16)
                            for hq in range(NHQ):
                                par = hq % 2
                                if hq >= 2:
                                    sy.wait_ge(TT, tt_gu[(ch, (hq - 2) * 4 + 3)])
                                for d in range(ND):
                                    sy.dma_start_transpose(
                                        s_Wgs[par][d][:, :],
                                        wg_s.ap()[hq * 512:(hq + 1) * 512,
                                                  d * 128:(d + 1) * 128]
                                    ).then_inc(LS[par], 16)
                                for d in range(ND):
                                    sy.dma_start_transpose(
                                        s_Wus[par][d][:, :],
                                        wu_s.ap()[hq * 512:(hq + 1) * 512,
                                                  d * 128:(d + 1) * 128]
                                    ).then_inc(LS[par], 16)

                        if ch == 1:
                            @blk.gpsimd
                            def _(g, ch=ch):
                                for i in (4, 5):
                                    pch, pdc = ldcis[i - 2]
                                    g.wait_ge(TT, tt_b[(pch, pdc, 3)])
                                    nch, ndc = ldcis[i]
                                    for a in range(ND):
                                        g.dma_start(s_Wo[ndc % 2][a][:, :],
                                                    Wo_in[a * 128:(a + 1) * 128,
                                                          ndc * 512:(ndc + 1) * 512]
                                                    ).then_inc(GO[ndc % 2], 16)

                        @blk.tensor
                        def _(t, ch=ch, s_xnTg=s_xnTg, s_Wgs=s_Wgs, s_Wus=s_Wus,
                              ps_g=ps_g, ps_u=ps_u):
                            if ch == 0:
                                t.wait_ge(VV, vv_F_end)
                            else:
                                t.wait_ge(VV, vv_add[(0, 3, 3)])
                            t.wait_ge(LX2, 16 * ND * (ch + 1))
                            for hq in range(NHQ):
                                par = hq % 2
                                lots = ch * (6 if par == 0 else 5) + hq // 2 + 1
                                t.wait_ge(LS[par], 16 * 32 * lots)
                                for hc4 in range(4):
                                    hc = hq * 4 + hc4
                                    if hc >= 2 or ch == 1:
                                        key = (ch, hc - 2) if hc >= 2 else (0, NHC - 2 + hc)
                                        t.wait_ge(VV, vv_gu[key])
                                    for d in range(ND):
                                        t.matmul(ps_g[hc % 2][:, :],
                                                 s_Wgs[par][d][:, hc4 * 128:(hc4 + 1) * 128],
                                                 s_xnTg[d][:, :], start=(d == 0),
                                                 stop=(d == ND - 1))
                                    for d in range(ND):
                                        mmu = t.matmul(
                                            ps_u[hc % 2][:, :],
                                            s_Wus[par][d][:, hc4 * 128:(hc4 + 1) * 128],
                                            s_xnTg[d][:, :], start=(d == 0),
                                            stop=(d == ND - 1))
                                    mmu.then_inc(TT, 1)

                        @blk.scalar
                        def _(sc, ch=ch, ps_g=ps_g, s_silu=s_silu):
                            for hc in range(NHC):
                                sc.wait_ge(TT, tt_gu[(ch, hc)])
                                sc.activation(s_silu[hc % 2][:, :], ps_g[hc % 2][:, :],
                                              AF.Silu).then_inc(AA, 1)

                        @blk.vector
                        def _(v, ch=ch, ps_u=ps_u, s_silu=s_silu):
                            for hc in range(NHC):
                                v.wait_ge(AA, aa_silu[(ch, hc)])
                                if ch == 1 and hc == 0:
                                    v.wait_ge(TT, tt_b[(0, 3, 3)])
                                v.tensor_mul(s_gu[hc][:, :], ps_u[hc % 2][:, :],
                                             s_silu[hc % 2][:, :]).then_inc(VV, 1)

                    with ExitStack() as bsc, nc.Block() as blk:
                        Eb_ = bsc.enter_context
                        s_WdT = [[Eb_(nc.sbuf_tensor(f"s_WdT{ch}_{p_}_{hc}", [128, 512],
                                                     BF16))
                                  for hc in range(NHC)] for p_ in range(2)]
                        s_xo = [Eb_(nc.sbuf_tensor(f"s_xo{ch}_{k}", [128, 512], F32))
                                for k in range(2)]
                        s_out = [Eb_(nc.sbuf_tensor(f"s_out{ch}_{k}", [128, 512], F32))
                                 for k in range(2)]
                        ps_out = [Eb_(nc.psum_tensor(f"ps_o{ch}_{k}", [128, 512], F32))
                                  for k in range(2)]

                        @blk.sync
                        def _(sy, ch=ch, s_WdT=s_WdT, s_xo=s_xo):
                            sy.wait_ge(TT, tt_gu[(ch, NHC - 1)])
                            sy.wait_ge(VV, vv_gu[(ch, NHC - 1)])
                            for dc in range(4):
                                ldci = 4 * ch + dc
                                if dc >= 2:
                                    pch, pdc = ldcis[ldci - 2]
                                    sy.wait_ge(TT, tt_b[(pch, pdc, 3)])
                                for hc in range(NHC):
                                    sy.dma_start_transpose(
                                        s_WdT[dc % 2][hc][:, :],
                                        wd_s.ap()[dc * 512:(dc + 1) * 512,
                                                  hc * 128:(hc + 1) * 128]
                                    ).then_inc(LDt[dc % 2], 16)
                                for t_ in range(4):
                                    lin = ldci * 4 + t_
                                    if lin >= 2:
                                        pld, pt = divmod(lin - 2, 4)
                                        pch, pdc = ldcis[pld]
                                        sy.wait_ge(VV, vv_add[(pch, pdc, pt)])
                                    sy.dma_start(
                                        s_xo[t_ % 2][:, :],
                                        x_own[(ch * 4 + t_) * 128:(ch * 4 + t_ + 1) * 128,
                                              dc * 512:(dc + 1) * 512]
                                    ).then_inc(LXO[t_ % 2], 16)

                        @blk.gpsimd
                        def _(g, ch=ch, s_out=s_out):
                            if ch == 0:
                                g.wait_ge(VV, vv_F_end)
                                for pre in range(2):
                                    pdc = ldcis[pre][1]
                                    for a in range(ND):
                                        g.dma_start(s_Wo[pdc % 2][a][:, :],
                                                    Wo_in[a * 128:(a + 1) * 128,
                                                          pdc * 512:(pdc + 1) * 512]
                                                    ).then_inc(GO[pdc % 2], 16)
                            for dc in range(4):
                                ldci = 4 * ch + dc
                                for t_ in range(4):
                                    g.wait_ge(VV, vv_add[(ch, dc, t_)])
                                    g.dma_start(
                                        out_own[(ch * 4 + t_) * 128:
                                                (ch * 4 + t_ + 1) * 128,
                                                dc * 512:(dc + 1) * 512],
                                        s_out[t_ % 2][:, :]).then_inc(GS[t_ % 2], 16)
                                if dc < 2 and (ldci + 2) // 4 == ch:
                                    nch, ndc = ldcis[ldci + 2]
                                    g.wait_ge(TT, tt_b[(ch, dc, 3)])
                                    for a in range(ND):
                                        g.dma_start(s_Wo[ndc % 2][a][:, :],
                                                    Wo_in[a * 128:(a + 1) * 128,
                                                          ndc * 512:(ndc + 1) * 512]
                                                    ).then_inc(GO[ndc % 2], 16)
                            if ch == 1:
                                g.wait_ge(GS[0], 16 * 16)
                                g.wait_ge(GS[1], 16 * 16)

                        @blk.tensor
                        def _(t, ch=ch, s_WdT=s_WdT, ps_out=ps_out):
                            t.wait_ge(VV, vv_gu[(ch, NHC - 1)])
                            for dc in range(4):
                                ldci = 4 * ch + dc
                                t.wait_ge(LDt[dc % 2], 16 * NHC * (ch * 2 + dc // 2 + 1))
                                t.wait_ge(GO[dc % 2], 16 * ND * (ch * 2 + dc // 2 + 1))
                                for t_ in range(4):
                                    if ldci * 4 + t_ >= 2:
                                        lin2 = ldci * 4 + t_ - 2
                                        t.wait_ge(GS[lin2 % 2], 16 * (lin2 // 2 + 1))
                                    for a in range(ND):
                                        t.matmul(ps_out[t_ % 2][:, :],
                                                 s_attnT[:,
                                                         a * 1024 + ch * 512 + t_ * 128:
                                                         a * 1024 + ch * 512 +
                                                         (t_ + 1) * 128],
                                                 s_Wo[dc % 2][a][:, :],
                                                 start=(a == 0), stop=False,
                                                 skip_group_check=True)
                                    for hc in range(NHC):
                                        mmd = t.matmul(ps_out[t_ % 2][:, :],
                                                       s_gu[hc][:, t_ * 128:(t_ + 1) * 128],
                                                       s_WdT[dc % 2][hc][:, :],
                                                       start=False, stop=(hc == NHC - 1),
                                                       skip_group_check=True)
                                    mmd.then_inc(TT, 1)

                        @blk.vector
                        def _(v, ch=ch, s_xo=s_xo, s_out=s_out, ps_out=ps_out):
                            for dc in range(4):
                                for t_ in range(4):
                                    lin = (4 * ch + dc) * 4 + t_
                                    v.wait_ge(TT, tt_b[(ch, dc, t_)])
                                    v.wait_ge(LXO[t_ % 2], 16 * (lin // 2 + 1))
                                    if lin >= 2:
                                        v.wait_ge(GS[t_ % 2], 16 * ((lin - 2) // 2 + 1))
                                    v.tensor_add(s_out[t_ % 2][:, :], s_xo[t_ % 2][:, :],
                                                 ps_out[t_ % 2][:, :]).then_inc(VV, 1)

        if _PH < 5:
            af2.close()
            with nc.Block() as blk:
                @blk.gpsimd
                def _(g):
                    g.wait_ge(VV, vv_F_end)
                    g.wait_ge(TT, tt_F_end)
                    g.wait_ge(AA, aa_F_end)
                    g.wait_ge(GV, gv_B)
                    g.wait_ge(SW, SW_T)
                    g.wait_ge(SX, SX_T)
                    if _PH >= 2:
                        g.wait_ge(SKs, SK_T)
                        g.wait_ge(GXo, GXo_T)
                    if _PH >= 3:
                        g.wait_ge(SQ[0], SQ_T[0])
                        g.wait_ge(SQ[1], SQ_T[1])
                    nd = 0
                    if _PH == 1 and os.environ.get("KD") == "wu":
                        g.dma_start(out_own[0:1024, :],
                                    wu_s.ap()[0:1024, 0:2048]).then_inc(GS[0], 16)
                        nd = 1
                    elif _PH == 1 and os.environ.get("KD") == "wg":
                        g.dma_start(out_own[0:1024, :],
                                    wg_s.ap()[0:1024, 0:2048]).then_inc(GS[0], 16)
                        nd = 1
                    elif _PH == 1:
                        g.dma_start(out_own[0:512, :],
                                    xn_s.ap()[0:512, :]).then_inc(GS[0], 16)
                        g.wait_ge(GS[0], 16)
                        g.dma_start(out_own[512:768, :],
                                    wg_s.ap()[0:256, :]).then_inc(GS[0], 16)
                        g.wait_ge(GS[0], 32)
                        g.dma_start(out_own[768:896, :],
                                    wd_s.ap()[0:128, 0:2048]).then_inc(GS[0], 16)
                        g.wait_ge(GS[0], 48)
                        g.dma_start(out_own[896:1024, :],
                                    wu_s.ap()[0:128, 0:2048]).then_inc(GS[0], 16)
                        nd = 4
                    elif _PH == 2:
                        g.dma_start(out_own[0:1024, 0:512],
                                    k_s.ap()[0:1024, :]).then_inc(GS[0], 16)
                        g.wait_ge(GS[0], 16)
                        g.dma_start(out_own[0:1024, 512:1024],
                                    v_row_s.ap()[0:1024, :]).then_inc(GS[0], 16)
                        g.wait_ge(GS[0], 32)
                        nd = 2
                        if os.environ.get("KXO", "1") != "0":
                            g.dma_start(out_own[0:1024, 1024:2048],
                                        xn_own.ap()[0:1024, 0:1024]).then_inc(GS[0], 16)
                            nd = 3
                    elif _PH == 3:
                        g.dma_start(out_own[:, :], q_s.ap()[:, :]).then_inc(GS[0], 16)
                        nd = 1
                    elif _PH == 4:
                        g.dma_start(bass.AP(out_own.tensor, 0, [[16384, 128], [1, 16384]]),
                                    s_attnT[:, :]).then_inc(GS[0], 16)
                        nd = 1
                    g.wait_ge(GS[0], 16 * nd)

    return nc



def kernel(L0_attn_x, L0_attn_Wq, L0_attn_Wk, L0_attn_Wv, L0_attn_Wo,
           L0_attn_gq, L0_attn_gk, L0_ffn_wq_gate, L0_ffn_s_gate,
           L0_ffn_wq_up, L0_ffn_s_up, L0_ffn_wq_down, L0_ffn_s_down):
    x = np.ascontiguousarray(np.asarray(L0_attn_x, np.float32).reshape(S, D))
    Wq = np.ascontiguousarray(np.asarray(L0_attn_Wq, np.float32))
    Wk = np.ascontiguousarray(np.asarray(L0_attn_Wk, np.float32))
    Wv = np.ascontiguousarray(np.asarray(L0_attn_Wv, np.float32))
    Wo = np.ascontiguousarray(np.asarray(L0_attn_Wo, np.float32))
    gq = np.ascontiguousarray(np.asarray(L0_attn_gq, np.float32).reshape(HD, 1))
    gk = np.ascontiguousarray(np.asarray(L0_attn_gk, np.float32).reshape(HD, 1))
    wg = np.ascontiguousarray(np.asarray(L0_ffn_wq_gate, np.int32))
    wu = np.ascontiguousarray(np.asarray(L0_ffn_wq_up, np.int32))
    wd = np.ascontiguousarray(np.asarray(L0_ffn_wq_down, np.int32))
    sg = np.ascontiguousarray(np.asarray(L0_ffn_s_gate, np.float32).reshape(NHC, 128).T)
    su = np.ascontiguousarray(np.asarray(L0_ffn_s_up, np.float32).reshape(NHC, 128).T)
    sd = np.ascontiguousarray(np.asarray(L0_ffn_s_down, np.float32).reshape(ND, 128).T)

    masks = np.zeros((128, 4 * TB), np.float32)
    for j in range(4):
        for kk in range(128):
            masks[kk, j * TB:(j + 1) * TB] = (j * 128 + kk <= np.arange(TB))
    masks = masks.astype(ml_dtypes.bfloat16)
    negsel = np.zeros((16, 16 * 128), np.float32)
    for h in range(16):
        negsel[h, h * 128:(h + 1) * 128] = -1.0
    negsel = negsel.astype(ml_dtypes.bfloat16)
    identb = np.eye(128, dtype=np.float32).astype(ml_dtypes.bfloat16)

    nc = build_nc()
    in_maps = []
    for core in range(8):
        J_lo, J_hi = core, 15 - core
        tbl = np.zeros((1, 16), np.int32)
        tbl[0, TBL_KD + 0] = J_lo * TB * (NKV * HD)
        tbl[0, TBL_KD + 1] = J_hi * TB * (NKV * HD)
        tbl[0, TBL_XO + 0] = J_lo * TB * D
        tbl[0, TBL_XO + 1] = J_hi * TB * D
        usel = np.zeros((33, NH * NSLOT), np.float32)
        for h in range(NH):
            for p_ in range(NSLOT):
                col = h * NSLOT + p_
                if p_ < 8:
                    real = (p_ == 0) or (p_ - 1 < J_lo)
                    seg = 0
                else:
                    real = (p_ == 8) or (p_ - 9 < J_hi)
                    seg = 1
                if real:
                    usel[seg * 16 + h, col] = 1.0
                else:
                    usel[32, col] = -1.0
        usel = usel.astype(ml_dtypes.bfloat16)
        x_own_c = np.ascontiguousarray(
            np.concatenate([x[J_lo * TB:(J_lo + 1) * TB],
                            x[J_hi * TB:(J_hi + 1) * TB]], axis=0))
        in_maps.append(dict(
            x_full=x, x_own=x_own_c, Wq_in=Wq, Wk_in=Wk, Wv_in=Wv, Wo_in=Wo,
            gq_col=gq, gk_col=gk, wg_i=wg, wu_i=wu, wd_i=wd,
            sg_in=sg, su_in=su, sd_in=sd, masks_in=masks, negsel_in=negsel,
            identb_in=identb, usel_in=usel, tbl_in=tbl,
        ))

    res = run_bass_kernel_spmd(nc, in_maps, core_ids=list(range(8)))
    out = np.zeros((S, D), np.float32)
    for core in range(8):
        J_lo, J_hi = core, 15 - core
        oo = np.asarray(res.results[core]["out_own"], np.float32)
        out[J_lo * TB:(J_lo + 1) * TB] = oo[0:TB]
        out[J_hi * TB:(J_hi + 1) * TB] = oo[TB:2 * TB]
    return out.reshape(1, S, D)


# version 03
# speedup vs baseline: 1.0721x; 1.0721x over previous
import os
import numpy as np
import ml_dtypes
from contextlib import ExitStack
import concourse.bass as bass
import concourse.mybir as mybir
from concourse.bass_utils import run_bass_kernel_spmd

F32 = mybir.dt.float32
BF16 = mybir.dt.bfloat16
I32 = mybir.dt.int32
U8 = mybir.dt.uint8
AF = mybir.ActivationFunctionType
ALU = mybir.AluOpType
AX = mybir.AxisListType

S, D, NB, TB = 8192, 2048, 16, 512
NT = S // 128
NH, NKV, HD = 16, 4, 128
HID = 5632
NHC = HID // 128
ND = D // 128
EPS_LN, EPS_RMS = 1e-5, 1e-6
NSLOT = 24
NHQ = NHC // 4
INV_SQRT_HD = float(1.0 / np.sqrt(128.0))

_PH = int(os.environ.get("KPHASES", "5"))

TBL_KD = 0
TBL_XO = 2


def build_nc():
    nc = bass.Bass("TRN2", target_bir_lowering=False)

    x_full = nc.dram_tensor("x_full", [S, D], F32, kind="ExternalInput").ap()
    x_own = nc.dram_tensor("x_own", [2 * TB, D], F32, kind="ExternalInput").ap()
    Wq_in = nc.dram_tensor("Wq_in", [D, D], F32, kind="ExternalInput").ap()
    Wk_in = nc.dram_tensor("Wk_in", [D, 512], F32, kind="ExternalInput").ap()
    Wv_in = nc.dram_tensor("Wv_in", [D, 512], F32, kind="ExternalInput").ap()
    Wo_in = nc.dram_tensor("Wo_in", [D, D], F32, kind="ExternalInput").ap()
    gq_col = nc.dram_tensor("gq_col", [HD, 1], F32, kind="ExternalInput").ap()
    gk_col = nc.dram_tensor("gk_col", [HD, 1], F32, kind="ExternalInput").ap()
    wg_i = nc.dram_tensor("wg_i", [HID, D], U8, kind="ExternalInput").ap()
    wu_i = nc.dram_tensor("wu_i", [HID, D], U8, kind="ExternalInput").ap()
    wd_i = nc.dram_tensor("wd_i", [D, HID], U8, kind="ExternalInput").ap()
    sg_in = nc.dram_tensor("sg_in", [128, NHC], F32, kind="ExternalInput").ap()
    su_in = nc.dram_tensor("su_in", [128, NHC], F32, kind="ExternalInput").ap()
    sd_in = nc.dram_tensor("sd_in", [128, ND], F32, kind="ExternalInput").ap()
    masks_in = nc.dram_tensor("masks_in", [128, 4 * TB], BF16, kind="ExternalInput").ap()
    negsel_in = nc.dram_tensor("negsel_in", [16, 16 * 128], BF16, kind="ExternalInput").ap()
    identb_in = nc.dram_tensor("identb_in", [128, 128], BF16, kind="ExternalInput").ap()
    usel_in = nc.dram_tensor("usel_in", [33, NH * NSLOT], BF16, kind="ExternalInput").ap()
    tbl_in = nc.dram_tensor("tbl_in", [1, 16], I32, kind="ExternalInput").ap()
    out_own = nc.dram_tensor("out_own", [2 * TB, D], F32, kind="ExternalOutput").ap()

    xn_s = nc.dram_tensor("xn_s", [S, D], BF16)
    xn_own = nc.dram_tensor("xn_own", [2 * TB, D], BF16)
    q_s = nc.dram_tensor("q_s", [2 * TB, D], BF16)
    k_s = nc.dram_tensor("k_s", [S, NKV * HD], BF16)
    v_row_s = nc.dram_tensor("v_row_s", [S, NKV * HD], BF16)
    ku_s = nc.dram_tensor("ku_s", [NSLOT * TB, NKV * HD], BF16)
    vu_s = nc.dram_tensor("vu_s", [NSLOT * TB, NKV * HD], BF16)
    wg_s = nc.dram_tensor("wg_s", [HID, D], BF16)
    wu_s = nc.dram_tensor("wu_s", [HID, D], BF16)
    wd_s = nc.dram_tensor("wd_s", [D, HID], BF16)

    with ExitStack() as top:
        Et = top.enter_context
        TT = Et(nc.semaphore("tt"))
        AA = Et(nc.semaphore("aa"))
        VV = Et(nc.semaphore("vv"))
        GV = Et(nc.semaphore("gv"))
        GXm = Et(nc.semaphore("gxm"))
        GKm = Et(nc.semaphore("gkm"))
        LA = Et(nc.semaphore("la"))
        LXs = [Et(nc.semaphore(f"lx{i}")) for i in range(2)]
        LC = Et(nc.semaphore("lc"))
        LC2 = Et(nc.semaphore("lc2"))
        LW = Et(nc.semaphore("lw"))
        SW = Et(nc.semaphore("sw"))
        SX = Et(nc.semaphore("sx"))
        LT = [Et(nc.semaphore(f"lt{i}")) for i in range(2)]
        SKs = Et(nc.semaphore("sk"))
        GXo = Et(nc.semaphore("gxo"))
        LE = Et(nc.semaphore("le"))
        LQ = Et(nc.semaphore("lq"))
        SQ = [Et(nc.semaphore(f"sq{i}")) for i in range(2)]
        LU = Et(nc.semaphore("lu"))
        GU = Et(nc.semaphore("gu2"))
        LQd = Et(nc.semaphore("lqd"))
        LKd = Et(nc.semaphore("lkd"))
        LG = [Et(nc.semaphore(f"lg{i}")) for i in range(2)]
        LX2 = Et(nc.semaphore("lx2"))
        LS = [Et(nc.semaphore(f"ls{i}")) for i in range(2)]
        LDt = [Et(nc.semaphore(f"ldt{i}")) for i in range(2)]
        LXO = [Et(nc.semaphore(f"lxo{i}")) for i in range(2)]
        GO = [Et(nc.semaphore(f"go{i}")) for i in range(2)]
        GS = [Et(nc.semaphore(f"gs{i}")) for i in range(2)]
        PV = Et(nc.semaphore("pv"))
        PA = Et(nc.semaphore("pa"))
        pv_n = [0]
        pa_n = [0]

        def vping(eng, inst):
            pv_n[0] += 1
            inst.then_inc(PV, 1)
            eng.wait_ge(PV, pv_n[0])

        def aping(eng, inst):
            pa_n[0] += 1
            inst.then_inc(PA, 1)
            eng.wait_ge(PA, pa_n[0])

        s_gq = Et(nc.sbuf_tensor("s_gq", [HD, 1], F32))
        s_gk = Et(nc.sbuf_tensor("s_gk", [HD, 1], F32))
        s_G = Et(nc.sbuf_tensor("s_G", [HD, 1], F32))
        s_sg = Et(nc.sbuf_tensor("s_sg", [128, NHC], F32))
        s_su = Et(nc.sbuf_tensor("s_su", [128, NHC], F32))
        s_sd = Et(nc.sbuf_tensor("s_sd", [128, ND], F32))
        s_tbl = Et(nc.sbuf_tensor("s_tbl", [1, 16], I32))
        s_onesb = Et(nc.sbuf_tensor("s_onesb", [128, 1], BF16))
        s_onesrf = Et(nc.sbuf_tensor("s_onesrf", [1, 128], F32))
        s_epsln = Et(nc.sbuf_tensor("s_epsln", [128, 1], F32))
        s_epsrms = Et(nc.sbuf_tensor("s_epsrms", [128, 1], F32))
        s_attnT = Et(nc.sbuf_tensor("s_attnT", [128, NH * 1024], BF16))
        af2 = ExitStack()
        Ea = af2.enter_context
        s_masks = Ea(nc.sbuf_tensor("s_masks", [128, 4 * TB], BF16))
        s_negsel = Ea(nc.sbuf_tensor("s_negsel", [16, 16 * 128], BF16))
        s_identb = Ea(nc.sbuf_tensor("s_identb", [128, 128], BF16))
        s_usel = Ea(nc.sbuf_tensor("s_usel", [33, NH * NSLOT], BF16))

        with nc.Block() as blk:
            @blk.sync
            def _(sy):
                for dst, srcp in [
                    (s_masks[:, :], masks_in), (s_negsel[:, :], negsel_in),
                    (s_identb[:, :], identb_in), (s_usel[:, :], usel_in),
                    (s_gq[:, :], gq_col), (s_gk[:, :], gk_col),
                    (s_sg[:, :], sg_in), (s_su[:, :], su_in), (s_sd[:, :], sd_in),
                    (s_tbl[:, :], tbl_in),
                ]:
                    sy.dma_start(dst, srcp).then_inc(LA, 16)

            @blk.gpsimd
            def _(g):
                g.memset(s_onesb[:, :], 1.0)
                g.memset(s_onesrf[:, :], 1.0)
                g.memset(s_epsln[:, :], EPS_LN)
                g.memset(s_epsrms[:, :], EPS_RMS)
                g.nop().then_inc(GV, 1)

            @blk.vector
            def _(v):
                v.wait_ge(LA, 160)
                vping(v, v.tensor_mul(s_G[:, :], s_gq[:, :], s_gk[:, :]))
                v.tensor_scalar_mul(s_G[:, :], s_G[:, :], INV_SQRT_HD).then_inc(VV, 1)
        LA_T = 160
        vv0 = 1
        gv0 = 1

        with ExitStack() as bd_scope:
            Eb = bd_scope.enter_context
            s_Wk = [Eb(nc.sbuf_tensor(f"s_Wk{d}", [128, 512], BF16)) for d in range(ND)]
            s_Wv = [Eb(nc.sbuf_tensor(f"s_Wv{d}", [128, 512], BF16)) for d in range(ND)]

            c_tiles = []
            for hc in range(NHC):
                c_tiles.append((wg_i[hc * 128:(hc + 1) * 128, :], s_sg[:, hc:hc + 1],
                                wg_s.ap()[hc * 128:(hc + 1) * 128, :], D))
            for hc in range(NHC):
                c_tiles.append((wu_i[hc * 128:(hc + 1) * 128, :], s_su[:, hc:hc + 1],
                                wu_s.ap()[hc * 128:(hc + 1) * 128, :], D))
            for dc in range(ND):
                c_tiles.append((wd_i[dc * 128:(dc + 1) * 128, :], s_sd[:, dc:dc + 1],
                                wd_s.ap()[dc * 128:(dc + 1) * 128, :], HID))
            NC_T = len(c_tiles)

            nci = []
            k = 0
            for i in range(NT):
                take = min(2, NC_T - k)
                nci.append(take)
                k += take
            csum = [0]
            for i in range(NT):
                csum.append(csum[-1] + nci[i])
            assert csum[-1] == NC_T

            vvB, gvB = vv0, gv0

            bdl = ExitStack()
            Ep = bdl.enter_context
            s_x = [Ep(nc.sbuf_tensor(f"s_x{k}", [128, D], F32)) for k in range(2)]
            s_xn = [Ep(nc.sbuf_tensor(f"s_xn{k}", [128, D], BF16)) for k in range(2)]
            s_sq = Ep(nc.sbuf_tensor("s_sq", [128, D], BF16))
            s_ssq = [Ep(nc.sbuf_tensor(f"s_ssq{k}", [128, 1], F32)) for k in range(2)]
            s_rsum = Ep(nc.sbuf_tensor("s_rsum", [128, 1], F32))
            s_negmean = [Ep(nc.sbuf_tensor(f"s_nm{k}", [128, 1], F32)) for k in range(2)]
            s_m2 = Ep(nc.sbuf_tensor("s_m2", [128, 1], F32))
            s_var = [Ep(nc.sbuf_tensor(f"s_var{k}", [128, 1], F32)) for k in range(2)]
            s_tmp1 = Ep(nc.sbuf_tensor("s_tmp1", [128, 1], F32))
            s_rstd = [Ep(nc.sbuf_tensor(f"s_rstd{k}", [128, 1], F32)) for k in range(2)]
            s_wint = [Ep(nc.sbuf_tensor(f"s_wint{k}", [128, HID], U8))
                      for k in range(2)]
            s_wbf = [Ep(nc.sbuf_tensor(f"s_wbf{k}", [128, HID], BF16)) for k in range(2)]
            s_xnT = [[Ep(nc.sbuf_tensor(f"s_xnT{p_}_{d}", [128, TB], BF16))
                      for d in range(ND)] for p_ in range(2)]
            s_kv = [Ep(nc.sbuf_tensor(f"s_kv{k}", [128, 1024], BF16)) for k in range(2)]
            s_ksq = [Ep(nc.sbuf_tensor(f"s_ksq{k}", [128, 512], F32)) for k in range(2)]
            s_kssq = Ep(nc.sbuf_tensor("s_kssq", [128, NKV], F32))
            s_krstd = Ep(nc.sbuf_tensor("s_krstd", [128, NKV], F32))

            with nc.Block() as blk:
                @blk.sync
                def _(sy):
                    sy.wait_ge(LA, LA_T)
                    for i in range(NT):
                        if i >= 2:
                            sy.wait_ge(GXm, i - 1)
                        sy.dma_start(s_x[i % 2][:, :],
                                     x_full[i * 128:(i + 1) * 128, :]).then_inc(LXs[i % 2], 16)
                        for j in range(nci[i]):
                            ci = csum[i] + j
                            if ci % 2 != 0:
                                continue
                            if ci >= 2:
                                sy.wait_ge(SW, 16 * (ci - 1))
                            srcp, scol, dst, width = c_tiles[ci]
                            sy.dma_start(s_wint[0][:, 0:width], srcp).then_inc(LC, 16)

                @blk.scalar
                def _(sc):
                    sc.wait_ge(GV, gv0)
                    for i in range(NT):
                        for j in range(nci[i]):
                            ci = csum[i] + j
                            if ci % 2 != 1:
                                continue
                            if ci >= 2:
                                sc.wait_ge(SW, 16 * (ci - 1))
                            srcp, scol, dst, width = c_tiles[ci]
                            sc.dma_start(s_wint[1][:, 0:width], srcp).then_inc(LC2, 16)
                        sc.wait_ge(LXs[i % 2], 16 * (i // 2 + 1))
                        sc.activation(s_sq[:, :], s_x[i % 2][:, :], AF.Square,
                                      accum_out=s_ssq[i % 2][:, :1]).then_inc(AA, 1)
                        sc.wait_ge(VV, vvB + 2 * i + 1)
                        aping(sc, sc.activation(s_tmp1[:, :1], s_var[i % 2][:, :1], AF.Ln,
                                                bias=s_epsln[:, :1]))
                        sc.activation(s_rstd[i % 2][:, :1], s_tmp1[:, :1], AF.Exp,
                                      scale=-0.5).then_inc(AA, 1)
                        for j in range(nci[i]):
                            ci = csum[i] + j
                            srcp, scol, dst, width = c_tiles[ci]
                            sc.wait_ge(GV, gvB + ci + 1)
                            sc.dma_start(dst, s_wbf[ci % 2][:, 0:width]).then_inc(SW, 16)

                @blk.vector
                def _(v):
                    for i in range(NT):
                        v.wait_ge(LXs[i % 2], 16 * (i // 2 + 1))
                        vping(v, v.reduce_sum(s_rsum[:, :1], s_x[i % 2][:, :], AX.X))
                        vping(v, v.tensor_scalar_mul(s_negmean[i % 2][:, :1],
                                                     s_rsum[:, :1], -1.0 / D))
                        v.wait_ge(AA, 2 * i + 1)
                        vping(v, v.tensor_mul(s_m2[:, :1], s_negmean[i % 2][:, :1],
                                              s_negmean[i % 2][:, :1]))
                        v.scalar_tensor_tensor(s_var[i % 2][:, :1], s_ssq[i % 2][:, :1],
                                               1.0 / D, s_m2[:, :1],
                                               ALU.mult, ALU.subtract).then_inc(VV, 1)
                        v.wait_ge(AA, 2 * i + 2)
                        if i >= 2:
                            v.wait_ge(GXm, i - 1)
                        v.tensor_scalar(s_xn[i % 2][:, :], s_x[i % 2][:, :],
                                        s_negmean[i % 2][:, :1], s_rstd[i % 2][:, :1],
                                        ALU.add, ALU.mult).then_inc(VV, 1)

                @blk.gpsimd
                def _(g):
                    g.wait_ge(LA, LA_T)
                    for d in range(ND):
                        g.dma_start(s_Wk[d][:, :], Wk_in[d * 128:(d + 1) * 128, :]).then_inc(LW, 16)
                    for d in range(ND):
                        g.dma_start(s_Wv[d][:, :], Wv_in[d * 128:(d + 1) * 128, :]).then_inc(LW, 16)
                    for i in range(NT):
                        for j in range(nci[i]):
                            ci = csum[i] + j
                            srcp, scol, dst, width = c_tiles[ci]
                            g.wait_ge(LC if ci % 2 == 0 else LC2, 16 * (ci // 2 + 1))
                            if ci >= 2:
                                g.wait_ge(SW, 16 * (ci - 1))
                            g.tensor_scalar(s_wbf[ci % 2][:, 0:width],
                                            s_wint[ci % 2][:, 0:width],
                                            128.0, scol, ALU.subtract, ALU.mult).then_inc(GV, 1)
                            g.wait_ge(GV, gvB + ci + 1)
                        g.wait_ge(VV, vvB + 2 * i + 2)
                        g.dma_start(xn_s.ap()[i * 128:(i + 1) * 128, :],
                                    s_xn[i % 2][:, :]).then_inc(SX, 16)
                        g.wait_ge(SX, 16 * (i + 1))
                        g.nop().then_inc(GXm, 1)

            vv_B = vvB + 2 * NT
            aa_B = 2 * NT
            gv_B = gvB + NC_T
            SW_T = 16 * NC_T
            SX_T = 16 * NT
            GXm_T = NT
            LW_T = 16 * 32
            SK_T = 0
            GKm_T = 0
            GXo_T = 0
            SQ_T = [0, 0]
            tt_E, aa_E, vv_E = 0, aa_B, vv_B

            if _PH >= 2:
                with ExitStack() as de_scope:
                    ttD, aaD, vvD = 0, aa_B, vv_B
                    NTD = int(os.environ.get("KNTD", str(NT)))
                    NBD = NTD // 4
                    KXO = os.environ.get("KXO", "1") != "0"
                    KDS = int(os.environ.get("KDS", "4"))
                    KDN = int(os.environ.get("KDN", "0"))
                    with ExitStack() as ph, nc.Block() as blk:
                        Ep2 = ph.enter_context
                        ps_kv = [Ep2(nc.psum_tensor(f"ps_kv{k}", [128, 1024], F32))
                                 for k in range(2)]

                        @blk.gpsimd
                        def _(g):
                            for idx in (range(NTD) if KDS >= 4 else []):
                                g.wait_ge(VV, vvD + 2 * idx + 2)
                                g.dma_start(k_s.ap()[idx * 128:(idx + 1) * 128, :],
                                            s_kv[idx % 2][:, 0:512]).then_inc(SKs, 16)
                                g.dma_start(v_row_s.ap()[idx * 128:(idx + 1) * 128, :],
                                            s_kv[idx % 2][:, 512:1024]).then_inc(SKs, 16)
                                g.wait_ge(SKs, 16 * 2 * (idx + 1))
                                g.nop().then_inc(GKm, 1)
                            g.wait_ge(GXm, GXm_T)
                            rxo = g.alloc_register("rxo")
                            for ch in (range(2) if KXO else []):
                                g.reg_load(rxo, s_tbl[:1, TBL_XO + ch:TBL_XO + ch + 1])
                                g.dma_start(
                                    bass.AP(xn_own, ch * TB * D, [[1, TB * D]]),
                                    bass.AP(xn_s, rxo, [[1, TB * D]])).then_inc(GXo, 16)

                        @blk.sync
                        def _(sy):
                            for c in range(NBD):
                                sy.wait_ge(GXm, 4 * c + 4)
                                if c >= 2 and KDS >= 2:
                                    sy.wait_ge(TT, ttD + 4 * (c - 1))
                                for d in range(ND):
                                    sy.dma_start_transpose(
                                        s_xnT[c % 2][d][:, :],
                                        xn_s.ap()[c * TB:(c + 1) * TB, d * 128:(d + 1) * 128]
                                    ).then_inc(LT[c % 2], 16)

                        @blk.tensor
                        def _(t):
                            if KDS >= 2:
                                t.wait_ge(LW, LW_T)
                            for c in (range(NBD) if KDS >= 2 else []):
                                t.wait_ge(LT[c % 2], 16 * ND * (c // 2 + 1))
                                for tt in range(4):
                                    idx = 4 * c + tt
                                    if idx >= 2 and KDS >= 4:
                                        t.wait_ge(GKm, idx - 1)
                                    for d in range(ND):
                                        lhs = s_xnT[c % 2][d][:, tt * 128:(tt + 1) * 128]
                                        t.matmul(ps_kv[idx % 2][:, 0:512], lhs, s_Wk[d][:, :],
                                                 start=(d == 0), stop=(d == ND - 1))
                                        mmv = t.matmul(ps_kv[idx % 2][:, 512:1024], lhs,
                                                       s_Wv[d][:, :],
                                                       start=(d == 0), stop=(d == ND - 1))
                                    mmv.then_inc(TT, 1)

                        @blk.scalar
                        def _(sc):
                            for idx in (range(NTD) if KDS >= 3 else []):
                                sc.wait_ge(TT, ttD + idx + 1)
                                if idx >= 2:
                                    sc.wait_ge(VV, vvD + 2 * (idx - 2) + 1)
                                sc.activation(s_ksq[idx % 2][:, :], ps_kv[idx % 2][:, 0:512],
                                              AF.Square).then_inc(AA, 1)
                                sc.wait_ge(VV, vvD + 2 * idx + 1)
                                if KDN & 4:
                                    aping(sc, sc.activation(s_krstd[:, 0:NKV],
                                                            s_kssq[:, 0:NKV], AF.Square))
                                    sc.activation(s_krstd[:, 0:NKV], s_krstd[:, 0:NKV],
                                                  AF.Square).then_inc(AA, 1)
                                else:
                                    aping(sc, sc.activation(s_krstd[:, 0:NKV], s_kssq[:, 0:NKV],
                                                            AF.Ln, scale=1.0 / HD,
                                                            bias=s_epsrms[:, :1]))
                                    sc.activation(s_krstd[:, 0:NKV], s_krstd[:, 0:NKV], AF.Exp,
                                                  scale=-0.5).then_inc(AA, 1)

                        @blk.vector
                        def _(v):
                            for idx in (range(NTD) if KDS >= 3 else []):
                                v.wait_ge(AA, aaD + 2 * idx + 1)
                                if KDN & 2:
                                    v.reduce_sum(s_kssq[:, 0:1], s_ksq[idx % 2][:, :],
                                                 AX.X).then_inc(VV, 1)
                                else:
                                    v.tensor_reduce(
                                        s_kssq[:, 0:NKV],
                                        s_ksq[idx % 2][:, :].rearrange("p (g n) -> p g n",
                                                                       g=NKV),
                                        AX.X, ALU.add).then_inc(VV, 1)
                                v.wait_ge(AA, aaD + 2 * idx + 2)
                                if idx >= 2 and KDS >= 4:
                                    v.wait_ge(GKm, idx - 1)
                                if KDN & 1:
                                    v.tensor_copy(s_kv[idx % 2][:, 0:512],
                                                  ps_kv[idx % 2][:, 0:512])
                                else:
                                    v.tensor_tensor(
                                        s_kv[idx % 2][:, 0:512].rearrange("p (g n) -> p g n",
                                                                          g=NKV),
                                        ps_kv[idx % 2][:, 0:512].rearrange("p (g n) -> p g n",
                                                                           g=NKV),
                                        s_krstd[:, 0:NKV].unsqueeze(2).broadcast_to(
                                            [128, NKV, 128]),
                                        ALU.mult)
                                v.tensor_copy(s_kv[idx % 2][:, 512:1024],
                                              ps_kv[idx % 2][:, 512:1024]).then_inc(VV, 1)

                    tt_D = ttD + (NTD if KDS >= 2 else 0)
                    aa_D = aaD + (2 * NTD if KDS >= 3 else 0)
                    vv_D = vvD + (2 * NTD if KDS >= 3 else 0)
                    SK_T = 16 * 2 * NTD if KDS >= 4 else 0
                    GKm_T = NTD if KDS >= 4 else 0
                    GXo_T = 16 * 2 if KXO else 0
                    bdl.close()
                    tt_E, aa_E, vv_E = tt_D, aa_D, vv_D

                    if _PH >= 3:
                        ttE, aaE, vvE = tt_D, aa_D, vv_D
                        with ExitStack() as ph, nc.Block() as blk:
                            Ep3 = ph.enter_context
                            s_Wq = [Ep3(nc.sbuf_tensor(f"s_Wq{d}", [128, D], BF16))
                                    for d in range(ND)]
                            s_xnTe = [Ep3(nc.sbuf_tensor(f"s_xnTe{d}", [128, TB], BF16))
                                      for d in range(ND)]
                            s_q = [Ep3(nc.sbuf_tensor(f"s_q{k}", [128, D], BF16))
                                   for k in range(2)]
                            s_qsq = Ep3(nc.sbuf_tensor("s_qsq", [128, D], F32))
                            s_qssq = Ep3(nc.sbuf_tensor("s_qssq", [128, NH], F32))
                            s_qrstd = Ep3(nc.sbuf_tensor("s_qrstd", [128, NH], F32))
                            ps_q = Ep3(nc.psum_tensor("ps_q", [128, D], F32))

                            @blk.sync
                            def _(sy):
                                sy.wait_ge(GXo, GXo_T)
                                for ch in range(2):
                                    if ch == 1:
                                        sy.wait_ge(TT, ttE + 4)
                                    for d in range(ND):
                                        sy.dma_start_transpose(
                                            s_xnTe[d][:, :],
                                            xn_own.ap()[ch * TB:(ch + 1) * TB,
                                                        d * 128:(d + 1) * 128]
                                        ).then_inc(LE, 16)

                            @blk.tensor
                            def _(t):
                                t.wait_ge(LQ, 16 * ND)
                                for ch in range(2):
                                    t.wait_ge(LE, 16 * ND * (ch + 1))
                                    for tt in range(4):
                                        tile = 4 * ch + tt
                                        if tile >= 1:
                                            t.wait_ge(SQ[(tile - 1) % 2],
                                                      16 * ((tile - 1) // 2 + 1))
                                        for d in range(ND):
                                            lhs = s_xnTe[d][:, tt * 128:(tt + 1) * 128]
                                            for qc in range(4):
                                                mmq = t.matmul(
                                                    ps_q[:, qc * 512:(qc + 1) * 512],
                                                    lhs,
                                                    s_Wq[d][:, qc * 512:(qc + 1) * 512],
                                                    start=(d == 0), stop=(d == ND - 1))
                                        mmq.then_inc(TT, 1)

                            @blk.scalar
                            def _(sc):
                                for tile in range(8):
                                    sc.wait_ge(TT, ttE + tile + 1)
                                    sc.activation(s_qsq[:, :], ps_q[:, :],
                                                  AF.Square).then_inc(AA, 1)
                                    sc.wait_ge(VV, vvE + 2 * tile + 1)
                                    aping(sc, sc.activation(s_qrstd[:, 0:NH], s_qssq[:, 0:NH],
                                                            AF.Ln, scale=1.0 / HD,
                                                            bias=s_epsrms[:, :1]))
                                    sc.activation(s_qrstd[:, 0:NH], s_qrstd[:, 0:NH], AF.Exp,
                                                  scale=-0.5).then_inc(AA, 1)

                            @blk.vector
                            def _(v):
                                for tile in range(8):
                                    v.wait_ge(AA, aaE + 2 * tile + 1)
                                    v.tensor_reduce(
                                        s_qssq[:, 0:NH],
                                        s_qsq[:, :].rearrange("p (g n) -> p g n", g=NH),
                                        AX.X, ALU.add).then_inc(VV, 1)
                                    v.wait_ge(AA, aaE + 2 * tile + 2)
                                    if tile >= 2:
                                        v.wait_ge(SQ[tile % 2], 16 * (tile // 2))
                                    v.tensor_tensor(
                                        s_q[tile % 2][:, :].rearrange("p (g n) -> p g n",
                                                                      g=NH),
                                        ps_q[:, :].rearrange("p (g n) -> p g n", g=NH),
                                        s_qrstd[:, 0:NH].unsqueeze(2).broadcast_to(
                                            [128, NH, 128]),
                                        ALU.mult).then_inc(VV, 1)

                            @blk.gpsimd
                            def _(g):
                                for d in range(ND):
                                    g.dma_start(s_Wq[d][:, :],
                                                Wq_in[d * 128:(d + 1) * 128, :]
                                                ).then_inc(LQ, 16)
                                for tile in range(8):
                                    g.wait_ge(VV, vvE + 2 * tile + 2)
                                    g.dma_start(q_s.ap()[tile * 128:(tile + 1) * 128, :],
                                                s_q[tile % 2][:, :]).then_inc(SQ[tile % 2], 16)

                        tt_E = ttE + 8
                        aa_E = aaE + 16
                        vv_E = vvE + 16
                        SQ_T = [16 * 4, 16 * 4]
            else:
                bdl.close()

        vv_F_end = vv_E
        tt_F_end = tt_E
        aa_F_end = aa_E
        if _PH >= 4:
            with ExitStack() as fs, nc.Block() as blk:
                Ef = fs.enter_context
                s_qTd = Ef(nc.sbuf_tensor("s_qTd", [128, NH * 1024], BF16))
                s_Kg = [Ef(nc.sbuf_tensor(f"s_Kg{p_}", [128, NSLOT * TB], BF16))
                        for p_ in range(2)]
                s_Vg = [Ef(nc.sbuf_tensor(f"s_Vg{p_}", [128, NSLOT * TB], BF16))
                        for p_ in range(2)]
                s_kdiag = Ef(nc.sbuf_tensor("s_kdiag", [128, 2 * NKV * TB], BF16))
                s_PT = [Ef(nc.sbuf_tensor(f"s_PT{j}", [128, TB], BF16)) for j in range(4)]
                s_PTm = [Ef(nc.sbuf_tensor(f"s_PTm{j}", [128, TB], BF16)) for j in range(4)]
                s_recip = Ef(nc.sbuf_tensor("s_recip", [1, 1024], F32))
                s_recipb = Ef(nc.sbuf_tensor("s_recipb", [128, 1024], F32))
                s_mbpad = Ef(nc.sbuf_tensor("s_mbpad", [128, 128], BF16))
                s_nbpad = Ef(nc.sbuf_tensor("s_nbpad", [128, 128], BF16))
                s_mcol = Ef(nc.sbuf_tensor("s_mcol", [128, 1], F32))
                s_rmax = Ef(nc.sbuf_tensor("s_rmax", [128, 1], F32))
                s_bmax = Ef(nc.sbuf_tensor("s_bmax", [16, 1], BF16))
                s_nbT = Ef(nc.sbuf_tensor("s_nbT", [33, 128], BF16))
                s_nbu = Ef(nc.sbuf_tensor("s_nbu", [128, NH * NSLOT], F32))
                ps_scores = Ef(nc.psum_tensor("ps_scores", [128, 1024], F32))
                ps_attn = Ef(nc.psum_tensor("ps_attn", [128, 1024], F32))
                ps_sums = Ef(nc.psum_tensor("ps_sums", [1, 1024], F32))
                ps_trb = Ef(nc.psum_tensor("ps_trb", [128, 128], BF16))
                ps_misc = Ef(nc.psum_tensor("ps_misc", [128, 512], F32))

                diag_lin = [(J, h, sq) for J in range(2) for h in range(NH)
                            for sq in range(4)]
                vv = vv_E
                vv += 1; vv_kd = vv
                vv_dmax = {}
                vv_mcol = {}
                for J in range(2):
                    for h in range(NH):
                        for sq in range(4):
                            vv += 1; vv_dmax[(J, h, sq)] = vv
                        vv += 1; vv_mcol[(J, h)] = vv
                vv += 1; vv_bmax0 = vv
                vv += 1; vv_bmax1 = vv
                vv += 1; vv_nbevac = vv
                vv += 1; vv_nbT = vv
                vv += 1; vv_nbu = vv
                vv += 1; vv_kg = {0: vv}
                vv += 1; vv_kg[1] = vv
                ghs = [(g, hh) for g in range(NKV) for hh in range(4)]
                vv_mask = {}
                vv_recip = {}
                vv_epi = {}
                for gi, (g, hh) in enumerate(ghs):
                    for p_ in (0, 8):
                        for j in range(4):
                            vv += 1; vv_mask[(g, hh, p_, j)] = vv
                    vv += 1; vv_recip[(g, hh)] = vv
                    vv += 1; vv_epi[(g, hh)] = vv
                    if hh == 0 and 2 <= g + 1 <= 3:
                        vv += 1; vv_kg[g + 1] = vv
                vv_F_end = vv

                tt = tt_E
                tt_diag = {}
                for J in range(2):
                    for h in range(NH):
                        for sq in range(4):
                            tt += 1; tt_diag[(J, h, sq)] = tt
                tt += 1; tt_tr1 = tt
                tt += 1; tt_sel0 = tt
                tt += 1; tt_sel1 = tt
                tt += 1; tt_tr2 = tt
                tt += 1; tt_usel = tt
                tt_score = {}
                tt_avdone = {}
                tt_repl = {}
                for g, hh in ghs:
                    for p_ in range(NSLOT):
                        for j in range(4):
                            tt += 1; tt_score[(g, hh, p_, j)] = tt
                        tt += 1; tt_avdone[(g, hh, p_)] = tt
                    tt += 1; tt_repl[(g, hh)] = tt
                tt_F_end = tt

                aa = aa_E
                aa_exp = {}
                for g, hh in ghs:
                    for p_ in range(NSLOT):
                        for j in range(4):
                            aa += 1; aa_exp[(g, hh, p_, j)] = aa
                aa_F_end = aa

                @blk.sync
                def _(sy):
                    sy.wait_ge(GKm, GKm_T)
                    rkd = sy.alloc_register("rkd")
                    sy.reg_load(rkd, s_tbl[:1, TBL_KD + 0:TBL_KD + 1])
                    sy.dma_start(bass.AP(ku_s, 0, [[1, TB * 512]]),
                                 bass.AP(k_s, rkd, [[1, TB * 512]])).then_inc(LU, 16)
                    sy.dma_start(bass.AP(ku_s, TB * 512, [[1, 7 * TB * 512]]),
                                 bass.AP(k_s, 0, [[1, 7 * TB * 512]])).then_inc(LU, 16)
                    sy.reg_load(rkd, s_tbl[:1, TBL_KD + 1:TBL_KD + 2])
                    sy.dma_start(bass.AP(ku_s, 8 * TB * 512, [[1, TB * 512]]),
                                 bass.AP(k_s, rkd, [[1, TB * 512]])).then_inc(LU, 16)
                    sy.dma_start(bass.AP(ku_s, 9 * TB * 512, [[1, 15 * TB * 512]]),
                                 bass.AP(k_s, 0, [[1, 15 * TB * 512]])).then_inc(LU, 16)
                    sy.wait_ge(SQ[0], SQ_T[0])
                    sy.wait_ge(SQ[1], SQ_T[1])
                    for h in range(NH):
                        for J in range(2):
                            sy.dma_start_transpose(
                                s_qTd[:, (h * 2 + J) * TB:(h * 2 + J + 1) * TB],
                                q_s.ap()[J * TB:(J + 1) * TB, h * 128:(h + 1) * 128]
                            ).then_inc(LQd, 16)
                    sy.wait_ge(LU, 16 * 4)
                    for J in range(2):
                        for g in range(NKV):
                            sy.dma_start_transpose(
                                s_kdiag[:, (J * NKV + g) * TB:(J * NKV + g + 1) * TB],
                                ku_s.ap()[J * 8 * TB:J * 8 * TB + TB, g * 128:(g + 1) * 128]
                            ).then_inc(LKd, 16)
                    sy.wait_ge(GU, 16 * 4)
                    for g in range(NKV):
                        if g >= 2:
                            sy.wait_ge(TT, tt_avdone[(g - 2, 3, NSLOT - 1)])
                        sy.dma_start_transpose(
                            s_Kg[g % 2][:, :],
                            ku_s.ap()[:, g * 128:(g + 1) * 128]).then_inc(LG[g % 2], 16)
                        for p_ in range(NSLOT):
                            srcp = vu_s.ap()[p_ * TB:(p_ + 1) * TB,
                                             g * 128:(g + 1) * 128].rearrange(
                                "(j p) q -> p j q", p=128)
                            dst = s_Vg[g % 2][:, p_ * TB:(p_ + 1) * TB].rearrange(
                                "p (j q) -> p j q", q=128)
                            sy.dma_start(dst, srcp).then_inc(LG[g % 2], 16)

                @blk.gpsimd
                def _(g_):
                    g_.wait_ge(GKm, GKm_T)
                    rvd = g_.alloc_register("rvd")
                    g_.reg_load(rvd, s_tbl[:1, TBL_KD + 0:TBL_KD + 1])
                    g_.dma_start(bass.AP(vu_s, 0, [[1, TB * 512]]),
                                 bass.AP(v_row_s, rvd, [[1, TB * 512]])).then_inc(GU, 16)
                    g_.dma_start(bass.AP(vu_s, TB * 512, [[1, 7 * TB * 512]]),
                                 bass.AP(v_row_s, 0, [[1, 7 * TB * 512]])).then_inc(GU, 16)
                    g_.reg_load(rvd, s_tbl[:1, TBL_KD + 1:TBL_KD + 2])
                    g_.dma_start(bass.AP(vu_s, 8 * TB * 512, [[1, TB * 512]]),
                                 bass.AP(v_row_s, rvd, [[1, TB * 512]])).then_inc(GU, 16)
                    g_.dma_start(bass.AP(vu_s, 9 * TB * 512, [[1, 15 * TB * 512]]),
                                 bass.AP(v_row_s, 0, [[1, 15 * TB * 512]])).then_inc(GU, 16)

                @blk.tensor
                def _(t):
                    t.wait_ge(GV, gv0)
                    t.wait_ge(LQd, 16 * 32)
                    t.wait_ge(LKd, 16 * 8)
                    t.wait_ge(VV, vv_kd)
                    n = 0
                    for J in range(2):
                        for h in range(NH):
                            g = h // 4
                            for sq in range(4):
                                par = n % 2
                                if n >= 2:
                                    t.wait_ge(VV, vv_dmax[diag_lin[n - 2]])
                                lhs = s_qTd[:, (h * 2 + J) * TB + sq * 128:
                                            (h * 2 + J) * TB + (sq + 1) * 128]
                                rhs = s_kdiag[:, (J * NKV + g) * TB:(J * NKV + g + 1) * TB]
                                t.matmul(ps_scores[:, par * 512:(par + 1) * 512], lhs, rhs,
                                         start=True, stop=True).then_inc(TT, 1)
                                n += 1
                    t.wait_ge(VV, vv_mcol[(1, NH - 1)])
                    t.transpose(ps_trb[:, :], s_mbpad[:, :],
                                s_identb[:, :]).then_inc(TT, 1)
                    t.wait_ge(VV, vv_bmax0)
                    for h in range(NH):
                        mms = t.matmul(ps_misc[:, h:h + 1],
                                       s_negsel[:, h * 128:(h + 1) * 128],
                                       s_bmax[:, :1], start=True, stop=True)
                    mms.then_inc(TT, 1)
                    t.wait_ge(VV, vv_bmax1)
                    for h in range(NH):
                        mms = t.matmul(ps_misc[:, 16 + h:17 + h],
                                       s_negsel[:, h * 128:(h + 1) * 128],
                                       s_bmax[:, :1], start=True, stop=True)
                    mms.then_inc(TT, 1)
                    t.wait_ge(VV, vv_nbevac)
                    t.transpose(ps_trb[:, :], s_nbpad[:, :],
                                s_identb[:, :]).then_inc(TT, 1)
                    t.wait_ge(VV, vv_nbT)
                    t.matmul(ps_misc[:, 32:32 + NH * NSLOT], s_nbT[0:33, :], s_usel[:, :],
                             start=True, stop=True).then_inc(TT, 1)
                    for gi, (g, hh) in enumerate(ghs):
                        h = 4 * g + hh
                        t.wait_ge(LG[g % 2], 16 * 25 * (g // 2 + 1))
                        t.wait_ge(VV, vv_kg[g])
                        if gi >= 1:
                            t.wait_ge(VV, vv_epi[ghs[gi - 1]])
                        for p_ in range(NSLOT):
                            seg = 0 if p_ < 8 else 1
                            half = seg * 512
                            for j in range(4):
                                tl = p_ * 4 + j
                                par = tl % 2
                                if tl >= 2:
                                    pu, pj = divmod(tl - 2, 4)
                                    t.wait_ge(AA, aa_exp[(g, hh, pu, pj)])
                                elif gi >= 1:
                                    pg, ph_ = ghs[gi - 1]
                                    t.wait_ge(AA, aa_exp[(pg, ph_, NSLOT - 1, 2 + j)])
                                else:
                                    t.wait_ge(VV, vv_dmax[diag_lin[126 + j]])
                                t.matmul(ps_scores[:, par * 512:(par + 1) * 512],
                                         s_Kg[g % 2][:, p_ * TB + j * 128:
                                                     p_ * TB + (j + 1) * 128],
                                         s_qTd[:, (h * 2 + seg) * TB:
                                               (h * 2 + seg + 1) * TB],
                                         start=True, stop=True).then_inc(TT, 1)
                            psrc = s_PTm if p_ in (0, 8) else s_PT
                            for j in range(4):
                                if p_ in (0, 8):
                                    t.wait_ge(VV, vv_mask[(g, hh, p_, j)])
                                else:
                                    t.wait_ge(AA, aa_exp[(g, hh, p_, j)])
                                t.matmul(ps_sums[:1, half:half + 512],
                                         s_onesb[:, :1], psrc[j][:, :],
                                         start=(p_ in (0, 8) and j == 0), stop=(j == 3),
                                         skip_group_check=True)
                            for j in range(4):
                                mma = t.matmul(ps_attn[:, half:half + 512],
                                               s_Vg[g % 2][:, p_ * TB + j * 128:
                                                           p_ * TB + (j + 1) * 128],
                                               psrc[j][:, :],
                                               start=(p_ in (0, 8) and j == 0),
                                               stop=(j == 3),
                                               skip_group_check=True)
                            mma.then_inc(TT, 1)
                        t.wait_ge(VV, vv_recip[(g, hh)])
                        t.wait_ge(AA, aa_exp[(g, hh, NSLOT - 1, 3)])
                        t.matmul(ps_scores[:, 0:512], s_onesrf[:1, :], s_recip[:1, 0:512],
                                 start=True, stop=True)
                        t.matmul(ps_scores[:, 512:1024], s_onesrf[:1, :],
                                 s_recip[:1, 512:1024],
                                 start=True, stop=True).then_inc(TT, 1)

                @blk.scalar
                def _(sc):
                    sc.wait_ge(VV, vv_nbu)
                    for gi, (g, hh) in enumerate(ghs):
                        h = 4 * g + hh
                        for p_ in range(NSLOT):
                            if p_ >= 1:
                                sc.wait_ge(TT, tt_avdone[(g, hh, p_ - 1)])
                            elif gi >= 1:
                                pg, ph_ = ghs[gi - 1]
                                sc.wait_ge(TT, tt_avdone[(pg, ph_, NSLOT - 1)])
                            for j in range(4):
                                par = (p_ * 4 + j) % 2
                                sc.wait_ge(TT, tt_score[(g, hh, p_, j)])
                                sc.activation(s_PT[j][:, :],
                                              ps_scores[:, par * 512:(par + 1) * 512],
                                              AF.Exp,
                                              bias=s_nbu[:, h * NSLOT + p_:
                                                         h * NSLOT + p_ + 1]).then_inc(AA, 1)

                @blk.vector
                def _(v):
                    v.wait_ge(LKd, 16 * 8)
                    v.memset(s_mbpad[:, :], 0.0)
                    v.memset(s_nbpad[:, :], 0.0)
                    v.memset(s_nbpad[:, 32:33], 1e30)
                    v.tensor_scalar_mul(s_kdiag[:, :], s_kdiag[:, :],
                                        s_G[:, :1]).then_inc(VV, 1)
                    for J in range(2):
                        for h in range(NH):
                            for sq in range(4):
                                v.wait_ge(TT, tt_diag[(J, h, sq)])
                                par = (((J * NH + h) * 4) + sq) % 2
                                if sq == 0:
                                    vping(v, v.reduce_max(
                                        s_mcol[:, :1],
                                        ps_scores[:, par * 512:(par + 1) * 512], AX.X))
                                    v.engine_nop().then_inc(VV, 1)
                                else:
                                    vping(v, v.reduce_max(
                                        s_rmax[:, :1],
                                        ps_scores[:, par * 512:(par + 1) * 512], AX.X))
                                    v.engine_nop().then_inc(VV, 1)
                                    vping(v, v.tensor_max(s_mcol[:, :1], s_mcol[:, :1],
                                                          s_rmax[:, :1]))
                            v.tensor_copy(s_mbpad[:, J * 32 + h:J * 32 + h + 1],
                                          s_mcol[:, :1]).then_inc(VV, 1)
                    v.wait_ge(TT, tt_tr1)
                    vping(v, v.reduce_max(s_bmax[:, :1], ps_trb[0:16, :], AX.X))
                    v.engine_nop().then_inc(VV, 1)
                    v.wait_ge(TT, tt_sel0)
                    vping(v, v.reduce_max(s_bmax[:, :1], ps_trb[32:48, :], AX.X))
                    v.engine_nop().then_inc(VV, 1)
                    v.wait_ge(TT, tt_sel1)
                    v.tensor_copy(s_nbpad[:, 0:32], ps_misc[:, 0:32]).then_inc(VV, 1)
                    v.wait_ge(TT, tt_tr2)
                    v.tensor_copy(s_nbT[0:33, :], ps_trb[0:33, :]).then_inc(VV, 1)
                    v.wait_ge(TT, tt_usel)
                    v.tensor_copy(s_nbu[:, :],
                                  ps_misc[:, 32:32 + NH * NSLOT]).then_inc(VV, 1)
                    v.wait_ge(LG[0], 16 * 25)
                    v.tensor_scalar_mul(s_Kg[0][:, :], s_Kg[0][:, :],
                                        s_G[:, :1]).then_inc(VV, 1)
                    v.wait_ge(LG[1], 16 * 25)
                    v.tensor_scalar_mul(s_Kg[1][:, :], s_Kg[1][:, :],
                                        s_G[:, :1]).then_inc(VV, 1)
                    for gi, (g, hh) in enumerate(ghs):
                        h = 4 * g + hh
                        for p_ in (0, 8):
                            for j in range(4):
                                v.wait_ge(AA, aa_exp[(g, hh, p_, j)])
                                if j == 0:
                                    if p_ == 8:
                                        v.wait_ge(TT, tt_avdone[(g, hh, 0)])
                                    elif gi >= 1:
                                        pg, ph_ = ghs[gi - 1]
                                        v.wait_ge(TT, tt_avdone[(pg, ph_, NSLOT - 1)])
                                v.tensor_mul(s_PTm[j][:, :], s_PT[j][:, :],
                                             s_masks[:, j * TB:(j + 1) * TB]).then_inc(VV, 1)
                        v.wait_ge(TT, tt_avdone[(g, hh, NSLOT - 1)])
                        if gi >= 1:
                            v.wait_ge(TT, tt_repl[ghs[gi - 1]])
                        vping(v, v.reciprocal(s_recip[:1, :], ps_sums[:1, :]))
                        v.engine_nop().then_inc(VV, 1)
                        v.wait_ge(TT, tt_repl[(g, hh)])
                        vping(v, v.tensor_copy(s_recipb[:, :], ps_scores[:, :]))
                        v.tensor_mul(s_attnT[:, h * 1024:(h + 1) * 1024], ps_attn[:, :],
                                     s_recipb[:, :])
                        vping(v, v.tensor_copy(s_rmax[:, :1],
                                               s_recipb[:, 0:1]))
                        v.engine_nop().then_inc(VV, 1)
                        if hh == 0 and 2 <= g + 1 <= 3:
                            v.wait_ge(LG[(g + 1) % 2], 16 * 25 * ((g + 1) // 2 + 1))
                            v.tensor_scalar_mul(s_Kg[(g + 1) % 2][:, :],
                                                s_Kg[(g + 1) % 2][:, :],
                                                s_G[:, :1]).then_inc(VV, 1)

        if _PH >= 5:
            af2.close()
            tt = tt_F_end
            tt_gu = {}
            tt_b = {}
            for ch in range(2):
                for hc in range(NHC):
                    tt += 1
                    tt_gu[(ch, hc)] = tt
                for dc in range(4):
                    for t_ in range(4):
                        tt += 1
                        tt_b[(ch, dc, t_)] = tt
            aa_silu = {}
            aa = aa_F_end
            for ch in range(2):
                for hc in range(NHC):
                    aa += 1
                    aa_silu[(ch, hc)] = aa
            vv = vv_F_end
            vv_gu = {}
            vv_add = {}
            for ch in range(2):
                for hc in range(NHC):
                    vv += 1
                    vv_gu[(ch, hc)] = vv
                for dc in range(4):
                    for t_ in range(4):
                        vv += 1
                        vv_add[(ch, dc, t_)] = vv
            ldcis = [(ch, dc) for ch in range(2) for dc in range(4)]

            with ExitStack() as gout:
                Eg = gout.enter_context
                s_gu = [Eg(nc.sbuf_tensor(f"s_gu{hc}", [128, TB], BF16))
                        for hc in range(NHC)]
                s_Wo = [[Eg(nc.sbuf_tensor(f"s_Wo{p_}_{a}", [128, 512], BF16))
                         for a in range(ND)] for p_ in range(2)]

                for ch in range(2):
                    with ExitStack() as asc, nc.Block() as blk:
                        Ea_ = asc.enter_context
                        s_xnTg = [Ea_(nc.sbuf_tensor(f"s_xnTg{ch}_{d}", [128, TB], BF16))
                                  for d in range(ND)]
                        s_Wgs = [[Ea_(nc.sbuf_tensor(f"s_Wgs{ch}_{p_}_{d}", [128, 512],
                                                     BF16))
                                  for d in range(ND)] for p_ in range(2)]
                        s_Wus = [[Ea_(nc.sbuf_tensor(f"s_Wus{ch}_{p_}_{d}", [128, 512],
                                                     BF16))
                                  for d in range(ND)] for p_ in range(2)]
                        s_silu = [Ea_(nc.sbuf_tensor(f"s_silu{ch}_{k}", [128, TB], BF16))
                                  for k in range(2)]
                        ps_g = [Ea_(nc.psum_tensor(f"ps_g{ch}_{k}", [128, TB], F32))
                                for k in range(2)]
                        ps_u = [Ea_(nc.psum_tensor(f"ps_u{ch}_{k}", [128, TB], F32))
                                for k in range(2)]

                        @blk.sync
                        def _(sy, ch=ch, s_xnTg=s_xnTg, s_Wgs=s_Wgs, s_Wus=s_Wus):
                            if ch == 0:
                                sy.wait_ge(VV, vv_F_end)
                                sy.wait_ge(SW, SW_T)
                            if ch == 1:
                                sy.wait_ge(TT, tt_b[(0, 3, 3)])
                                sy.wait_ge(VV, vv_add[(0, 3, 3)])
                                sy.wait_ge(GS[0], 16 * 8)
                                sy.wait_ge(GS[1], 16 * 8)
                            for d in range(ND):
                                sy.dma_start_transpose(
                                    s_xnTg[d][:, :],
                                    xn_own.ap()[ch * TB:(ch + 1) * TB,
                                                d * 128:(d + 1) * 128]
                                ).then_inc(LX2, 16)
                            for hq in range(NHQ):
                                par = hq % 2
                                if hq >= 2:
                                    sy.wait_ge(TT, tt_gu[(ch, (hq - 2) * 4 + 3)])
                                for d in range(ND):
                                    sy.dma_start_transpose(
                                        s_Wgs[par][d][:, :],
                                        wg_s.ap()[hq * 512:(hq + 1) * 512,
                                                  d * 128:(d + 1) * 128]
                                    ).then_inc(LS[par], 16)
                                for d in range(ND):
                                    sy.dma_start_transpose(
                                        s_Wus[par][d][:, :],
                                        wu_s.ap()[hq * 512:(hq + 1) * 512,
                                                  d * 128:(d + 1) * 128]
                                    ).then_inc(LS[par], 16)

                        if ch == 1:
                            @blk.gpsimd
                            def _(g, ch=ch):
                                for i in (4, 5):
                                    pch, pdc = ldcis[i - 2]
                                    g.wait_ge(TT, tt_b[(pch, pdc, 3)])
                                    nch, ndc = ldcis[i]
                                    for a in range(ND):
                                        g.dma_start(s_Wo[ndc % 2][a][:, :],
                                                    Wo_in[a * 128:(a + 1) * 128,
                                                          ndc * 512:(ndc + 1) * 512]
                                                    ).then_inc(GO[ndc % 2], 16)

                        @blk.tensor
                        def _(t, ch=ch, s_xnTg=s_xnTg, s_Wgs=s_Wgs, s_Wus=s_Wus,
                              ps_g=ps_g, ps_u=ps_u):
                            if ch == 0:
                                t.wait_ge(VV, vv_F_end)
                            else:
                                t.wait_ge(VV, vv_add[(0, 3, 3)])
                            t.wait_ge(LX2, 16 * ND * (ch + 1))
                            for hq in range(NHQ):
                                par = hq % 2
                                lots = ch * (6 if par == 0 else 5) + hq // 2 + 1
                                t.wait_ge(LS[par], 16 * 32 * lots)
                                for hc4 in range(4):
                                    hc = hq * 4 + hc4
                                    if hc >= 2 or ch == 1:
                                        key = (ch, hc - 2) if hc >= 2 else (0, NHC - 2 + hc)
                                        t.wait_ge(VV, vv_gu[key])
                                    for d in range(ND):
                                        t.matmul(ps_g[hc % 2][:, :],
                                                 s_Wgs[par][d][:, hc4 * 128:(hc4 + 1) * 128],
                                                 s_xnTg[d][:, :], start=(d == 0),
                                                 stop=(d == ND - 1))
                                    for d in range(ND):
                                        mmu = t.matmul(
                                            ps_u[hc % 2][:, :],
                                            s_Wus[par][d][:, hc4 * 128:(hc4 + 1) * 128],
                                            s_xnTg[d][:, :], start=(d == 0),
                                            stop=(d == ND - 1))
                                    mmu.then_inc(TT, 1)

                        @blk.scalar
                        def _(sc, ch=ch, ps_g=ps_g, s_silu=s_silu):
                            for hc in range(NHC):
                                sc.wait_ge(TT, tt_gu[(ch, hc)])
                                sc.activation(s_silu[hc % 2][:, :], ps_g[hc % 2][:, :],
                                              AF.Silu).then_inc(AA, 1)

                        @blk.vector
                        def _(v, ch=ch, ps_u=ps_u, s_silu=s_silu):
                            for hc in range(NHC):
                                v.wait_ge(AA, aa_silu[(ch, hc)])
                                if ch == 1 and hc == 0:
                                    v.wait_ge(TT, tt_b[(0, 3, 3)])
                                v.tensor_mul(s_gu[hc][:, :], ps_u[hc % 2][:, :],
                                             s_silu[hc % 2][:, :]).then_inc(VV, 1)

                    with ExitStack() as bsc, nc.Block() as blk:
                        Eb_ = bsc.enter_context
                        s_WdT = [[Eb_(nc.sbuf_tensor(f"s_WdT{ch}_{p_}_{hc}", [128, 512],
                                                     BF16))
                                  for hc in range(NHC)] for p_ in range(2)]
                        s_xo = [Eb_(nc.sbuf_tensor(f"s_xo{ch}_{k}", [128, 512], F32))
                                for k in range(2)]
                        s_out = [Eb_(nc.sbuf_tensor(f"s_out{ch}_{k}", [128, 512], F32))
                                 for k in range(2)]
                        ps_out = [Eb_(nc.psum_tensor(f"ps_o{ch}_{k}", [128, 512], F32))
                                  for k in range(2)]

                        @blk.sync
                        def _(sy, ch=ch, s_WdT=s_WdT, s_xo=s_xo):
                            sy.wait_ge(TT, tt_gu[(ch, NHC - 1)])
                            sy.wait_ge(VV, vv_gu[(ch, NHC - 1)])
                            for dc in range(4):
                                ldci = 4 * ch + dc
                                if dc >= 2:
                                    pch, pdc = ldcis[ldci - 2]
                                    sy.wait_ge(TT, tt_b[(pch, pdc, 3)])
                                for hc in range(NHC):
                                    sy.dma_start_transpose(
                                        s_WdT[dc % 2][hc][:, :],
                                        wd_s.ap()[dc * 512:(dc + 1) * 512,
                                                  hc * 128:(hc + 1) * 128]
                                    ).then_inc(LDt[dc % 2], 16)
                                for t_ in range(4):
                                    lin = ldci * 4 + t_
                                    if lin >= 2:
                                        pld, pt = divmod(lin - 2, 4)
                                        pch, pdc = ldcis[pld]
                                        sy.wait_ge(VV, vv_add[(pch, pdc, pt)])
                                    sy.dma_start(
                                        s_xo[t_ % 2][:, :],
                                        x_own[(ch * 4 + t_) * 128:(ch * 4 + t_ + 1) * 128,
                                              dc * 512:(dc + 1) * 512]
                                    ).then_inc(LXO[t_ % 2], 16)

                        @blk.gpsimd
                        def _(g, ch=ch, s_out=s_out):
                            if ch == 0:
                                g.wait_ge(VV, vv_F_end)
                                for pre in range(2):
                                    pdc = ldcis[pre][1]
                                    for a in range(ND):
                                        g.dma_start(s_Wo[pdc % 2][a][:, :],
                                                    Wo_in[a * 128:(a + 1) * 128,
                                                          pdc * 512:(pdc + 1) * 512]
                                                    ).then_inc(GO[pdc % 2], 16)
                            for dc in range(4):
                                ldci = 4 * ch + dc
                                for t_ in range(4):
                                    g.wait_ge(VV, vv_add[(ch, dc, t_)])
                                    g.dma_start(
                                        out_own[(ch * 4 + t_) * 128:
                                                (ch * 4 + t_ + 1) * 128,
                                                dc * 512:(dc + 1) * 512],
                                        s_out[t_ % 2][:, :]).then_inc(GS[t_ % 2], 16)
                                if dc < 2 and (ldci + 2) // 4 == ch:
                                    nch, ndc = ldcis[ldci + 2]
                                    g.wait_ge(TT, tt_b[(ch, dc, 3)])
                                    for a in range(ND):
                                        g.dma_start(s_Wo[ndc % 2][a][:, :],
                                                    Wo_in[a * 128:(a + 1) * 128,
                                                          ndc * 512:(ndc + 1) * 512]
                                                    ).then_inc(GO[ndc % 2], 16)
                            if ch == 1:
                                g.wait_ge(GS[0], 16 * 16)
                                g.wait_ge(GS[1], 16 * 16)

                        @blk.tensor
                        def _(t, ch=ch, s_WdT=s_WdT, ps_out=ps_out):
                            t.wait_ge(VV, vv_gu[(ch, NHC - 1)])
                            for dc in range(4):
                                ldci = 4 * ch + dc
                                t.wait_ge(LDt[dc % 2], 16 * NHC * (ch * 2 + dc // 2 + 1))
                                t.wait_ge(GO[dc % 2], 16 * ND * (ch * 2 + dc // 2 + 1))
                                for t_ in range(4):
                                    if ldci * 4 + t_ >= 2:
                                        lin2 = ldci * 4 + t_ - 2
                                        t.wait_ge(GS[lin2 % 2], 16 * (lin2 // 2 + 1))
                                    for a in range(ND):
                                        t.matmul(ps_out[t_ % 2][:, :],
                                                 s_attnT[:,
                                                         a * 1024 + ch * 512 + t_ * 128:
                                                         a * 1024 + ch * 512 +
                                                         (t_ + 1) * 128],
                                                 s_Wo[dc % 2][a][:, :],
                                                 start=(a == 0), stop=False,
                                                 skip_group_check=True)
                                    for hc in range(NHC):
                                        mmd = t.matmul(ps_out[t_ % 2][:, :],
                                                       s_gu[hc][:, t_ * 128:(t_ + 1) * 128],
                                                       s_WdT[dc % 2][hc][:, :],
                                                       start=False, stop=(hc == NHC - 1),
                                                       skip_group_check=True)
                                    mmd.then_inc(TT, 1)

                        @blk.vector
                        def _(v, ch=ch, s_xo=s_xo, s_out=s_out, ps_out=ps_out):
                            for dc in range(4):
                                for t_ in range(4):
                                    lin = (4 * ch + dc) * 4 + t_
                                    v.wait_ge(TT, tt_b[(ch, dc, t_)])
                                    v.wait_ge(LXO[t_ % 2], 16 * (lin // 2 + 1))
                                    if lin >= 2:
                                        v.wait_ge(GS[t_ % 2], 16 * ((lin - 2) // 2 + 1))
                                    v.tensor_add(s_out[t_ % 2][:, :], s_xo[t_ % 2][:, :],
                                                 ps_out[t_ % 2][:, :]).then_inc(VV, 1)

        if _PH < 5:
            af2.close()
            with nc.Block() as blk:
                @blk.gpsimd
                def _(g):
                    g.wait_ge(VV, vv_F_end)
                    g.wait_ge(TT, tt_F_end)
                    g.wait_ge(AA, aa_F_end)
                    g.wait_ge(GV, gv_B)
                    g.wait_ge(SW, SW_T)
                    g.wait_ge(SX, SX_T)
                    if _PH >= 2:
                        g.wait_ge(SKs, SK_T)
                        g.wait_ge(GXo, GXo_T)
                    if _PH >= 3:
                        g.wait_ge(SQ[0], SQ_T[0])
                        g.wait_ge(SQ[1], SQ_T[1])
                    nd = 0
                    if _PH == 1 and os.environ.get("KD") == "wu":
                        g.dma_start(out_own[0:1024, :],
                                    wu_s.ap()[0:1024, 0:2048]).then_inc(GS[0], 16)
                        nd = 1
                    elif _PH == 1 and os.environ.get("KD") == "wg":
                        g.dma_start(out_own[0:1024, :],
                                    wg_s.ap()[0:1024, 0:2048]).then_inc(GS[0], 16)
                        nd = 1
                    elif _PH == 1:
                        g.dma_start(out_own[0:512, :],
                                    xn_s.ap()[0:512, :]).then_inc(GS[0], 16)
                        g.wait_ge(GS[0], 16)
                        g.dma_start(out_own[512:768, :],
                                    wg_s.ap()[0:256, :]).then_inc(GS[0], 16)
                        g.wait_ge(GS[0], 32)
                        g.dma_start(out_own[768:896, :],
                                    wd_s.ap()[0:128, 0:2048]).then_inc(GS[0], 16)
                        g.wait_ge(GS[0], 48)
                        g.dma_start(out_own[896:1024, :],
                                    wu_s.ap()[0:128, 0:2048]).then_inc(GS[0], 16)
                        nd = 4
                    elif _PH == 2:
                        g.dma_start(out_own[0:1024, 0:512],
                                    k_s.ap()[0:1024, :]).then_inc(GS[0], 16)
                        g.wait_ge(GS[0], 16)
                        g.dma_start(out_own[0:1024, 512:1024],
                                    v_row_s.ap()[0:1024, :]).then_inc(GS[0], 16)
                        g.wait_ge(GS[0], 32)
                        nd = 2
                        if os.environ.get("KXO", "1") != "0":
                            g.dma_start(out_own[0:1024, 1024:2048],
                                        xn_own.ap()[0:1024, 0:1024]).then_inc(GS[0], 16)
                            nd = 3
                    elif _PH == 3:
                        g.dma_start(out_own[:, :], q_s.ap()[:, :]).then_inc(GS[0], 16)
                        nd = 1
                    elif _PH == 4:
                        g.dma_start(bass.AP(out_own.tensor, 0, [[16384, 128], [1, 16384]]),
                                    s_attnT[:, :]).then_inc(GS[0], 16)
                        nd = 1
                    g.wait_ge(GS[0], 16 * nd)

    return nc



def kernel(L0_attn_x, L0_attn_Wq, L0_attn_Wk, L0_attn_Wv, L0_attn_Wo,
           L0_attn_gq, L0_attn_gk, L0_ffn_wq_gate, L0_ffn_s_gate,
           L0_ffn_wq_up, L0_ffn_s_up, L0_ffn_wq_down, L0_ffn_s_down):
    x = np.ascontiguousarray(np.asarray(L0_attn_x, np.float32).reshape(S, D))
    Wq = np.ascontiguousarray(np.asarray(L0_attn_Wq, np.float32))
    Wk = np.ascontiguousarray(np.asarray(L0_attn_Wk, np.float32))
    Wv = np.ascontiguousarray(np.asarray(L0_attn_Wv, np.float32))
    Wo = np.ascontiguousarray(np.asarray(L0_attn_Wo, np.float32))
    gq = np.ascontiguousarray(np.asarray(L0_attn_gq, np.float32).reshape(HD, 1))
    gk = np.ascontiguousarray(np.asarray(L0_attn_gk, np.float32).reshape(HD, 1))
    wg = np.ascontiguousarray(np.asarray(L0_ffn_wq_gate, np.int32).astype(np.uint8))
    wu = np.ascontiguousarray(np.asarray(L0_ffn_wq_up, np.int32).astype(np.uint8))
    wd = np.ascontiguousarray(np.asarray(L0_ffn_wq_down, np.int32).astype(np.uint8))
    sg = np.ascontiguousarray(np.asarray(L0_ffn_s_gate, np.float32).reshape(NHC, 128).T)
    su = np.ascontiguousarray(np.asarray(L0_ffn_s_up, np.float32).reshape(NHC, 128).T)
    sd = np.ascontiguousarray(np.asarray(L0_ffn_s_down, np.float32).reshape(ND, 128).T)

    masks = np.zeros((128, 4 * TB), np.float32)
    for j in range(4):
        for kk in range(128):
            masks[kk, j * TB:(j + 1) * TB] = (j * 128 + kk <= np.arange(TB))
    masks = masks.astype(ml_dtypes.bfloat16)
    negsel = np.zeros((16, 16 * 128), np.float32)
    for h in range(16):
        negsel[h, h * 128:(h + 1) * 128] = -1.0
    negsel = negsel.astype(ml_dtypes.bfloat16)
    identb = np.eye(128, dtype=np.float32).astype(ml_dtypes.bfloat16)

    nc = build_nc()
    in_maps = []
    for core in range(8):
        J_lo, J_hi = core, 15 - core
        tbl = np.zeros((1, 16), np.int32)
        tbl[0, TBL_KD + 0] = J_lo * TB * (NKV * HD)
        tbl[0, TBL_KD + 1] = J_hi * TB * (NKV * HD)
        tbl[0, TBL_XO + 0] = J_lo * TB * D
        tbl[0, TBL_XO + 1] = J_hi * TB * D
        usel = np.zeros((33, NH * NSLOT), np.float32)
        for h in range(NH):
            for p_ in range(NSLOT):
                col = h * NSLOT + p_
                if p_ < 8:
                    real = (p_ == 0) or (p_ - 1 < J_lo)
                    seg = 0
                else:
                    real = (p_ == 8) or (p_ - 9 < J_hi)
                    seg = 1
                if real:
                    usel[seg * 16 + h, col] = 1.0
                else:
                    usel[32, col] = -1.0
        usel = usel.astype(ml_dtypes.bfloat16)
        x_own_c = np.ascontiguousarray(
            np.concatenate([x[J_lo * TB:(J_lo + 1) * TB],
                            x[J_hi * TB:(J_hi + 1) * TB]], axis=0))
        in_maps.append(dict(
            x_full=x, x_own=x_own_c, Wq_in=Wq, Wk_in=Wk, Wv_in=Wv, Wo_in=Wo,
            gq_col=gq, gk_col=gk, wg_i=wg, wu_i=wu, wd_i=wd,
            sg_in=sg, su_in=su, sd_in=sd, masks_in=masks, negsel_in=negsel,
            identb_in=identb, usel_in=usel, tbl_in=tbl,
        ))

    res = run_bass_kernel_spmd(nc, in_maps, core_ids=list(range(8)))
    out = np.zeros((S, D), np.float32)
    for core in range(8):
        J_lo, J_hi = core, 15 - core
        oo = np.asarray(res.results[core]["out_own"], np.float32)
        out[J_lo * TB:(J_lo + 1) * TB] = oo[0:TB]
        out[J_hi * TB:(J_hi + 1) * TB] = oo[TB:2 * TB]
    return out.reshape(1, S, D)


# version 07
# speedup vs baseline: 1.7087x; 1.5937x over previous
import os
import numpy as np
import ml_dtypes
from contextlib import ExitStack
import concourse.bass as bass
import concourse.mybir as mybir
from concourse.bass_utils import run_bass_kernel_spmd

F32 = mybir.dt.float32
BF16 = mybir.dt.bfloat16
I32 = mybir.dt.int32
U8 = mybir.dt.uint8
AF = mybir.ActivationFunctionType
ALU = mybir.AluOpType
AX = mybir.AxisListType

S, D, NB, TB = 8192, 2048, 16, 512
NT = S // 128
NH, NKV, HD = 16, 4, 128
HID = 5632
NHC = HID // 128
ND = D // 128
EPS_LN, EPS_RMS = 1e-5, 1e-6
NSLOT = 24
NHQ = NHC // 4
INV_SQRT_HD = float(1.0 / np.sqrt(128.0))

_PH = int(os.environ.get("KPHASES", "5"))

TBL_KD = 0
TBL_XO = 2


def build_nc():
    nc = bass.Bass("TRN2", target_bir_lowering=False)

    x_full = nc.dram_tensor("x_full", [S, D], F32, kind="ExternalInput").ap()
    x_own = nc.dram_tensor("x_own", [2 * TB, D], F32, kind="ExternalInput").ap()
    Wq_in = nc.dram_tensor("Wq_in", [D, D], F32, kind="ExternalInput").ap()
    Wk_in = nc.dram_tensor("Wk_in", [D, 512], F32, kind="ExternalInput").ap()
    Wv_in = nc.dram_tensor("Wv_in", [D, 512], F32, kind="ExternalInput").ap()
    Wo_in = nc.dram_tensor("Wo_in", [D, D], F32, kind="ExternalInput").ap()
    gq_col = nc.dram_tensor("gq_col", [HD, 1], F32, kind="ExternalInput").ap()
    gk_col = nc.dram_tensor("gk_col", [HD, 1], F32, kind="ExternalInput").ap()
    wg_i = nc.dram_tensor("wg_i", [HID, D], U8, kind="ExternalInput").ap()
    wu_i = nc.dram_tensor("wu_i", [HID, D], U8, kind="ExternalInput").ap()
    wd_i = nc.dram_tensor("wd_i", [D, HID], U8, kind="ExternalInput").ap()
    sg_in = nc.dram_tensor("sg_in", [128, NHC], F32, kind="ExternalInput").ap()
    su_in = nc.dram_tensor("su_in", [128, NHC], F32, kind="ExternalInput").ap()
    sd_in = nc.dram_tensor("sd_in", [128, ND], F32, kind="ExternalInput").ap()
    masks_in = nc.dram_tensor("masks_in", [128, 4 * TB], BF16, kind="ExternalInput").ap()
    negsel_in = nc.dram_tensor("negsel_in", [16, 16 * 128], BF16, kind="ExternalInput").ap()
    identb_in = nc.dram_tensor("identb_in", [128, 128], BF16, kind="ExternalInput").ap()
    usel_in = nc.dram_tensor("usel_in", [33, NH * NSLOT], BF16, kind="ExternalInput").ap()
    tbl_in = nc.dram_tensor("tbl_in", [1, 16], I32, kind="ExternalInput").ap()
    out_own = nc.dram_tensor("out_own", [2 * TB, D], F32, kind="ExternalOutput").ap()

    xn_s = nc.dram_tensor("xn_s", [S, D], BF16)
    xn_own = nc.dram_tensor("xn_own", [2 * TB, D], BF16)
    q_s = nc.dram_tensor("q_s", [2 * TB, D], BF16)
    k_s = nc.dram_tensor("k_s", [S, NKV * HD], BF16)
    v_row_s = nc.dram_tensor("v_row_s", [S, NKV * HD], BF16)
    ku_s = nc.dram_tensor("ku_s", [NSLOT * TB, NKV * HD], BF16)
    vu_s = nc.dram_tensor("vu_s", [NSLOT * TB, NKV * HD], BF16)
    wg_s = nc.dram_tensor("wg_s", [HID, D], BF16)
    wu_s = nc.dram_tensor("wu_s", [HID, D], BF16)
    wd_s = nc.dram_tensor("wd_s", [D, HID], BF16)

    with ExitStack() as top:
        Et = top.enter_context
        TT = Et(nc.semaphore("tt"))
        AA = Et(nc.semaphore("aa"))
        VV = Et(nc.semaphore("vv"))
        GV = Et(nc.semaphore("gv"))
        GXm = Et(nc.semaphore("gxm"))
        GKm = Et(nc.semaphore("gkm"))
        LA = Et(nc.semaphore("la"))
        LXs = [Et(nc.semaphore(f"lx{i}")) for i in range(2)]
        LC = Et(nc.semaphore("lc"))
        LC2 = Et(nc.semaphore("lc2"))
        LSu = [Et(nc.semaphore(f"lsu{i}")) for i in range(2)]
        SVs = Et(nc.semaphore("svs"))
        LW = Et(nc.semaphore("lw"))
        SWe = Et(nc.semaphore("swe"))
        SWo = Et(nc.semaphore("swo"))
        SX = Et(nc.semaphore("sx"))
        LT = [Et(nc.semaphore(f"lt{i}")) for i in range(2)]
        SKs = Et(nc.semaphore("sk"))
        GXo = Et(nc.semaphore("gxo"))
        LE = Et(nc.semaphore("le"))
        LQ = Et(nc.semaphore("lq"))
        SQ = [Et(nc.semaphore(f"sq{i}")) for i in range(2)]
        LU = Et(nc.semaphore("lu"))
        GU = Et(nc.semaphore("gu2"))
        LQd = Et(nc.semaphore("lqd"))
        LKd = Et(nc.semaphore("lkd"))
        LG = [Et(nc.semaphore(f"lg{i}")) for i in range(2)]
        LX2 = Et(nc.semaphore("lx2"))
        LS = [Et(nc.semaphore(f"ls{i}")) for i in range(2)]
        LDt = [Et(nc.semaphore(f"ldt{i}")) for i in range(2)]
        LXO = [Et(nc.semaphore(f"lxo{i}")) for i in range(2)]
        GO = [Et(nc.semaphore(f"go{i}")) for i in range(2)]
        GS = [Et(nc.semaphore(f"gs{i}")) for i in range(2)]
        PV = Et(nc.semaphore("pv"))
        PA = Et(nc.semaphore("pa"))
        pv_n = [0]
        pa_n = [0]

        def vping(eng, inst):
            pv_n[0] += 1
            inst.then_inc(PV, 1)
            eng.wait_ge(PV, pv_n[0])

        def aping(eng, inst):
            pa_n[0] += 1
            inst.then_inc(PA, 1)
            eng.wait_ge(PA, pa_n[0])

        s_gq = Et(nc.sbuf_tensor("s_gq", [HD, 1], F32))
        s_gk = Et(nc.sbuf_tensor("s_gk", [HD, 1], F32))
        s_G = Et(nc.sbuf_tensor("s_G", [HD, 1], F32))
        s_sg = Et(nc.sbuf_tensor("s_sg", [128, NHC], F32))
        s_su = Et(nc.sbuf_tensor("s_su", [128, NHC], F32))
        s_sd = Et(nc.sbuf_tensor("s_sd", [128, ND], F32))
        s_tbl = Et(nc.sbuf_tensor("s_tbl", [1, 16], I32))
        s_onesb = Et(nc.sbuf_tensor("s_onesb", [128, 1], BF16))
        s_onesrf = Et(nc.sbuf_tensor("s_onesrf", [1, 128], F32))
        s_epsln = Et(nc.sbuf_tensor("s_epsln", [128, 1], F32))
        s_epsrms = Et(nc.sbuf_tensor("s_epsrms", [128, 1], F32))
        s_attnT = Et(nc.sbuf_tensor("s_attnT", [128, NH * 1024], BF16))
        af2 = ExitStack()
        Ea = af2.enter_context
        s_masks = Ea(nc.sbuf_tensor("s_masks", [128, 4 * TB], BF16))
        s_negsel = Ea(nc.sbuf_tensor("s_negsel", [16, 16 * 128], BF16))
        s_identb = Ea(nc.sbuf_tensor("s_identb", [128, 128], BF16))
        s_usel = Ea(nc.sbuf_tensor("s_usel", [33, NH * NSLOT], BF16))

        with nc.Block() as blk:
            @blk.sync
            def _(sy):
                for dst, srcp in [
                    (s_masks[:, :], masks_in), (s_negsel[:, :], negsel_in),
                    (s_identb[:, :], identb_in), (s_usel[:, :], usel_in),
                    (s_gq[:, :], gq_col), (s_gk[:, :], gk_col),
                    (s_sg[:, :], sg_in), (s_su[:, :], su_in), (s_sd[:, :], sd_in),
                    (s_tbl[:, :], tbl_in),
                ]:
                    sy.dma_start(dst, srcp).then_inc(LA, 16)

            @blk.gpsimd
            def _(g):
                g.memset(s_onesb[:, :], 1.0)
                g.memset(s_onesrf[:, :], 1.0)
                g.memset(s_epsln[:, :], EPS_LN)
                g.memset(s_epsrms[:, :], EPS_RMS)
                g.nop().then_inc(GV, 1)

            @blk.vector
            def _(v):
                v.wait_ge(LA, 160)
                vping(v, v.tensor_mul(s_G[:, :], s_gq[:, :], s_gk[:, :]))
                v.tensor_scalar_mul(s_G[:, :], s_G[:, :], INV_SQRT_HD).then_inc(VV, 1)
        LA_T = 160
        vv0 = 1
        gv0 = 1

        with ExitStack() as bd_scope:
            Eb = bd_scope.enter_context
            s_Wk = [Eb(nc.sbuf_tensor(f"s_Wk{d}", [128, 512], BF16)) for d in range(ND)]
            s_Wv = [Eb(nc.sbuf_tensor(f"s_Wv{d}", [128, 512], BF16)) for d in range(ND)]

            c_tiles = []
            for hc in range(NHC):
                c_tiles.append((wg_i[hc * 128:(hc + 1) * 128, :], s_sg[:, hc:hc + 1],
                                wg_s.ap()[hc * 128:(hc + 1) * 128, :], D))
            for hc in range(NHC):
                c_tiles.append((wu_i[hc * 128:(hc + 1) * 128, :], s_su[:, hc:hc + 1],
                                wu_s.ap()[hc * 128:(hc + 1) * 128, :], D))
            for dc in range(ND):
                c_tiles.append((wd_i[dc * 128:(dc + 1) * 128, :], s_sd[:, dc:dc + 1],
                                wd_s.ap()[dc * 128:(dc + 1) * 128, :], HID))
            NC_T = len(c_tiles)

            nci = []
            k = 0
            for i in range(NT):
                take = min(2, NC_T - k)
                nci.append(take)
                k += take
            csum = [0]
            for i in range(NT):
                csum.append(csum[-1] + nci[i])
            assert csum[-1] == NC_T

            vvB, gvB = vv0, gv0

            wqs = ExitStack()
            s_WqA = [wqs.enter_context(nc.sbuf_tensor(f"s_Wq{d}", [128, D], BF16))
                     for d in range(8)]
            bdl = ExitStack()
            Ep = bdl.enter_context
            s_x = [Ep(nc.sbuf_tensor(f"s_x{k}", [128, D], F32)) for k in range(2)]
            s_xn = [Ep(nc.sbuf_tensor(f"s_xn{k}", [128, D], BF16)) for k in range(2)]
            s_sq = Ep(nc.sbuf_tensor("s_sq", [128, D], BF16))
            s_ssq = [Ep(nc.sbuf_tensor(f"s_ssq{k}", [128, 1], F32)) for k in range(2)]
            s_rsum = Ep(nc.sbuf_tensor("s_rsum", [128, 1], F32))
            s_negmean = [Ep(nc.sbuf_tensor(f"s_nm{k}", [128, 1], F32)) for k in range(2)]
            s_m2 = Ep(nc.sbuf_tensor("s_m2", [128, 1], F32))
            s_var = [Ep(nc.sbuf_tensor(f"s_var{k}", [128, 1], F32)) for k in range(2)]
            s_tmp1 = Ep(nc.sbuf_tensor("s_tmp1", [128, 1], F32))
            s_rstd = [Ep(nc.sbuf_tensor(f"s_rstd{k}", [128, 1], F32)) for k in range(2)]
            s_wint = [Ep(nc.sbuf_tensor(f"s_wint{k}", [128, HID], U8))
                      for k in range(2)]
            s_wbf = [Ep(nc.sbuf_tensor(f"s_wbf{k}", [128, HID], BF16)) for k in range(2)]
            s_xnT = [[Ep(nc.sbuf_tensor(f"s_xnT{p_}_{d}", [128, TB], BF16))
                      for d in range(ND)] for p_ in range(2)]
            s_kv = [Ep(nc.sbuf_tensor(f"s_kv{k}", [128, 1024], BF16)) for k in range(2)]
            s_ksq = [Ep(nc.sbuf_tensor(f"s_ksq{k}", [128, 512], F32)) for k in range(2)]
            s_kssq = Ep(nc.sbuf_tensor("s_kssq", [128, NKV], F32))
            s_krstd = Ep(nc.sbuf_tensor("s_krstd", [128, NKV], F32))

            with nc.Block() as blk:
                @blk.sync
                def _(sy):
                    sy.wait_ge(LA, LA_T)
                    for i in range(NT):
                        if i >= 2:
                            sy.wait_ge(GXm, i - 1)
                        sy.dma_start(s_x[i % 2][:, :],
                                     x_full[i * 128:(i + 1) * 128, :]).then_inc(LXs[i % 2], 16)

                @blk.scalar
                def _(sc):
                    sc.wait_ge(GV, gv0)
                    for i in range(NT):
                        for j in range(nci[i]):
                            ci = csum[i] + j
                            if ci >= 2:
                                sc.wait_ge(SWe if ci % 2 == 0 else SWo,
                                           16 * (ci // 2))
                            srcp, scol, dst, width = c_tiles[ci]
                            sc.dma_start(s_wint[ci % 2][:, 0:width],
                                         srcp).then_inc(LC2, 16)
                        sc.wait_ge(LXs[i % 2], 16 * (i // 2 + 1))
                        sc.activation(s_sq[:, :], s_x[i % 2][:, :], AF.Square,
                                      accum_out=s_ssq[i % 2][:, :1]).then_inc(AA, 1)
                        sc.wait_ge(VV, vvB + 2 * i + 1)
                        aping(sc, sc.activation(s_tmp1[:, :1], s_var[i % 2][:, :1], AF.Ln,
                                                bias=s_epsln[:, :1]))
                        sc.activation(s_rstd[i % 2][:, :1], s_tmp1[:, :1], AF.Exp,
                                      scale=-0.5).then_inc(AA, 1)
                        for j in range(nci[i]):
                            ci = csum[i] + j
                            if ci % 2 != 0:
                                continue
                            srcp, scol, dst, width = c_tiles[ci]
                            sc.wait_ge(GV, gvB + ci + 1)
                            sc.dma_start(dst, s_wbf[ci % 2][:, 0:width]).then_inc(SWe, 16)

                @blk.vector
                def _(v):
                    for i in range(NT):
                        v.wait_ge(LXs[i % 2], 16 * (i // 2 + 1))
                        vping(v, v.reduce_sum(s_rsum[:, :1], s_x[i % 2][:, :], AX.X))
                        vping(v, v.tensor_scalar_mul(s_negmean[i % 2][:, :1],
                                                     s_rsum[:, :1], -1.0 / D))
                        v.wait_ge(AA, 2 * i + 1)
                        vping(v, v.tensor_mul(s_m2[:, :1], s_negmean[i % 2][:, :1],
                                              s_negmean[i % 2][:, :1]))
                        v.scalar_tensor_tensor(s_var[i % 2][:, :1], s_ssq[i % 2][:, :1],
                                               1.0 / D, s_m2[:, :1],
                                               ALU.mult, ALU.subtract).then_inc(VV, 1)
                        v.wait_ge(AA, 2 * i + 2)
                        if i >= 2:
                            v.wait_ge(GXm, i - 1)
                        v.tensor_scalar(s_xn[i % 2][:, :], s_x[i % 2][:, :],
                                        s_negmean[i % 2][:, :1], s_rstd[i % 2][:, :1],
                                        ALU.add, ALU.mult).then_inc(VV, 1)
                        for j in range(nci[i]):
                            ci = csum[i] + j
                            srcp, scol, dst, width = c_tiles[ci]
                            v.wait_ge(LC2, 16 * (ci + 1))
                            if ci >= 2:
                                v.wait_ge(SWe if ci % 2 == 0 else SWo,
                                          16 * (ci // 2))
                            vping(v, v.tensor_scalar(
                                s_wbf[ci % 2][:, 0:width],
                                s_wint[ci % 2][:, 0:width],
                                128.0, scol, ALU.subtract, ALU.mult))
                            v.engine_nop().then_inc(GV, 1)

                @blk.gpsimd
                def _(g):
                    g.wait_ge(LA, LA_T)
                    for d in range(ND):
                        g.dma_start(s_Wk[d][:, :], Wk_in[d * 128:(d + 1) * 128, :]).then_inc(LW, 16)
                    for d in range(ND):
                        g.dma_start(s_Wv[d][:, :], Wv_in[d * 128:(d + 1) * 128, :]).then_inc(LW, 16)
                    for i in range(NT):
                        for j in range(nci[i]):
                            ci = csum[i] + j
                            if ci % 2 != 1:
                                continue
                            srcp, scol, dst, width = c_tiles[ci]
                            g.wait_ge(GV, gvB + ci + 1)
                            g.dma_start(dst,
                                        s_wbf[ci % 2][:, 0:width]).then_inc(SWo, 16)
                        g.wait_ge(VV, vvB + 2 * i + 2)
                        g.dma_start(xn_s.ap()[i * 128:(i + 1) * 128, :],
                                    s_xn[i % 2][:, :]).then_inc(SX, 16)
                        g.wait_ge(SX, 16 * (i + 1))
                        g.nop().then_inc(GXm, 1)
                    for d in range(8):
                        g.dma_start(s_WqA[d][:, :],
                                    Wq_in[d * 128:(d + 1) * 128, :]).then_inc(LQ, 16)

            vv_B = vvB + 2 * NT
            aa_B = 2 * NT
            gv_B = gvB + NC_T
            SWe_T = 16 * ((NC_T + 1) // 2)
            SWo_T = 16 * (NC_T // 2)
            SX_T = 16 * NT
            GXm_T = NT
            LW_T = 16 * 32
            SK_T = 0
            GKm_T = 0
            GXo_T = 0
            SQ_T = [0, 0]
            tt_E, aa_E, vv_E = 0, aa_B, vv_B

            if _PH >= 2:
                with ExitStack() as de_scope:
                    ttD, aaD, vvD = 0, aa_B, vv_B
                    NTD = int(os.environ.get("KNTD", str(NT)))
                    NBD = NTD // 4
                    KXO = os.environ.get("KXO", "1") != "0"
                    KDS = int(os.environ.get("KDS", "4"))
                    KDN = int(os.environ.get("KDN", "0"))
                    with ExitStack() as ph, nc.Block() as blk:
                        Ep2 = ph.enter_context
                        ps_kv = [Ep2(nc.psum_tensor(f"ps_kv{k}", [128, 1024], F32))
                                 for k in range(2)]

                        @blk.gpsimd
                        def _(g):
                            for idx in (range(NTD) if KDS >= 4 else []):
                                g.wait_ge(VV, vvD + 2 * idx + 2)
                                g.dma_start(k_s.ap()[idx * 128:(idx + 1) * 128, :],
                                            s_kv[idx % 2][:, 0:512]).then_inc(SKs, 16)
                                g.wait_ge(SKs, 16 * (idx + 1))
                                g.wait_ge(SVs, 16 * (idx + 1))
                                g.nop().then_inc(GKm, 1)
                            g.wait_ge(GXm, GXm_T)
                            rxo = g.alloc_register("rxo")
                            for ch in (range(2) if KXO else []):
                                g.reg_load(rxo, s_tbl[:1, TBL_XO + ch:TBL_XO + ch + 1])
                                g.dma_start(
                                    bass.AP(xn_own, ch * TB * D, [[1, TB * D]]),
                                    bass.AP(xn_s, rxo, [[1, TB * D]])).then_inc(GXo, 16)

                        @blk.sync
                        def _(sy):
                            for c in range(NBD):
                                sy.wait_ge(GXm, 4 * c + 4)
                                if c >= 2 and KDS >= 2:
                                    sy.wait_ge(TT, ttD + 4 * (c - 1))
                                for d in range(ND):
                                    sy.dma_start_transpose(
                                        s_xnT[c % 2][d][:, :],
                                        xn_s.ap()[c * TB:(c + 1) * TB, d * 128:(d + 1) * 128]
                                    ).then_inc(LT[c % 2], 16)

                        @blk.tensor
                        def _(t):
                            if KDS >= 2:
                                t.wait_ge(LW, LW_T)
                            for c in (range(NBD) if KDS >= 2 else []):
                                t.wait_ge(LT[c % 2], 16 * ND * (c // 2 + 1))
                                for tt in range(4):
                                    idx = 4 * c + tt
                                    if idx >= 2 and KDS >= 4:
                                        t.wait_ge(GKm, idx - 1)
                                    for d in range(ND):
                                        lhs = s_xnT[c % 2][d][:, tt * 128:(tt + 1) * 128]
                                        t.matmul(ps_kv[idx % 2][:, 0:512], lhs, s_Wk[d][:, :],
                                                 start=(d == 0), stop=(d == ND - 1))
                                        mmv = t.matmul(ps_kv[idx % 2][:, 512:1024], lhs,
                                                       s_Wv[d][:, :],
                                                       start=(d == 0), stop=(d == ND - 1))
                                    mmv.then_inc(TT, 1)

                        @blk.scalar
                        def _(sc):
                            for idx in (range(NTD) if KDS >= 3 else []):
                                sc.wait_ge(TT, ttD + idx + 1)
                                if idx >= 2:
                                    sc.wait_ge(VV, vvD + 2 * (idx - 2) + 1)
                                sc.activation(s_ksq[idx % 2][:, :], ps_kv[idx % 2][:, 0:512],
                                              AF.Square).then_inc(AA, 1)
                                sc.wait_ge(VV, vvD + 2 * idx + 1)
                                if KDN & 4:
                                    aping(sc, sc.activation(s_krstd[:, 0:NKV],
                                                            s_kssq[:, 0:NKV], AF.Square))
                                    sc.activation(s_krstd[:, 0:NKV], s_krstd[:, 0:NKV],
                                                  AF.Square).then_inc(AA, 1)
                                else:
                                    aping(sc, sc.activation(s_krstd[:, 0:NKV], s_kssq[:, 0:NKV],
                                                            AF.Ln, scale=1.0 / HD,
                                                            bias=s_epsrms[:, :1]))
                                    sc.activation(s_krstd[:, 0:NKV], s_krstd[:, 0:NKV], AF.Exp,
                                                  scale=-0.5).then_inc(AA, 1)
                                if KDS >= 4:
                                    sc.wait_ge(VV, vvD + 2 * idx + 2)
                                    sc.dma_start(
                                        v_row_s.ap()[idx * 128:(idx + 1) * 128, :],
                                        s_kv[idx % 2][:, 512:1024]).then_inc(SVs, 16)

                        @blk.vector
                        def _(v):
                            for idx in (range(NTD) if KDS >= 3 else []):
                                v.wait_ge(AA, aaD + 2 * idx + 1)
                                if KDN & 2:
                                    v.reduce_sum(s_kssq[:, 0:1], s_ksq[idx % 2][:, :],
                                                 AX.X).then_inc(VV, 1)
                                else:
                                    v.tensor_reduce(
                                        s_kssq[:, 0:NKV],
                                        s_ksq[idx % 2][:, :].rearrange("p (g n) -> p g n",
                                                                       g=NKV),
                                        AX.X, ALU.add).then_inc(VV, 1)
                                v.wait_ge(AA, aaD + 2 * idx + 2)
                                if idx >= 2 and KDS >= 4:
                                    v.wait_ge(GKm, idx - 1)
                                if KDN & 1:
                                    v.tensor_copy(s_kv[idx % 2][:, 0:512],
                                                  ps_kv[idx % 2][:, 0:512])
                                else:
                                    v.tensor_tensor(
                                        s_kv[idx % 2][:, 0:512].rearrange("p (g n) -> p g n",
                                                                          g=NKV),
                                        ps_kv[idx % 2][:, 0:512].rearrange("p (g n) -> p g n",
                                                                           g=NKV),
                                        s_krstd[:, 0:NKV].unsqueeze(2).broadcast_to(
                                            [128, NKV, 128]),
                                        ALU.mult)
                                v.tensor_copy(s_kv[idx % 2][:, 512:1024],
                                              ps_kv[idx % 2][:, 512:1024]).then_inc(VV, 1)

                    tt_D = ttD + (NTD if KDS >= 2 else 0)
                    aa_D = aaD + (2 * NTD if KDS >= 3 else 0)
                    vv_D = vvD + (2 * NTD if KDS >= 3 else 0)
                    SK_T = 16 * NTD if KDS >= 4 else 0
                    SV_T = 16 * NTD if KDS >= 4 else 0
                    GKm_T = NTD if KDS >= 4 else 0
                    GXo_T = 16 * 2 if KXO else 0
                    bdl.close()
                    tt_E, aa_E, vv_E = tt_D, aa_D, vv_D

                    if _PH >= 3:
                        ttE, aaE, vvE = tt_D, aa_D, vv_D
                        with ExitStack() as ph, nc.Block() as blk:
                            Ep3 = ph.enter_context
                            s_WqB = [Ep3(nc.sbuf_tensor(f"s_Wqb{d}", [128, D], BF16))
                                     for d in range(8)]
                            s_Wq = s_WqA + s_WqB
                            s_xnTe = [Ep3(nc.sbuf_tensor(f"s_xnTe{d}", [128, TB], BF16))
                                      for d in range(ND)]
                            s_q = [Ep3(nc.sbuf_tensor(f"s_q{k}", [128, D], BF16))
                                   for k in range(2)]
                            s_qsq = Ep3(nc.sbuf_tensor("s_qsq", [128, D], F32))
                            s_qssq = Ep3(nc.sbuf_tensor("s_qssq", [128, NH], F32))
                            s_qrstd = Ep3(nc.sbuf_tensor("s_qrstd", [128, NH], F32))
                            ps_q = [Ep3(nc.psum_tensor(f"ps_q{k}", [128, D], F32))
                                    for k in range(2)]

                            @blk.sync
                            def _(sy):
                                sy.wait_ge(GXo, GXo_T)
                                for ch in range(2):
                                    if ch == 1:
                                        sy.wait_ge(TT, ttE + 4)
                                    for d in range(ND):
                                        sy.dma_start_transpose(
                                            s_xnTe[d][:, :],
                                            xn_own.ap()[ch * TB:(ch + 1) * TB,
                                                        d * 128:(d + 1) * 128]
                                        ).then_inc(LE, 16)

                            @blk.tensor
                            def _(t):
                                t.wait_ge(LQ, 16 * ND)
                                t.wait_ge(GKm, GKm_T)
                                for ch in range(2):
                                    t.wait_ge(LE, 16 * ND * (ch + 1))
                                    for tt in range(4):
                                        tile = 4 * ch + tt
                                        if tile >= 2:
                                            t.wait_ge(SQ[tile % 2],
                                                      16 * (tile // 2))
                                        for d in range(ND):
                                            lhs = s_xnTe[d][:, tt * 128:(tt + 1) * 128]
                                            for qc in range(4):
                                                mmq = t.matmul(
                                                    ps_q[tile % 2][:, qc * 512:(qc + 1) * 512],
                                                    lhs,
                                                    s_Wq[d][:, qc * 512:(qc + 1) * 512],
                                                    start=(d == 0), stop=(d == ND - 1))
                                        mmq.then_inc(TT, 1)

                            @blk.scalar
                            def _(sc):
                                for tile in range(8):
                                    sc.wait_ge(TT, ttE + tile + 1)
                                    sc.activation(s_qsq[:, :], ps_q[tile % 2][:, :],
                                                  AF.Square).then_inc(AA, 1)
                                    sc.wait_ge(VV, vvE + 2 * tile + 1)
                                    aping(sc, sc.activation(s_qrstd[:, 0:NH], s_qssq[:, 0:NH],
                                                            AF.Ln, scale=1.0 / HD,
                                                            bias=s_epsrms[:, :1]))
                                    sc.activation(s_qrstd[:, 0:NH], s_qrstd[:, 0:NH], AF.Exp,
                                                  scale=-0.5).then_inc(AA, 1)

                            @blk.vector
                            def _(v):
                                for tile in range(8):
                                    v.wait_ge(AA, aaE + 2 * tile + 1)
                                    v.tensor_reduce(
                                        s_qssq[:, 0:NH],
                                        s_qsq[:, :].rearrange("p (g n) -> p g n", g=NH),
                                        AX.X, ALU.add).then_inc(VV, 1)
                                    v.wait_ge(AA, aaE + 2 * tile + 2)
                                    if tile >= 2:
                                        v.wait_ge(SQ[tile % 2], 16 * (tile // 2))
                                    v.tensor_tensor(
                                        s_q[tile % 2][:, :].rearrange("p (g n) -> p g n",
                                                                      g=NH),
                                        ps_q[tile % 2][:, :].rearrange("p (g n) -> p g n",
                                                                       g=NH),
                                        s_qrstd[:, 0:NH].unsqueeze(2).broadcast_to(
                                            [128, NH, 128]),
                                        ALU.mult).then_inc(VV, 1)

                            @blk.gpsimd
                            def _(g):
                                for d in range(8, ND):
                                    g.dma_start(s_Wq[d][:, :],
                                                Wq_in[d * 128:(d + 1) * 128, :]
                                                ).then_inc(LQ, 16)
                                for tile in range(8):
                                    g.wait_ge(VV, vvE + 2 * tile + 2)
                                    g.dma_start(q_s.ap()[tile * 128:(tile + 1) * 128, :],
                                                s_q[tile % 2][:, :]).then_inc(SQ[tile % 2], 16)

                        tt_E = ttE + 8
                        aa_E = aaE + 16
                        vv_E = vvE + 16
                        SQ_T = [16 * 4, 16 * 4]
            else:
                bdl.close()
            wqs.close()

        vv_F_end = vv_E
        tt_F_end = tt_E
        aa_F_end = aa_E
        if _PH >= 4:
            with ExitStack() as fs, nc.Block() as blk:
                Ef = fs.enter_context
                s_qTd = Ef(nc.sbuf_tensor("s_qTd", [128, NH * 1024], BF16))
                s_Kg = [Ef(nc.sbuf_tensor(f"s_Kg{p_}", [128, NSLOT * TB], BF16))
                        for p_ in range(2)]
                s_Vg = [Ef(nc.sbuf_tensor(f"s_Vg{p_}", [128, NSLOT * TB], BF16))
                        for p_ in range(2)]
                s_kdiag = Ef(nc.sbuf_tensor("s_kdiag", [128, 2 * NKV * TB], BF16))
                s_PT = [Ef(nc.sbuf_tensor(f"s_PT{j}", [128, TB], BF16)) for j in range(4)]
                s_PTm = [Ef(nc.sbuf_tensor(f"s_PTm{j}", [128, TB], BF16)) for j in range(4)]
                s_recip = Ef(nc.sbuf_tensor("s_recip", [1, 1024], F32))
                s_recipb = Ef(nc.sbuf_tensor("s_recipb", [128, 1024], F32))
                s_mbpad = Ef(nc.sbuf_tensor("s_mbpad", [128, 128], BF16))
                s_nbpad = Ef(nc.sbuf_tensor("s_nbpad", [128, 128], BF16))
                s_mcol = Ef(nc.sbuf_tensor("s_mcol", [128, 1], F32))
                s_rmax = Ef(nc.sbuf_tensor("s_rmax", [128, 1], F32))
                s_bmax = Ef(nc.sbuf_tensor("s_bmax", [16, 1], BF16))
                s_nbT = Ef(nc.sbuf_tensor("s_nbT", [33, 128], BF16))
                s_nbu = Ef(nc.sbuf_tensor("s_nbu", [128, NH * NSLOT], F32))
                ps_scores = Ef(nc.psum_tensor("ps_scores", [128, 1024], F32))
                ps_attn = Ef(nc.psum_tensor("ps_attn", [128, 1024], F32))
                ps_sums = Ef(nc.psum_tensor("ps_sums", [1, 1024], F32))
                ps_trb = Ef(nc.psum_tensor("ps_trb", [128, 128], BF16))
                ps_misc = Ef(nc.psum_tensor("ps_misc", [128, 512], F32))

                diag_lin = [(J, h, sq) for J in range(2) for h in range(NH)
                            for sq in range(4)]
                vv = vv_E
                vv += 1; vv_kd = vv
                vv_dmax = {}
                vv_mcol = {}
                for J in range(2):
                    for h in range(NH):
                        for sq in range(4):
                            vv += 1; vv_dmax[(J, h, sq)] = vv
                        vv += 1; vv_mcol[(J, h)] = vv
                vv += 1; vv_bmax0 = vv
                vv += 1; vv_bmax1 = vv
                vv += 1; vv_nbevac = vv
                vv += 1; vv_nbT = vv
                vv += 1; vv_nbu = vv
                vv += 1; vv_kg = {0: vv}
                vv += 1; vv_kg[1] = vv
                ghs = [(g, hh) for g in range(NKV) for hh in range(4)]
                vv_mask = {}
                vv_recip = {}
                vv_epi = {}
                for gi, (g, hh) in enumerate(ghs):
                    for p_ in (0, 8):
                        for j in range(4):
                            vv += 1; vv_mask[(g, hh, p_, j)] = vv
                    vv += 1; vv_recip[(g, hh)] = vv
                    vv += 1; vv_epi[(g, hh)] = vv
                    if hh == 0 and 2 <= g + 1 <= 3:
                        vv += 1; vv_kg[g + 1] = vv
                vv_F_end = vv

                tt = tt_E
                tt_diag = {}
                for J in range(2):
                    for h in range(NH):
                        for sq in range(4):
                            tt += 1; tt_diag[(J, h, sq)] = tt
                tt += 1; tt_tr1 = tt
                tt += 1; tt_sel0 = tt
                tt += 1; tt_sel1 = tt
                tt += 1; tt_tr2 = tt
                tt += 1; tt_usel = tt
                tt_score = {}
                tt_avdone = {}
                tt_repl = {}
                for g, hh in ghs:
                    for p_ in range(NSLOT):
                        for j in range(4):
                            tt += 1; tt_score[(g, hh, p_, j)] = tt
                        tt += 1; tt_avdone[(g, hh, p_)] = tt
                    tt += 1; tt_repl[(g, hh)] = tt
                tt_F_end = tt

                aa = aa_E
                aa_exp = {}
                for g, hh in ghs:
                    for p_ in range(NSLOT):
                        for j in range(4):
                            aa += 1; aa_exp[(g, hh, p_, j)] = aa
                aa_F_end = aa

                @blk.sync
                def _(sy):
                    sy.wait_ge(GKm, GKm_T)
                    rkd = sy.alloc_register("rkd")
                    sy.reg_load(rkd, s_tbl[:1, TBL_KD + 0:TBL_KD + 1])
                    sy.dma_start(bass.AP(ku_s, 0, [[1, TB * 512]]),
                                 bass.AP(k_s, rkd, [[1, TB * 512]])).then_inc(LU, 16)
                    sy.dma_start(bass.AP(ku_s, TB * 512, [[1, 7 * TB * 512]]),
                                 bass.AP(k_s, 0, [[1, 7 * TB * 512]])).then_inc(LU, 16)
                    sy.reg_load(rkd, s_tbl[:1, TBL_KD + 1:TBL_KD + 2])
                    sy.dma_start(bass.AP(ku_s, 8 * TB * 512, [[1, TB * 512]]),
                                 bass.AP(k_s, rkd, [[1, TB * 512]])).then_inc(LU, 16)
                    sy.dma_start(bass.AP(ku_s, 9 * TB * 512, [[1, 15 * TB * 512]]),
                                 bass.AP(k_s, 0, [[1, 15 * TB * 512]])).then_inc(LU, 16)
                    sy.wait_ge(SQ[0], SQ_T[0])
                    sy.wait_ge(SQ[1], SQ_T[1])
                    for h in range(NH):
                        for J in range(2):
                            sy.dma_start_transpose(
                                s_qTd[:, (h * 2 + J) * TB:(h * 2 + J + 1) * TB],
                                q_s.ap()[J * TB:(J + 1) * TB, h * 128:(h + 1) * 128]
                            ).then_inc(LQd, 16)
                    sy.wait_ge(LU, 16 * 4)
                    for J in range(2):
                        for g in range(NKV):
                            sy.dma_start_transpose(
                                s_kdiag[:, (J * NKV + g) * TB:(J * NKV + g + 1) * TB],
                                ku_s.ap()[J * 8 * TB:J * 8 * TB + TB, g * 128:(g + 1) * 128]
                            ).then_inc(LKd, 16)
                    sy.wait_ge(GU, 16 * 4)
                    for g in range(NKV):
                        if g >= 2:
                            sy.wait_ge(TT, tt_avdone[(g - 2, 3, NSLOT - 1)])
                        sy.dma_start_transpose(
                            s_Kg[g % 2][:, :],
                            ku_s.ap()[:, g * 128:(g + 1) * 128]).then_inc(LG[g % 2], 16)
                        for p_ in range(NSLOT):
                            srcp = vu_s.ap()[p_ * TB:(p_ + 1) * TB,
                                             g * 128:(g + 1) * 128].rearrange(
                                "(j p) q -> p j q", p=128)
                            dst = s_Vg[g % 2][:, p_ * TB:(p_ + 1) * TB].rearrange(
                                "p (j q) -> p j q", q=128)
                            sy.dma_start(dst, srcp).then_inc(LG[g % 2], 16)

                @blk.gpsimd
                def _(g_):
                    g_.wait_ge(GKm, GKm_T)
                    rvd = g_.alloc_register("rvd")
                    g_.reg_load(rvd, s_tbl[:1, TBL_KD + 0:TBL_KD + 1])
                    g_.dma_start(bass.AP(vu_s, 0, [[1, TB * 512]]),
                                 bass.AP(v_row_s, rvd, [[1, TB * 512]])).then_inc(GU, 16)
                    g_.dma_start(bass.AP(vu_s, TB * 512, [[1, 7 * TB * 512]]),
                                 bass.AP(v_row_s, 0, [[1, 7 * TB * 512]])).then_inc(GU, 16)
                    g_.reg_load(rvd, s_tbl[:1, TBL_KD + 1:TBL_KD + 2])
                    g_.dma_start(bass.AP(vu_s, 8 * TB * 512, [[1, TB * 512]]),
                                 bass.AP(v_row_s, rvd, [[1, TB * 512]])).then_inc(GU, 16)
                    g_.dma_start(bass.AP(vu_s, 9 * TB * 512, [[1, 15 * TB * 512]]),
                                 bass.AP(v_row_s, 0, [[1, 15 * TB * 512]])).then_inc(GU, 16)

                @blk.tensor
                def _(t):
                    t.wait_ge(GV, gv0)
                    t.wait_ge(LQd, 16 * 32)
                    t.wait_ge(LKd, 16 * 8)
                    t.wait_ge(VV, vv_kd)
                    n = 0
                    for J in range(2):
                        for h in range(NH):
                            g = h // 4
                            for sq in range(4):
                                par = n % 2
                                if n >= 2:
                                    t.wait_ge(VV, vv_dmax[diag_lin[n - 2]])
                                lhs = s_qTd[:, (h * 2 + J) * TB + sq * 128:
                                            (h * 2 + J) * TB + (sq + 1) * 128]
                                rhs = s_kdiag[:, (J * NKV + g) * TB:(J * NKV + g + 1) * TB]
                                t.matmul(ps_scores[:, par * 512:(par + 1) * 512], lhs, rhs,
                                         start=True, stop=True).then_inc(TT, 1)
                                n += 1
                    t.wait_ge(VV, vv_mcol[(1, NH - 1)])
                    t.transpose(ps_trb[:, :], s_mbpad[:, :],
                                s_identb[:, :]).then_inc(TT, 1)
                    t.wait_ge(VV, vv_bmax0)
                    for h in range(NH):
                        mms = t.matmul(ps_misc[:, h:h + 1],
                                       s_negsel[:, h * 128:(h + 1) * 128],
                                       s_bmax[:, :1], start=True, stop=True)
                    mms.then_inc(TT, 1)
                    t.wait_ge(VV, vv_bmax1)
                    for h in range(NH):
                        mms = t.matmul(ps_misc[:, 16 + h:17 + h],
                                       s_negsel[:, h * 128:(h + 1) * 128],
                                       s_bmax[:, :1], start=True, stop=True)
                    mms.then_inc(TT, 1)
                    t.wait_ge(VV, vv_nbevac)
                    t.transpose(ps_trb[:, :], s_nbpad[:, :],
                                s_identb[:, :]).then_inc(TT, 1)
                    t.wait_ge(VV, vv_nbT)
                    t.matmul(ps_misc[:, 32:32 + NH * NSLOT], s_nbT[0:33, :], s_usel[:, :],
                             start=True, stop=True).then_inc(TT, 1)
                    for gi, (g, hh) in enumerate(ghs):
                        h = 4 * g + hh
                        t.wait_ge(LG[g % 2], 16 * 25 * (g // 2 + 1))
                        t.wait_ge(VV, vv_kg[g])
                        if gi >= 1:
                            t.wait_ge(VV, vv_epi[ghs[gi - 1]])
                        for p_ in range(NSLOT):
                            seg = 0 if p_ < 8 else 1
                            half = seg * 512
                            for j in range(4):
                                tl = p_ * 4 + j
                                par = tl % 2
                                if tl >= 2:
                                    pu, pj = divmod(tl - 2, 4)
                                    t.wait_ge(AA, aa_exp[(g, hh, pu, pj)])
                                elif gi >= 1:
                                    pg, ph_ = ghs[gi - 1]
                                    t.wait_ge(AA, aa_exp[(pg, ph_, NSLOT - 1, 2 + j)])
                                else:
                                    t.wait_ge(VV, vv_dmax[diag_lin[126 + j]])
                                t.matmul(ps_scores[:, par * 512:(par + 1) * 512],
                                         s_Kg[g % 2][:, p_ * TB + j * 128:
                                                     p_ * TB + (j + 1) * 128],
                                         s_qTd[:, (h * 2 + seg) * TB:
                                               (h * 2 + seg + 1) * TB],
                                         start=True, stop=True).then_inc(TT, 1)
                            psrc = s_PTm if p_ in (0, 8) else s_PT
                            for j in range(4):
                                if p_ in (0, 8):
                                    t.wait_ge(VV, vv_mask[(g, hh, p_, j)])
                                else:
                                    t.wait_ge(AA, aa_exp[(g, hh, p_, j)])
                                t.matmul(ps_sums[:1, half:half + 512],
                                         s_onesb[:, :1], psrc[j][:, :],
                                         start=(p_ in (0, 8) and j == 0), stop=(j == 3),
                                         skip_group_check=True)
                            for j in range(4):
                                mma = t.matmul(ps_attn[:, half:half + 512],
                                               s_Vg[g % 2][:, p_ * TB + j * 128:
                                                           p_ * TB + (j + 1) * 128],
                                               psrc[j][:, :],
                                               start=(p_ in (0, 8) and j == 0),
                                               stop=(j == 3),
                                               skip_group_check=True)
                            mma.then_inc(TT, 1)
                        t.wait_ge(VV, vv_recip[(g, hh)])
                        t.wait_ge(AA, aa_exp[(g, hh, NSLOT - 1, 3)])
                        t.matmul(ps_scores[:, 0:512], s_onesrf[:1, :], s_recip[:1, 0:512],
                                 start=True, stop=True)
                        t.matmul(ps_scores[:, 512:1024], s_onesrf[:1, :],
                                 s_recip[:1, 512:1024],
                                 start=True, stop=True).then_inc(TT, 1)

                @blk.scalar
                def _(sc):
                    sc.wait_ge(VV, vv_nbu)
                    for gi, (g, hh) in enumerate(ghs):
                        h = 4 * g + hh
                        for p_ in range(NSLOT):
                            if p_ >= 1:
                                sc.wait_ge(TT, tt_avdone[(g, hh, p_ - 1)])
                            elif gi >= 1:
                                pg, ph_ = ghs[gi - 1]
                                sc.wait_ge(TT, tt_avdone[(pg, ph_, NSLOT - 1)])
                            for j in range(4):
                                par = (p_ * 4 + j) % 2
                                sc.wait_ge(TT, tt_score[(g, hh, p_, j)])
                                sc.activation(s_PT[j][:, :],
                                              ps_scores[:, par * 512:(par + 1) * 512],
                                              AF.Exp,
                                              bias=s_nbu[:, h * NSLOT + p_:
                                                         h * NSLOT + p_ + 1]).then_inc(AA, 1)

                @blk.vector
                def _(v):
                    v.wait_ge(LKd, 16 * 8)
                    v.memset(s_mbpad[:, :], 0.0)
                    v.memset(s_nbpad[:, :], 0.0)
                    v.memset(s_nbpad[:, 32:33], 1e30)
                    v.tensor_scalar_mul(s_kdiag[:, :], s_kdiag[:, :],
                                        s_G[:, :1]).then_inc(VV, 1)
                    for J in range(2):
                        for h in range(NH):
                            for sq in range(4):
                                v.wait_ge(TT, tt_diag[(J, h, sq)])
                                par = (((J * NH + h) * 4) + sq) % 2
                                if sq == 0:
                                    vping(v, v.reduce_max(
                                        s_mcol[:, :1],
                                        ps_scores[:, par * 512:(par + 1) * 512], AX.X))
                                    v.engine_nop().then_inc(VV, 1)
                                else:
                                    vping(v, v.reduce_max(
                                        s_rmax[:, :1],
                                        ps_scores[:, par * 512:(par + 1) * 512], AX.X))
                                    v.engine_nop().then_inc(VV, 1)
                                    vping(v, v.tensor_max(s_mcol[:, :1], s_mcol[:, :1],
                                                          s_rmax[:, :1]))
                            v.tensor_copy(s_mbpad[:, J * 32 + h:J * 32 + h + 1],
                                          s_mcol[:, :1]).then_inc(VV, 1)
                    v.wait_ge(TT, tt_tr1)
                    vping(v, v.reduce_max(s_bmax[:, :1], ps_trb[0:16, :], AX.X))
                    v.engine_nop().then_inc(VV, 1)
                    v.wait_ge(TT, tt_sel0)
                    vping(v, v.reduce_max(s_bmax[:, :1], ps_trb[32:48, :], AX.X))
                    v.engine_nop().then_inc(VV, 1)
                    v.wait_ge(TT, tt_sel1)
                    v.tensor_copy(s_nbpad[:, 0:32], ps_misc[:, 0:32]).then_inc(VV, 1)
                    v.wait_ge(TT, tt_tr2)
                    v.tensor_copy(s_nbT[0:33, :], ps_trb[0:33, :]).then_inc(VV, 1)
                    v.wait_ge(TT, tt_usel)
                    v.tensor_copy(s_nbu[:, :],
                                  ps_misc[:, 32:32 + NH * NSLOT]).then_inc(VV, 1)
                    v.wait_ge(LG[0], 16 * 25)
                    v.tensor_scalar_mul(s_Kg[0][:, :], s_Kg[0][:, :],
                                        s_G[:, :1]).then_inc(VV, 1)
                    v.wait_ge(LG[1], 16 * 25)
                    v.tensor_scalar_mul(s_Kg[1][:, :], s_Kg[1][:, :],
                                        s_G[:, :1]).then_inc(VV, 1)
                    for gi, (g, hh) in enumerate(ghs):
                        h = 4 * g + hh
                        for p_ in (0, 8):
                            for j in range(4):
                                v.wait_ge(AA, aa_exp[(g, hh, p_, j)])
                                if j == 0:
                                    if p_ == 8:
                                        v.wait_ge(TT, tt_avdone[(g, hh, 0)])
                                    elif gi >= 1:
                                        pg, ph_ = ghs[gi - 1]
                                        v.wait_ge(TT, tt_avdone[(pg, ph_, NSLOT - 1)])
                                v.tensor_mul(s_PTm[j][:, :], s_PT[j][:, :],
                                             s_masks[:, j * TB:(j + 1) * TB]).then_inc(VV, 1)
                        v.wait_ge(TT, tt_avdone[(g, hh, NSLOT - 1)])
                        if gi >= 1:
                            v.wait_ge(TT, tt_repl[ghs[gi - 1]])
                        vping(v, v.reciprocal(s_recip[:1, :], ps_sums[:1, :]))
                        v.engine_nop().then_inc(VV, 1)
                        v.wait_ge(TT, tt_repl[(g, hh)])
                        vping(v, v.tensor_copy(s_recipb[:, :], ps_scores[:, :]))
                        v.tensor_mul(s_attnT[:, h * 1024:(h + 1) * 1024], ps_attn[:, :],
                                     s_recipb[:, :])
                        vping(v, v.tensor_copy(s_rmax[:, :1],
                                               s_recipb[:, 0:1]))
                        v.engine_nop().then_inc(VV, 1)
                        if hh == 0 and 2 <= g + 1 <= 3:
                            v.wait_ge(LG[(g + 1) % 2], 16 * 25 * ((g + 1) // 2 + 1))
                            v.tensor_scalar_mul(s_Kg[(g + 1) % 2][:, :],
                                                s_Kg[(g + 1) % 2][:, :],
                                                s_G[:, :1]).then_inc(VV, 1)

        if _PH >= 5:
            af2.close()
            tt = tt_F_end
            tt_gu = {}
            tt_b = {}
            for ch in range(2):
                for hc in range(NHC):
                    tt += 1
                    tt_gu[(ch, hc)] = tt
                for dc in range(4):
                    for t_ in range(4):
                        tt += 1
                        tt_b[(ch, dc, t_)] = tt
            aa_silu = {}
            aa = aa_F_end
            for ch in range(2):
                for hc in range(NHC):
                    aa += 1
                    aa_silu[(ch, hc)] = aa
            vv = vv_F_end
            vv_gu = {}
            vv_add = {}
            for ch in range(2):
                for hc in range(NHC):
                    vv += 1
                    vv_gu[(ch, hc)] = vv
                for dc in range(4):
                    for t_ in range(4):
                        vv += 1
                        vv_add[(ch, dc, t_)] = vv
            ldcis = [(ch, dc) for ch in range(2) for dc in range(4)]

            with ExitStack() as gout:
                Eg = gout.enter_context
                s_gu = [Eg(nc.sbuf_tensor(f"s_gu{hc}", [128, TB], BF16))
                        for hc in range(NHC)]
                s_Wo = [[Eg(nc.sbuf_tensor(f"s_Wo{p_}_{a}", [128, 512], BF16))
                         for a in range(ND)] for p_ in range(2)]

                for ch in range(2):
                    with ExitStack() as asc, nc.Block() as blk:
                        Ea_ = asc.enter_context
                        s_xnTg = [Ea_(nc.sbuf_tensor(f"s_xnTg{ch}_{d}", [128, TB], BF16))
                                  for d in range(ND)]
                        s_Wgs = [[Ea_(nc.sbuf_tensor(f"s_Wgs{ch}_{p_}_{d}", [128, 512],
                                                     BF16))
                                  for d in range(ND)] for p_ in range(2)]
                        s_Wus = [[Ea_(nc.sbuf_tensor(f"s_Wus{ch}_{p_}_{d}", [128, 512],
                                                     BF16))
                                  for d in range(ND)] for p_ in range(2)]
                        s_silu = [Ea_(nc.sbuf_tensor(f"s_silu{ch}_{k}", [128, TB], BF16))
                                  for k in range(2)]
                        ps_g = [Ea_(nc.psum_tensor(f"ps_g{ch}_{k}", [128, TB], F32))
                                for k in range(2)]
                        ps_u = [Ea_(nc.psum_tensor(f"ps_u{ch}_{k}", [128, TB], F32))
                                for k in range(2)]

                        @blk.sync
                        def _(sy, ch=ch, s_xnTg=s_xnTg, s_Wgs=s_Wgs, s_Wus=s_Wus):
                            if ch == 0:
                                sy.wait_ge(VV, vv_F_end)
                                sy.wait_ge(SWe, SWe_T)
                                sy.wait_ge(SWo, SWo_T)
                            if ch == 1:
                                sy.wait_ge(TT, tt_b[(0, 3, 3)])
                                sy.wait_ge(VV, vv_add[(0, 3, 3)])
                                sy.wait_ge(GS[0], 16 * 8)
                                sy.wait_ge(GS[1], 16 * 8)
                            for d in range(ND):
                                sy.dma_start_transpose(
                                    s_xnTg[d][:, :],
                                    xn_own.ap()[ch * TB:(ch + 1) * TB,
                                                d * 128:(d + 1) * 128]
                                ).then_inc(LX2, 16)
                            for hq in range(NHQ):
                                par = hq % 2
                                if hq >= 2:
                                    sy.wait_ge(TT, tt_gu[(ch, (hq - 2) * 4 + 3)])
                                for d in range(ND):
                                    sy.dma_start_transpose(
                                        s_Wgs[par][d][:, :],
                                        wg_s.ap()[hq * 512:(hq + 1) * 512,
                                                  d * 128:(d + 1) * 128]
                                    ).then_inc(LS[par], 16)
                                for d in range(ND):
                                    sy.dma_start_transpose(
                                        s_Wus[par][d][:, :],
                                        wu_s.ap()[hq * 512:(hq + 1) * 512,
                                                  d * 128:(d + 1) * 128]
                                    ).then_inc(LS[par], 16)

                        if ch == 1:
                            @blk.gpsimd
                            def _(g, ch=ch):
                                for i in (4, 5):
                                    pch, pdc = ldcis[i - 2]
                                    g.wait_ge(TT, tt_b[(pch, pdc, 3)])
                                    nch, ndc = ldcis[i]
                                    for a in range(ND):
                                        g.dma_start(s_Wo[ndc % 2][a][:, :],
                                                    Wo_in[a * 128:(a + 1) * 128,
                                                          ndc * 512:(ndc + 1) * 512]
                                                    ).then_inc(GO[ndc % 2], 16)

                        @blk.tensor
                        def _(t, ch=ch, s_xnTg=s_xnTg, s_Wgs=s_Wgs, s_Wus=s_Wus,
                              ps_g=ps_g, ps_u=ps_u):
                            if ch == 0:
                                t.wait_ge(VV, vv_F_end)
                            else:
                                t.wait_ge(VV, vv_add[(0, 3, 3)])
                            t.wait_ge(LX2, 16 * ND * (ch + 1))
                            for hq in range(NHQ):
                                par = hq % 2
                                lots = ch * (6 if par == 0 else 5) + hq // 2 + 1
                                t.wait_ge(LS[par], 16 * 32 * lots)
                                for hc4 in range(4):
                                    hc = hq * 4 + hc4
                                    if hc >= 2 or ch == 1:
                                        key = (ch, hc - 2) if hc >= 2 else (0, NHC - 2 + hc)
                                        t.wait_ge(VV, vv_gu[key])
                                    for d in range(ND):
                                        t.matmul(ps_g[hc % 2][:, :],
                                                 s_Wgs[par][d][:, hc4 * 128:(hc4 + 1) * 128],
                                                 s_xnTg[d][:, :], start=(d == 0),
                                                 stop=(d == ND - 1))
                                    for d in range(ND):
                                        mmu = t.matmul(
                                            ps_u[hc % 2][:, :],
                                            s_Wus[par][d][:, hc4 * 128:(hc4 + 1) * 128],
                                            s_xnTg[d][:, :], start=(d == 0),
                                            stop=(d == ND - 1))
                                    mmu.then_inc(TT, 1)

                        @blk.scalar
                        def _(sc, ch=ch, ps_g=ps_g, s_silu=s_silu):
                            for hc in range(NHC):
                                sc.wait_ge(TT, tt_gu[(ch, hc)])
                                sc.activation(s_silu[hc % 2][:, :], ps_g[hc % 2][:, :],
                                              AF.Silu).then_inc(AA, 1)

                        @blk.vector
                        def _(v, ch=ch, ps_u=ps_u, s_silu=s_silu):
                            for hc in range(NHC):
                                v.wait_ge(AA, aa_silu[(ch, hc)])
                                if ch == 1 and hc == 0:
                                    v.wait_ge(TT, tt_b[(0, 3, 3)])
                                v.tensor_mul(s_gu[hc][:, :], ps_u[hc % 2][:, :],
                                             s_silu[hc % 2][:, :]).then_inc(VV, 1)

                    with ExitStack() as bsc, nc.Block() as blk:
                        Eb_ = bsc.enter_context
                        s_WdT = [[Eb_(nc.sbuf_tensor(f"s_WdT{ch}_{p_}_{hc}", [128, 512],
                                                     BF16))
                                  for hc in range(NHC)] for p_ in range(2)]
                        s_xo = [Eb_(nc.sbuf_tensor(f"s_xo{ch}_{k}", [128, 512], F32))
                                for k in range(2)]
                        s_out = [Eb_(nc.sbuf_tensor(f"s_out{ch}_{k}", [128, 512], F32))
                                 for k in range(2)]
                        ps_out = [Eb_(nc.psum_tensor(f"ps_o{ch}_{k}", [128, 512], F32))
                                  for k in range(2)]

                        @blk.sync
                        def _(sy, ch=ch, s_WdT=s_WdT, s_xo=s_xo):
                            sy.wait_ge(TT, tt_gu[(ch, NHC - 1)])
                            sy.wait_ge(VV, vv_gu[(ch, NHC - 1)])
                            for dc in range(4):
                                ldci = 4 * ch + dc
                                if dc >= 2:
                                    pch, pdc = ldcis[ldci - 2]
                                    sy.wait_ge(TT, tt_b[(pch, pdc, 3)])
                                for hc in range(NHC):
                                    sy.dma_start_transpose(
                                        s_WdT[dc % 2][hc][:, :],
                                        wd_s.ap()[dc * 512:(dc + 1) * 512,
                                                  hc * 128:(hc + 1) * 128]
                                    ).then_inc(LDt[dc % 2], 16)
                                for t_ in range(4):
                                    lin = ldci * 4 + t_
                                    if lin >= 2:
                                        pld, pt = divmod(lin - 2, 4)
                                        pch, pdc = ldcis[pld]
                                        sy.wait_ge(VV, vv_add[(pch, pdc, pt)])
                                    sy.dma_start(
                                        s_xo[t_ % 2][:, :],
                                        x_own[(ch * 4 + t_) * 128:(ch * 4 + t_ + 1) * 128,
                                              dc * 512:(dc + 1) * 512]
                                    ).then_inc(LXO[t_ % 2], 16)

                        @blk.gpsimd
                        def _(g, ch=ch, s_out=s_out):
                            if ch == 0:
                                g.wait_ge(VV, vv_F_end)
                                for pre in range(2):
                                    pdc = ldcis[pre][1]
                                    for a in range(ND):
                                        g.dma_start(s_Wo[pdc % 2][a][:, :],
                                                    Wo_in[a * 128:(a + 1) * 128,
                                                          pdc * 512:(pdc + 1) * 512]
                                                    ).then_inc(GO[pdc % 2], 16)
                            for dc in range(4):
                                ldci = 4 * ch + dc
                                for t_ in range(4):
                                    g.wait_ge(VV, vv_add[(ch, dc, t_)])
                                    g.dma_start(
                                        out_own[(ch * 4 + t_) * 128:
                                                (ch * 4 + t_ + 1) * 128,
                                                dc * 512:(dc + 1) * 512],
                                        s_out[t_ % 2][:, :]).then_inc(GS[t_ % 2], 16)
                                if dc < 2 and (ldci + 2) // 4 == ch:
                                    nch, ndc = ldcis[ldci + 2]
                                    g.wait_ge(TT, tt_b[(ch, dc, 3)])
                                    for a in range(ND):
                                        g.dma_start(s_Wo[ndc % 2][a][:, :],
                                                    Wo_in[a * 128:(a + 1) * 128,
                                                          ndc * 512:(ndc + 1) * 512]
                                                    ).then_inc(GO[ndc % 2], 16)
                            if ch == 1:
                                g.wait_ge(GS[0], 16 * 16)
                                g.wait_ge(GS[1], 16 * 16)

                        @blk.tensor
                        def _(t, ch=ch, s_WdT=s_WdT, ps_out=ps_out):
                            t.wait_ge(VV, vv_gu[(ch, NHC - 1)])
                            for dc in range(4):
                                ldci = 4 * ch + dc
                                t.wait_ge(LDt[dc % 2], 16 * NHC * (ch * 2 + dc // 2 + 1))
                                t.wait_ge(GO[dc % 2], 16 * ND * (ch * 2 + dc // 2 + 1))
                                for t_ in range(4):
                                    if ldci * 4 + t_ >= 2:
                                        lin2 = ldci * 4 + t_ - 2
                                        t.wait_ge(GS[lin2 % 2], 16 * (lin2 // 2 + 1))
                                    for a in range(ND):
                                        t.matmul(ps_out[t_ % 2][:, :],
                                                 s_attnT[:,
                                                         a * 1024 + ch * 512 + t_ * 128:
                                                         a * 1024 + ch * 512 +
                                                         (t_ + 1) * 128],
                                                 s_Wo[dc % 2][a][:, :],
                                                 start=(a == 0), stop=False,
                                                 skip_group_check=True)
                                    for hc in range(NHC):
                                        mmd = t.matmul(ps_out[t_ % 2][:, :],
                                                       s_gu[hc][:, t_ * 128:(t_ + 1) * 128],
                                                       s_WdT[dc % 2][hc][:, :],
                                                       start=False, stop=(hc == NHC - 1),
                                                       skip_group_check=True)
                                    mmd.then_inc(TT, 1)

                        @blk.vector
                        def _(v, ch=ch, s_xo=s_xo, s_out=s_out, ps_out=ps_out):
                            for dc in range(4):
                                for t_ in range(4):
                                    lin = (4 * ch + dc) * 4 + t_
                                    v.wait_ge(TT, tt_b[(ch, dc, t_)])
                                    v.wait_ge(LXO[t_ % 2], 16 * (lin // 2 + 1))
                                    if lin >= 2:
                                        v.wait_ge(GS[t_ % 2], 16 * ((lin - 2) // 2 + 1))
                                    v.tensor_add(s_out[t_ % 2][:, :], s_xo[t_ % 2][:, :],
                                                 ps_out[t_ % 2][:, :]).then_inc(VV, 1)

        if _PH < 5:
            af2.close()
            with nc.Block() as blk:
                @blk.gpsimd
                def _(g):
                    g.wait_ge(VV, vv_F_end)
                    g.wait_ge(TT, tt_F_end)
                    g.wait_ge(AA, aa_F_end)
                    g.wait_ge(GV, gv_B)
                    g.wait_ge(SWe, SWe_T)
                    g.wait_ge(SWo, SWo_T)
                    g.wait_ge(SX, SX_T)
                    if _PH >= 2:
                        g.wait_ge(SKs, SK_T)
                        g.wait_ge(SVs, SV_T)
                        g.wait_ge(GXo, GXo_T)
                    if _PH >= 3:
                        g.wait_ge(SQ[0], SQ_T[0])
                        g.wait_ge(SQ[1], SQ_T[1])
                    nd = 0
                    if _PH == 1 and os.environ.get("KD") == "wu":
                        g.dma_start(out_own[0:1024, :],
                                    wu_s.ap()[0:1024, 0:2048]).then_inc(GS[0], 16)
                        nd = 1
                    elif _PH == 1 and os.environ.get("KD") == "wg":
                        g.dma_start(out_own[0:1024, :],
                                    wg_s.ap()[0:1024, 0:2048]).then_inc(GS[0], 16)
                        nd = 1
                    elif _PH == 1:
                        g.dma_start(out_own[0:512, :],
                                    xn_s.ap()[0:512, :]).then_inc(GS[0], 16)
                        g.wait_ge(GS[0], 16)
                        g.dma_start(out_own[512:768, :],
                                    wg_s.ap()[0:256, :]).then_inc(GS[0], 16)
                        g.wait_ge(GS[0], 32)
                        g.dma_start(out_own[768:896, :],
                                    wd_s.ap()[0:128, 0:2048]).then_inc(GS[0], 16)
                        g.wait_ge(GS[0], 48)
                        g.dma_start(out_own[896:1024, :],
                                    wu_s.ap()[0:128, 0:2048]).then_inc(GS[0], 16)
                        nd = 4
                    elif _PH == 2:
                        g.dma_start(out_own[0:1024, 0:512],
                                    k_s.ap()[0:1024, :]).then_inc(GS[0], 16)
                        g.wait_ge(GS[0], 16)
                        g.dma_start(out_own[0:1024, 512:1024],
                                    v_row_s.ap()[0:1024, :]).then_inc(GS[0], 16)
                        g.wait_ge(GS[0], 32)
                        nd = 2
                        if os.environ.get("KXO", "1") != "0":
                            g.dma_start(out_own[0:1024, 1024:2048],
                                        xn_own.ap()[0:1024, 0:1024]).then_inc(GS[0], 16)
                            nd = 3
                    elif _PH == 3:
                        g.dma_start(out_own[:, :], q_s.ap()[:, :]).then_inc(GS[0], 16)
                        nd = 1
                    elif _PH == 4:
                        g.dma_start(bass.AP(out_own.tensor, 0, [[16384, 128], [1, 16384]]),
                                    s_attnT[:, :]).then_inc(GS[0], 16)
                        nd = 1
                    g.wait_ge(GS[0], 16 * nd)

    return nc



def kernel(L0_attn_x, L0_attn_Wq, L0_attn_Wk, L0_attn_Wv, L0_attn_Wo,
           L0_attn_gq, L0_attn_gk, L0_ffn_wq_gate, L0_ffn_s_gate,
           L0_ffn_wq_up, L0_ffn_s_up, L0_ffn_wq_down, L0_ffn_s_down):
    x = np.ascontiguousarray(np.asarray(L0_attn_x, np.float32).reshape(S, D))
    Wq = np.ascontiguousarray(np.asarray(L0_attn_Wq, np.float32))
    Wk = np.ascontiguousarray(np.asarray(L0_attn_Wk, np.float32))
    Wv = np.ascontiguousarray(np.asarray(L0_attn_Wv, np.float32))
    Wo = np.ascontiguousarray(np.asarray(L0_attn_Wo, np.float32))
    gq = np.ascontiguousarray(np.asarray(L0_attn_gq, np.float32).reshape(HD, 1))
    gk = np.ascontiguousarray(np.asarray(L0_attn_gk, np.float32).reshape(HD, 1))
    wg = np.ascontiguousarray(np.asarray(L0_ffn_wq_gate, np.int32).astype(np.uint8))
    wu = np.ascontiguousarray(np.asarray(L0_ffn_wq_up, np.int32).astype(np.uint8))
    wd = np.ascontiguousarray(np.asarray(L0_ffn_wq_down, np.int32).astype(np.uint8))
    sg = np.ascontiguousarray(np.asarray(L0_ffn_s_gate, np.float32).reshape(NHC, 128).T)
    su = np.ascontiguousarray(np.asarray(L0_ffn_s_up, np.float32).reshape(NHC, 128).T)
    sd = np.ascontiguousarray(np.asarray(L0_ffn_s_down, np.float32).reshape(ND, 128).T)

    masks = np.zeros((128, 4 * TB), np.float32)
    for j in range(4):
        for kk in range(128):
            masks[kk, j * TB:(j + 1) * TB] = (j * 128 + kk <= np.arange(TB))
    masks = masks.astype(ml_dtypes.bfloat16)
    negsel = np.zeros((16, 16 * 128), np.float32)
    for h in range(16):
        negsel[h, h * 128:(h + 1) * 128] = -1.0
    negsel = negsel.astype(ml_dtypes.bfloat16)
    identb = np.eye(128, dtype=np.float32).astype(ml_dtypes.bfloat16)

    nc = build_nc()
    in_maps = []
    for core in range(8):
        J_lo, J_hi = core, 15 - core
        tbl = np.zeros((1, 16), np.int32)
        tbl[0, TBL_KD + 0] = J_lo * TB * (NKV * HD)
        tbl[0, TBL_KD + 1] = J_hi * TB * (NKV * HD)
        tbl[0, TBL_XO + 0] = J_lo * TB * D
        tbl[0, TBL_XO + 1] = J_hi * TB * D
        usel = np.zeros((33, NH * NSLOT), np.float32)
        for h in range(NH):
            for p_ in range(NSLOT):
                col = h * NSLOT + p_
                if p_ < 8:
                    real = (p_ == 0) or (p_ - 1 < J_lo)
                    seg = 0
                else:
                    real = (p_ == 8) or (p_ - 9 < J_hi)
                    seg = 1
                if real:
                    usel[seg * 16 + h, col] = 1.0
                else:
                    usel[32, col] = -1.0
        usel = usel.astype(ml_dtypes.bfloat16)
        x_own_c = np.ascontiguousarray(
            np.concatenate([x[J_lo * TB:(J_lo + 1) * TB],
                            x[J_hi * TB:(J_hi + 1) * TB]], axis=0))
        in_maps.append(dict(
            x_full=x, x_own=x_own_c, Wq_in=Wq, Wk_in=Wk, Wv_in=Wv, Wo_in=Wo,
            gq_col=gq, gk_col=gk, wg_i=wg, wu_i=wu, wd_i=wd,
            sg_in=sg, su_in=su, sd_in=sd, masks_in=masks, negsel_in=negsel,
            identb_in=identb, usel_in=usel, tbl_in=tbl,
        ))

    res = run_bass_kernel_spmd(nc, in_maps, core_ids=list(range(8)))
    out = np.zeros((S, D), np.float32)
    for core in range(8):
        J_lo, J_hi = core, 15 - core
        oo = np.asarray(res.results[core]["out_own"], np.float32)
        out[J_lo * TB:(J_lo + 1) * TB] = oo[0:TB]
        out[J_hi * TB:(J_hi + 1) * TB] = oo[TB:2 * TB]
    return out.reshape(1, S, D)


# version 08 (final)
# speedup vs baseline: 1.7300x; 1.0125x over previous
import os
import numpy as np
import ml_dtypes
from contextlib import ExitStack
import concourse.bass as bass
import concourse.mybir as mybir
from concourse.bass_utils import run_bass_kernel_spmd

F32 = mybir.dt.float32
BF16 = mybir.dt.bfloat16
I32 = mybir.dt.int32
U8 = mybir.dt.uint8
AF = mybir.ActivationFunctionType
ALU = mybir.AluOpType
AX = mybir.AxisListType

S, D, NB, TB = 8192, 2048, 16, 512
NT = S // 128
NH, NKV, HD = 16, 4, 128
HID = 5632
NHC = HID // 128
ND = D // 128
EPS_LN, EPS_RMS = 1e-5, 1e-6
NSLOT = 24
NHQ = NHC // 4
INV_SQRT_HD = float(1.0 / np.sqrt(128.0))

_PH = int(os.environ.get("KPHASES", "5"))

TBL_KD = 0
TBL_XO = 2


def build_nc():
    nc = bass.Bass("TRN2", target_bir_lowering=False)

    x_full = nc.dram_tensor("x_full", [S, D], F32, kind="ExternalInput").ap()
    x_own = nc.dram_tensor("x_own", [2 * TB, D], F32, kind="ExternalInput").ap()
    Wq_in = nc.dram_tensor("Wq_in", [D, D], F32, kind="ExternalInput").ap()
    Wk_in = nc.dram_tensor("Wk_in", [D, 512], F32, kind="ExternalInput").ap()
    Wv_in = nc.dram_tensor("Wv_in", [D, 512], F32, kind="ExternalInput").ap()
    Wo_in = nc.dram_tensor("Wo_in", [D, D], F32, kind="ExternalInput").ap()
    gq_col = nc.dram_tensor("gq_col", [HD, 1], F32, kind="ExternalInput").ap()
    gk_col = nc.dram_tensor("gk_col", [HD, 1], F32, kind="ExternalInput").ap()
    wg_i = nc.dram_tensor("wg_i", [HID, D], U8, kind="ExternalInput").ap()
    wu_i = nc.dram_tensor("wu_i", [HID, D], U8, kind="ExternalInput").ap()
    wd_i = nc.dram_tensor("wd_i", [D, HID], U8, kind="ExternalInput").ap()
    sg_in = nc.dram_tensor("sg_in", [128, NHC], F32, kind="ExternalInput").ap()
    su_in = nc.dram_tensor("su_in", [128, NHC], F32, kind="ExternalInput").ap()
    sd_in = nc.dram_tensor("sd_in", [128, ND], F32, kind="ExternalInput").ap()
    masks_in = nc.dram_tensor("masks_in", [128, 4 * TB], BF16, kind="ExternalInput").ap()
    negsel_in = nc.dram_tensor("negsel_in", [16, 16 * 128], BF16, kind="ExternalInput").ap()
    identb_in = nc.dram_tensor("identb_in", [128, 128], BF16, kind="ExternalInput").ap()
    usel_in = nc.dram_tensor("usel_in", [33, NH * NSLOT], BF16, kind="ExternalInput").ap()
    tbl_in = nc.dram_tensor("tbl_in", [1, 16], I32, kind="ExternalInput").ap()
    out_own = nc.dram_tensor("out_own", [2 * TB, D], F32, kind="ExternalOutput").ap()

    xn_s = nc.dram_tensor("xn_s", [S, D], BF16)
    xn_own = nc.dram_tensor("xn_own", [2 * TB, D], BF16)
    q_s = nc.dram_tensor("q_s", [2 * TB, D], BF16)
    k_s = nc.dram_tensor("k_s", [S, NKV * HD], BF16)
    v_row_s = nc.dram_tensor("v_row_s", [S, NKV * HD], BF16)
    ku_s = nc.dram_tensor("ku_s", [NSLOT * TB, NKV * HD], BF16)
    vu_s = nc.dram_tensor("vu_s", [NSLOT * TB, NKV * HD], BF16)
    wg_s = nc.dram_tensor("wg_s", [HID, D], BF16)
    wu_s = nc.dram_tensor("wu_s", [HID, D], BF16)
    wd_s = nc.dram_tensor("wd_s", [D, HID], BF16)

    with ExitStack() as top:
        Et = top.enter_context
        TT = Et(nc.semaphore("tt"))
        AA = Et(nc.semaphore("aa"))
        VV = Et(nc.semaphore("vv"))
        GV = Et(nc.semaphore("gv"))
        GXm = Et(nc.semaphore("gxm"))
        GKm = Et(nc.semaphore("gkm"))
        LA = Et(nc.semaphore("la"))
        LXs = [Et(nc.semaphore(f"lx{i}")) for i in range(2)]
        LC = Et(nc.semaphore("lc"))
        LC2 = Et(nc.semaphore("lc2"))
        LSu = [Et(nc.semaphore(f"lsu{i}")) for i in range(2)]
        SVs = Et(nc.semaphore("svs"))
        LW = Et(nc.semaphore("lw"))
        SWe = Et(nc.semaphore("swe"))
        SWo = Et(nc.semaphore("swo"))
        SX = Et(nc.semaphore("sx"))
        LT = [Et(nc.semaphore(f"lt{i}")) for i in range(2)]
        SKs = Et(nc.semaphore("sk"))
        GXo = Et(nc.semaphore("gxo"))
        LE = Et(nc.semaphore("le"))
        LQ = Et(nc.semaphore("lq"))
        SQ = [Et(nc.semaphore(f"sq{i}")) for i in range(2)]
        LU = Et(nc.semaphore("lu"))
        GU = Et(nc.semaphore("gu2"))
        LQd = Et(nc.semaphore("lqd"))
        LKd = Et(nc.semaphore("lkd"))
        LG = [Et(nc.semaphore(f"lg{i}")) for i in range(2)]
        LX2 = Et(nc.semaphore("lx2"))
        LS = [Et(nc.semaphore(f"ls{i}")) for i in range(2)]
        LDt = [Et(nc.semaphore(f"ldt{i}")) for i in range(2)]
        LXO = [Et(nc.semaphore(f"lxo{i}")) for i in range(2)]
        GO = [Et(nc.semaphore(f"go{i}")) for i in range(2)]
        GS = [Et(nc.semaphore(f"gs{i}")) for i in range(2)]
        PV = Et(nc.semaphore("pv"))
        PA = Et(nc.semaphore("pa"))
        pv_n = [0]
        pa_n = [0]

        def vping(eng, inst):
            pv_n[0] += 1
            inst.then_inc(PV, 1)
            eng.wait_ge(PV, pv_n[0])

        def aping(eng, inst):
            pa_n[0] += 1
            inst.then_inc(PA, 1)
            eng.wait_ge(PA, pa_n[0])

        s_gq = Et(nc.sbuf_tensor("s_gq", [HD, 1], F32))
        s_gk = Et(nc.sbuf_tensor("s_gk", [HD, 1], F32))
        s_G = Et(nc.sbuf_tensor("s_G", [HD, 1], F32))
        s_sg = Et(nc.sbuf_tensor("s_sg", [128, NHC], F32))
        s_su = Et(nc.sbuf_tensor("s_su", [128, NHC], F32))
        s_sd = Et(nc.sbuf_tensor("s_sd", [128, ND], F32))
        s_tbl = Et(nc.sbuf_tensor("s_tbl", [1, 16], I32))
        s_onesb = Et(nc.sbuf_tensor("s_onesb", [128, 1], BF16))
        s_onesrf = Et(nc.sbuf_tensor("s_onesrf", [1, 128], F32))
        s_epsln = Et(nc.sbuf_tensor("s_epsln", [128, 1], F32))
        s_epsrms = Et(nc.sbuf_tensor("s_epsrms", [128, 1], F32))
        s_attnT = Et(nc.sbuf_tensor("s_attnT", [128, NH * 1024], BF16))
        af2 = ExitStack()
        Ea = af2.enter_context
        s_masks = Ea(nc.sbuf_tensor("s_masks", [128, 4 * TB], BF16))
        s_negsel = Ea(nc.sbuf_tensor("s_negsel", [16, 16 * 128], BF16))
        s_identb = Ea(nc.sbuf_tensor("s_identb", [128, 128], BF16))
        s_usel = Ea(nc.sbuf_tensor("s_usel", [33, NH * NSLOT], BF16))

        with nc.Block() as blk:
            @blk.sync
            def _(sy):
                for dst, srcp in [
                    (s_masks[:, :], masks_in), (s_negsel[:, :], negsel_in),
                    (s_identb[:, :], identb_in), (s_usel[:, :], usel_in),
                    (s_gq[:, :], gq_col), (s_gk[:, :], gk_col),
                    (s_sg[:, :], sg_in), (s_su[:, :], su_in), (s_sd[:, :], sd_in),
                    (s_tbl[:, :], tbl_in),
                ]:
                    sy.dma_start(dst, srcp).then_inc(LA, 16)

            @blk.gpsimd
            def _(g):
                g.memset(s_onesb[:, :], 1.0)
                g.memset(s_onesrf[:, :], 1.0)
                g.memset(s_epsln[:, :], EPS_LN)
                g.memset(s_epsrms[:, :], EPS_RMS)
                g.nop().then_inc(GV, 1)

            @blk.vector
            def _(v):
                v.wait_ge(LA, 160)
                vping(v, v.tensor_mul(s_G[:, :], s_gq[:, :], s_gk[:, :]))
                v.tensor_scalar_mul(s_G[:, :], s_G[:, :], INV_SQRT_HD).then_inc(VV, 1)
        LA_T = 160
        vv0 = 1
        gv0 = 1

        with ExitStack() as bd_scope:
            Eb = bd_scope.enter_context
            s_Wk = [Eb(nc.sbuf_tensor(f"s_Wk{d}", [128, 512], BF16)) for d in range(ND)]
            s_Wv = [Eb(nc.sbuf_tensor(f"s_Wv{d}", [128, 512], BF16)) for d in range(ND)]

            c_tiles = []
            for hc in range(NHC):
                c_tiles.append((wg_i[hc * 128:(hc + 1) * 128, :], s_sg[:, hc:hc + 1],
                                wg_s.ap()[hc * 128:(hc + 1) * 128, :], D))
            for hc in range(NHC):
                c_tiles.append((wu_i[hc * 128:(hc + 1) * 128, :], s_su[:, hc:hc + 1],
                                wu_s.ap()[hc * 128:(hc + 1) * 128, :], D))
            for dc in range(ND):
                c_tiles.append((wd_i[dc * 128:(dc + 1) * 128, :], s_sd[:, dc:dc + 1],
                                wd_s.ap()[dc * 128:(dc + 1) * 128, :], HID))
            NC_T = len(c_tiles)

            nci = []
            k = 0
            for i in range(NT):
                take = min(2, NC_T - k)
                nci.append(take)
                k += take
            csum = [0]
            for i in range(NT):
                csum.append(csum[-1] + nci[i])
            assert csum[-1] == NC_T

            vvB, gvB = vv0, gv0

            wqs = ExitStack()
            s_WqA = [wqs.enter_context(nc.sbuf_tensor(f"s_Wq{d}", [128, D], BF16))
                     for d in range(8)]
            bdl = ExitStack()
            Ep = bdl.enter_context
            s_x = [Ep(nc.sbuf_tensor(f"s_x{k}", [128, D], F32)) for k in range(2)]
            s_xn = [Ep(nc.sbuf_tensor(f"s_xn{k}", [128, D], BF16)) for k in range(2)]
            s_sq = Ep(nc.sbuf_tensor("s_sq", [128, D], BF16))
            s_ssq = [Ep(nc.sbuf_tensor(f"s_ssq{k}", [128, 1], F32)) for k in range(2)]
            s_rsum = Ep(nc.sbuf_tensor("s_rsum", [128, 1], F32))
            s_negmean = [Ep(nc.sbuf_tensor(f"s_nm{k}", [128, 1], F32)) for k in range(2)]
            s_m2 = Ep(nc.sbuf_tensor("s_m2", [128, 1], F32))
            s_var = [Ep(nc.sbuf_tensor(f"s_var{k}", [128, 1], F32)) for k in range(2)]
            s_tmp1 = Ep(nc.sbuf_tensor("s_tmp1", [128, 1], F32))
            s_rstd = [Ep(nc.sbuf_tensor(f"s_rstd{k}", [128, 1], F32)) for k in range(2)]
            s_wint = [Ep(nc.sbuf_tensor(f"s_wint{k}", [128, HID], U8))
                      for k in range(2)]
            s_wbf = [Ep(nc.sbuf_tensor(f"s_wbf{k}", [128, HID], BF16)) for k in range(2)]
            s_xnT = [[Ep(nc.sbuf_tensor(f"s_xnT{p_}_{d}", [128, TB], BF16))
                      for d in range(ND)] for p_ in range(2)]
            s_kv = [Ep(nc.sbuf_tensor(f"s_kv{k}", [128, 1024], BF16)) for k in range(2)]
            s_ksq = [Ep(nc.sbuf_tensor(f"s_ksq{k}", [128, 512], F32)) for k in range(2)]
            s_kssq = Ep(nc.sbuf_tensor("s_kssq", [128, NKV], F32))
            s_krstd = Ep(nc.sbuf_tensor("s_krstd", [128, NKV], F32))

            with nc.Block() as blk:
                @blk.sync
                def _(sy):
                    sy.wait_ge(LA, LA_T)
                    for i in range(NT):
                        if i >= 2:
                            sy.wait_ge(GXm, i - 1)
                        sy.dma_start(s_x[i % 2][:, :],
                                     x_full[i * 128:(i + 1) * 128, :]).then_inc(LXs[i % 2], 16)

                @blk.scalar
                def _(sc):
                    sc.wait_ge(GV, gv0)
                    for i in range(NT):
                        for j in range(nci[i]):
                            ci = csum[i] + j
                            if ci >= 2:
                                sc.wait_ge(SWe if ci % 2 == 0 else SWo,
                                           16 * (ci // 2))
                            srcp, scol, dst, width = c_tiles[ci]
                            sc.dma_start(s_wint[ci % 2][:, 0:width],
                                         srcp).then_inc(LC2, 16)
                        sc.wait_ge(LXs[i % 2], 16 * (i // 2 + 1))
                        sc.activation(s_sq[:, :], s_x[i % 2][:, :], AF.Square,
                                      accum_out=s_ssq[i % 2][:, :1]).then_inc(AA, 1)
                        sc.wait_ge(VV, vvB + 2 * i + 1)
                        aping(sc, sc.activation(s_tmp1[:, :1], s_var[i % 2][:, :1], AF.Ln,
                                                bias=s_epsln[:, :1]))
                        sc.activation(s_rstd[i % 2][:, :1], s_tmp1[:, :1], AF.Exp,
                                      scale=-0.5).then_inc(AA, 1)
                        for j in range(nci[i]):
                            ci = csum[i] + j
                            if ci % 2 != 0:
                                continue
                            srcp, scol, dst, width = c_tiles[ci]
                            sc.wait_ge(GV, gvB + ci + 1)
                            sc.dma_start(dst, s_wbf[ci % 2][:, 0:width]).then_inc(SWe, 16)

                @blk.vector
                def _(v):
                    for i in range(NT):
                        v.wait_ge(LXs[i % 2], 16 * (i // 2 + 1))
                        vping(v, v.reduce_sum(s_rsum[:, :1], s_x[i % 2][:, :], AX.X))
                        vping(v, v.tensor_scalar_mul(s_negmean[i % 2][:, :1],
                                                     s_rsum[:, :1], -1.0 / D))
                        v.wait_ge(AA, 2 * i + 1)
                        vping(v, v.tensor_mul(s_m2[:, :1], s_negmean[i % 2][:, :1],
                                              s_negmean[i % 2][:, :1]))
                        v.scalar_tensor_tensor(s_var[i % 2][:, :1], s_ssq[i % 2][:, :1],
                                               1.0 / D, s_m2[:, :1],
                                               ALU.mult, ALU.subtract).then_inc(VV, 1)
                        v.wait_ge(AA, 2 * i + 2)
                        if i >= 2:
                            v.wait_ge(GXm, i - 1)
                        v.tensor_scalar(s_xn[i % 2][:, :], s_x[i % 2][:, :],
                                        s_negmean[i % 2][:, :1], s_rstd[i % 2][:, :1],
                                        ALU.add, ALU.mult).then_inc(VV, 1)
                        for j in range(nci[i]):
                            ci = csum[i] + j
                            srcp, scol, dst, width = c_tiles[ci]
                            v.wait_ge(LC2, 16 * (csum[i] + nci[i]))
                            if ci >= 2:
                                v.wait_ge(SWe if ci % 2 == 0 else SWo,
                                          16 * (ci // 2))
                            vping(v, v.tensor_scalar(
                                s_wbf[ci % 2][:, 0:width],
                                s_wint[ci % 2][:, 0:width],
                                128.0, scol, ALU.subtract, ALU.mult))
                            v.engine_nop().then_inc(GV, 1)

                @blk.gpsimd
                def _(g):
                    g.wait_ge(LA, LA_T)
                    for d in range(ND):
                        g.dma_start(s_Wk[d][:, :], Wk_in[d * 128:(d + 1) * 128, :]).then_inc(LW, 16)
                    for d in range(ND):
                        g.dma_start(s_Wv[d][:, :], Wv_in[d * 128:(d + 1) * 128, :]).then_inc(LW, 16)
                    for i in range(NT):
                        for j in range(nci[i]):
                            ci = csum[i] + j
                            if ci % 2 != 1:
                                continue
                            srcp, scol, dst, width = c_tiles[ci]
                            g.wait_ge(GV, gvB + ci + 1)
                            g.dma_start(dst,
                                        s_wbf[ci % 2][:, 0:width]).then_inc(SWo, 16)
                        g.wait_ge(VV, vvB + 2 * i + 2)
                        g.dma_start(xn_s.ap()[i * 128:(i + 1) * 128, :],
                                    s_xn[i % 2][:, :]).then_inc(SX, 16)
                        g.wait_ge(SX, 16 * (i + 1))
                        g.nop().then_inc(GXm, 1)
                    for d in range(8):
                        g.dma_start(s_WqA[d][:, :],
                                    Wq_in[d * 128:(d + 1) * 128, :]).then_inc(LQ, 16)

            vv_B = vvB + 2 * NT
            aa_B = 2 * NT
            gv_B = gvB + NC_T
            SWe_T = 16 * ((NC_T + 1) // 2)
            SWo_T = 16 * (NC_T // 2)
            SX_T = 16 * NT
            GXm_T = NT
            LW_T = 16 * 32
            SK_T = 0
            GKm_T = 0
            GXo_T = 0
            SQ_T = [0, 0]
            tt_E, aa_E, vv_E = 0, aa_B, vv_B

            if _PH >= 2:
                with ExitStack() as de_scope:
                    ttD, aaD, vvD = 0, aa_B, vv_B
                    NTD = int(os.environ.get("KNTD", str(NT)))
                    NBD = NTD // 4
                    KXO = os.environ.get("KXO", "1") != "0"
                    KDS = int(os.environ.get("KDS", "4"))
                    KDN = int(os.environ.get("KDN", "0"))
                    with ExitStack() as ph, nc.Block() as blk:
                        Ep2 = ph.enter_context
                        ps_kv = [Ep2(nc.psum_tensor(f"ps_kv{k}", [128, 1024], F32))
                                 for k in range(2)]

                        @blk.gpsimd
                        def _(g):
                            for idx in (range(NTD) if KDS >= 4 else []):
                                g.wait_ge(VV, vvD + 2 * idx + 2)
                                g.dma_start(k_s.ap()[idx * 128:(idx + 1) * 128, :],
                                            s_kv[idx % 2][:, 0:512]).then_inc(SKs, 16)
                                g.wait_ge(SKs, 16 * (idx + 1))
                                g.wait_ge(SVs, 16 * (idx + 1))
                                g.nop().then_inc(GKm, 1)
                            g.wait_ge(GXm, GXm_T)
                            rxo = g.alloc_register("rxo")
                            for ch in (range(2) if KXO else []):
                                g.reg_load(rxo, s_tbl[:1, TBL_XO + ch:TBL_XO + ch + 1])
                                g.dma_start(
                                    bass.AP(xn_own, ch * TB * D, [[1, TB * D]]),
                                    bass.AP(xn_s, rxo, [[1, TB * D]])).then_inc(GXo, 16)

                        @blk.sync
                        def _(sy):
                            for c in range(NBD):
                                sy.wait_ge(GXm, 4 * c + 4)
                                if c >= 2 and KDS >= 2:
                                    sy.wait_ge(TT, ttD + 4 * (c - 1))
                                for d in range(ND):
                                    sy.dma_start_transpose(
                                        s_xnT[c % 2][d][:, :],
                                        xn_s.ap()[c * TB:(c + 1) * TB, d * 128:(d + 1) * 128]
                                    ).then_inc(LT[c % 2], 16)

                        @blk.tensor
                        def _(t):
                            if KDS >= 2:
                                t.wait_ge(LW, LW_T)
                            for c in (range(NBD) if KDS >= 2 else []):
                                t.wait_ge(LT[c % 2], 16 * ND * (c // 2 + 1))
                                for tt in range(4):
                                    idx = 4 * c + tt
                                    if idx >= 2 and KDS >= 4:
                                        t.wait_ge(GKm, idx - 1)
                                    for d in range(ND):
                                        lhs = s_xnT[c % 2][d][:, tt * 128:(tt + 1) * 128]
                                        t.matmul(ps_kv[idx % 2][:, 0:512], lhs, s_Wk[d][:, :],
                                                 start=(d == 0), stop=(d == ND - 1))
                                        mmv = t.matmul(ps_kv[idx % 2][:, 512:1024], lhs,
                                                       s_Wv[d][:, :],
                                                       start=(d == 0), stop=(d == ND - 1))
                                    mmv.then_inc(TT, 1)

                        @blk.scalar
                        def _(sc):
                            for idx in (range(NTD) if KDS >= 3 else []):
                                sc.wait_ge(TT, ttD + idx + 1)
                                if idx >= 2:
                                    sc.wait_ge(VV, vvD + 2 * (idx - 2) + 1)
                                sc.activation(s_ksq[idx % 2][:, :], ps_kv[idx % 2][:, 0:512],
                                              AF.Square).then_inc(AA, 1)
                                sc.wait_ge(VV, vvD + 2 * idx + 1)
                                if KDN & 4:
                                    aping(sc, sc.activation(s_krstd[:, 0:NKV],
                                                            s_kssq[:, 0:NKV], AF.Square))
                                    sc.activation(s_krstd[:, 0:NKV], s_krstd[:, 0:NKV],
                                                  AF.Square).then_inc(AA, 1)
                                else:
                                    aping(sc, sc.activation(s_krstd[:, 0:NKV], s_kssq[:, 0:NKV],
                                                            AF.Ln, scale=1.0 / HD,
                                                            bias=s_epsrms[:, :1]))
                                    sc.activation(s_krstd[:, 0:NKV], s_krstd[:, 0:NKV], AF.Exp,
                                                  scale=-0.5).then_inc(AA, 1)
                                if KDS >= 4:
                                    sc.wait_ge(VV, vvD + 2 * idx + 2)
                                    sc.dma_start(
                                        v_row_s.ap()[idx * 128:(idx + 1) * 128, :],
                                        s_kv[idx % 2][:, 512:1024]).then_inc(SVs, 16)

                        @blk.vector
                        def _(v):
                            for idx in (range(NTD) if KDS >= 3 else []):
                                v.wait_ge(AA, aaD + 2 * idx + 1)
                                if KDN & 2:
                                    v.reduce_sum(s_kssq[:, 0:1], s_ksq[idx % 2][:, :],
                                                 AX.X).then_inc(VV, 1)
                                else:
                                    v.tensor_reduce(
                                        s_kssq[:, 0:NKV],
                                        s_ksq[idx % 2][:, :].rearrange("p (g n) -> p g n",
                                                                       g=NKV),
                                        AX.X, ALU.add).then_inc(VV, 1)
                                v.wait_ge(AA, aaD + 2 * idx + 2)
                                if idx >= 2 and KDS >= 4:
                                    v.wait_ge(GKm, idx - 1)
                                if KDN & 1:
                                    v.tensor_copy(s_kv[idx % 2][:, 0:512],
                                                  ps_kv[idx % 2][:, 0:512])
                                else:
                                    v.tensor_tensor(
                                        s_kv[idx % 2][:, 0:512].rearrange("p (g n) -> p g n",
                                                                          g=NKV),
                                        ps_kv[idx % 2][:, 0:512].rearrange("p (g n) -> p g n",
                                                                           g=NKV),
                                        s_krstd[:, 0:NKV].unsqueeze(2).broadcast_to(
                                            [128, NKV, 128]),
                                        ALU.mult)
                                v.tensor_copy(s_kv[idx % 2][:, 512:1024],
                                              ps_kv[idx % 2][:, 512:1024]).then_inc(VV, 1)

                    tt_D = ttD + (NTD if KDS >= 2 else 0)
                    aa_D = aaD + (2 * NTD if KDS >= 3 else 0)
                    vv_D = vvD + (2 * NTD if KDS >= 3 else 0)
                    SK_T = 16 * NTD if KDS >= 4 else 0
                    SV_T = 16 * NTD if KDS >= 4 else 0
                    GKm_T = NTD if KDS >= 4 else 0
                    GXo_T = 16 * 2 if KXO else 0
                    bdl.close()
                    tt_E, aa_E, vv_E = tt_D, aa_D, vv_D

                    if _PH >= 3:
                        ttE, aaE, vvE = tt_D, aa_D, vv_D
                        with ExitStack() as ph, nc.Block() as blk:
                            Ep3 = ph.enter_context
                            s_WqB = [Ep3(nc.sbuf_tensor(f"s_Wqb{d}", [128, D], BF16))
                                     for d in range(8)]
                            s_Wq = s_WqA + s_WqB
                            s_xnTe = [Ep3(nc.sbuf_tensor(f"s_xnTe{d}", [128, TB], BF16))
                                      for d in range(ND)]
                            s_q = [Ep3(nc.sbuf_tensor(f"s_q{k}", [128, D], BF16))
                                   for k in range(2)]
                            s_qsq = Ep3(nc.sbuf_tensor("s_qsq", [128, D], F32))
                            s_qssq = Ep3(nc.sbuf_tensor("s_qssq", [128, NH], F32))
                            s_qrstd = Ep3(nc.sbuf_tensor("s_qrstd", [128, NH], F32))
                            ps_q = [Ep3(nc.psum_tensor(f"ps_q{k}", [128, D], F32))
                                    for k in range(2)]

                            @blk.sync
                            def _(sy):
                                sy.wait_ge(GXo, GXo_T)
                                for ch in range(2):
                                    if ch == 1:
                                        sy.wait_ge(TT, ttE + 4)
                                    for d in range(ND):
                                        sy.dma_start_transpose(
                                            s_xnTe[d][:, :],
                                            xn_own.ap()[ch * TB:(ch + 1) * TB,
                                                        d * 128:(d + 1) * 128]
                                        ).then_inc(LE, 16)

                            @blk.tensor
                            def _(t):
                                t.wait_ge(LQ, 16 * ND)
                                t.wait_ge(GKm, GKm_T)
                                for ch in range(2):
                                    t.wait_ge(LE, 16 * ND * (ch + 1))
                                    for tt in range(4):
                                        tile = 4 * ch + tt
                                        if tile >= 2:
                                            t.wait_ge(SQ[tile % 2],
                                                      16 * (tile // 2))
                                        for d in range(ND):
                                            lhs = s_xnTe[d][:, tt * 128:(tt + 1) * 128]
                                            for qc in range(4):
                                                mmq = t.matmul(
                                                    ps_q[tile % 2][:, qc * 512:(qc + 1) * 512],
                                                    lhs,
                                                    s_Wq[d][:, qc * 512:(qc + 1) * 512],
                                                    start=(d == 0), stop=(d == ND - 1))
                                        mmq.then_inc(TT, 1)

                            @blk.scalar
                            def _(sc):
                                for tile in range(8):
                                    sc.wait_ge(TT, ttE + tile + 1)
                                    sc.activation(s_qsq[:, :], ps_q[tile % 2][:, :],
                                                  AF.Square).then_inc(AA, 1)
                                    sc.wait_ge(VV, vvE + 2 * tile + 1)
                                    aping(sc, sc.activation(s_qrstd[:, 0:NH], s_qssq[:, 0:NH],
                                                            AF.Ln, scale=1.0 / HD,
                                                            bias=s_epsrms[:, :1]))
                                    sc.activation(s_qrstd[:, 0:NH], s_qrstd[:, 0:NH], AF.Exp,
                                                  scale=-0.5).then_inc(AA, 1)

                            @blk.vector
                            def _(v):
                                for tile in range(8):
                                    v.wait_ge(AA, aaE + 2 * tile + 1)
                                    v.tensor_reduce(
                                        s_qssq[:, 0:NH],
                                        s_qsq[:, :].rearrange("p (g n) -> p g n", g=NH),
                                        AX.X, ALU.add).then_inc(VV, 1)
                                    v.wait_ge(AA, aaE + 2 * tile + 2)
                                    if tile >= 2:
                                        v.wait_ge(SQ[tile % 2], 16 * (tile // 2))
                                    v.tensor_tensor(
                                        s_q[tile % 2][:, :].rearrange("p (g n) -> p g n",
                                                                      g=NH),
                                        ps_q[tile % 2][:, :].rearrange("p (g n) -> p g n",
                                                                       g=NH),
                                        s_qrstd[:, 0:NH].unsqueeze(2).broadcast_to(
                                            [128, NH, 128]),
                                        ALU.mult).then_inc(VV, 1)

                            @blk.gpsimd
                            def _(g):
                                for d in range(8, ND):
                                    g.dma_start(s_Wq[d][:, :],
                                                Wq_in[d * 128:(d + 1) * 128, :]
                                                ).then_inc(LQ, 16)
                                for tile in range(8):
                                    g.wait_ge(VV, vvE + 2 * tile + 2)
                                    g.dma_start(q_s.ap()[tile * 128:(tile + 1) * 128, :],
                                                s_q[tile % 2][:, :]).then_inc(SQ[tile % 2], 16)

                        tt_E = ttE + 8
                        aa_E = aaE + 16
                        vv_E = vvE + 16
                        SQ_T = [16 * 4, 16 * 4]
            else:
                bdl.close()
            wqs.close()

        vv_F_end = vv_E
        tt_F_end = tt_E
        aa_F_end = aa_E
        if _PH >= 4:
            with ExitStack() as fs, nc.Block() as blk:
                Ef = fs.enter_context
                s_qTd = Ef(nc.sbuf_tensor("s_qTd", [128, NH * 1024], BF16))
                s_Kg = [Ef(nc.sbuf_tensor(f"s_Kg{p_}", [128, NSLOT * TB], BF16))
                        for p_ in range(2)]
                s_Vg = [Ef(nc.sbuf_tensor(f"s_Vg{p_}", [128, NSLOT * TB], BF16))
                        for p_ in range(2)]
                s_kdiag = Ef(nc.sbuf_tensor("s_kdiag", [128, 2 * NKV * TB], BF16))
                s_PT = [Ef(nc.sbuf_tensor(f"s_PT{j}", [128, TB], BF16)) for j in range(4)]
                s_PTm = [Ef(nc.sbuf_tensor(f"s_PTm{j}", [128, TB], BF16)) for j in range(4)]
                s_recip = Ef(nc.sbuf_tensor("s_recip", [1, 1024], F32))
                s_recipb = Ef(nc.sbuf_tensor("s_recipb", [128, 1024], F32))
                s_mbpad = Ef(nc.sbuf_tensor("s_mbpad", [128, 128], BF16))
                s_nbpad = Ef(nc.sbuf_tensor("s_nbpad", [128, 128], BF16))
                s_mcol = Ef(nc.sbuf_tensor("s_mcol", [128, 1], F32))
                s_rmax = Ef(nc.sbuf_tensor("s_rmax", [128, 1], F32))
                s_bmax = Ef(nc.sbuf_tensor("s_bmax", [16, 1], BF16))
                s_nbT = Ef(nc.sbuf_tensor("s_nbT", [33, 128], BF16))
                s_nbu = Ef(nc.sbuf_tensor("s_nbu", [128, NH * NSLOT], F32))
                ps_scores = Ef(nc.psum_tensor("ps_scores", [128, 1024], F32))
                ps_attn = Ef(nc.psum_tensor("ps_attn", [128, 1024], F32))
                ps_sums = Ef(nc.psum_tensor("ps_sums", [1, 1024], F32))
                ps_trb = Ef(nc.psum_tensor("ps_trb", [128, 128], BF16))
                ps_misc = Ef(nc.psum_tensor("ps_misc", [128, 512], F32))

                diag_lin = [(J, h, sq) for J in range(2) for h in range(NH)
                            for sq in range(4)]
                vv = vv_E
                vv += 1; vv_kd = vv
                vv_dmax = {}
                vv_mcol = {}
                for J in range(2):
                    for h in range(NH):
                        for sq in range(4):
                            vv += 1; vv_dmax[(J, h, sq)] = vv
                        vv += 1; vv_mcol[(J, h)] = vv
                vv += 1; vv_bmax0 = vv
                vv += 1; vv_bmax1 = vv
                vv += 1; vv_nbevac = vv
                vv += 1; vv_nbT = vv
                vv += 1; vv_nbu = vv
                vv += 1; vv_kg = {0: vv}
                vv += 1; vv_kg[1] = vv
                ghs = [(g, hh) for g in range(NKV) for hh in range(4)]
                vv_mask = {}
                vv_recip = {}
                vv_epi = {}
                for gi, (g, hh) in enumerate(ghs):
                    for p_ in (0, 8):
                        for j in range(4):
                            vv += 1; vv_mask[(g, hh, p_, j)] = vv
                    vv += 1; vv_recip[(g, hh)] = vv
                    vv += 1; vv_epi[(g, hh)] = vv
                    if hh == 0 and 2 <= g + 1 <= 3:
                        vv += 1; vv_kg[g + 1] = vv
                vv_F_end = vv

                tt = tt_E
                tt_diag = {}
                for J in range(2):
                    for h in range(NH):
                        for sq in range(4):
                            tt += 1; tt_diag[(J, h, sq)] = tt
                tt += 1; tt_tr1 = tt
                tt += 1; tt_sel0 = tt
                tt += 1; tt_sel1 = tt
                tt += 1; tt_tr2 = tt
                tt += 1; tt_usel = tt
                tt_score = {}
                tt_avdone = {}
                tt_repl = {}
                for g, hh in ghs:
                    for p_ in range(NSLOT):
                        for j in range(4):
                            tt += 1; tt_score[(g, hh, p_, j)] = tt
                        tt += 1; tt_avdone[(g, hh, p_)] = tt
                    tt += 1; tt_repl[(g, hh)] = tt
                tt_F_end = tt

                aa = aa_E
                aa_exp = {}
                for g, hh in ghs:
                    for p_ in range(NSLOT):
                        for j in range(4):
                            aa += 1; aa_exp[(g, hh, p_, j)] = aa
                aa_F_end = aa

                @blk.sync
                def _(sy):
                    sy.wait_ge(GKm, GKm_T)
                    rkd = sy.alloc_register("rkd")
                    sy.reg_load(rkd, s_tbl[:1, TBL_KD + 0:TBL_KD + 1])
                    sy.dma_start(bass.AP(ku_s, 0, [[1, TB * 512]]),
                                 bass.AP(k_s, rkd, [[1, TB * 512]])).then_inc(LU, 16)
                    sy.dma_start(bass.AP(ku_s, TB * 512, [[1, 7 * TB * 512]]),
                                 bass.AP(k_s, 0, [[1, 7 * TB * 512]])).then_inc(LU, 16)
                    sy.reg_load(rkd, s_tbl[:1, TBL_KD + 1:TBL_KD + 2])
                    sy.dma_start(bass.AP(ku_s, 8 * TB * 512, [[1, TB * 512]]),
                                 bass.AP(k_s, rkd, [[1, TB * 512]])).then_inc(LU, 16)
                    sy.dma_start(bass.AP(ku_s, 9 * TB * 512, [[1, 15 * TB * 512]]),
                                 bass.AP(k_s, 0, [[1, 15 * TB * 512]])).then_inc(LU, 16)
                    sy.wait_ge(SQ[0], SQ_T[0])
                    sy.wait_ge(SQ[1], SQ_T[1])
                    for h in range(NH):
                        for J in range(2):
                            sy.dma_start_transpose(
                                s_qTd[:, (h * 2 + J) * TB:(h * 2 + J + 1) * TB],
                                q_s.ap()[J * TB:(J + 1) * TB, h * 128:(h + 1) * 128]
                            ).then_inc(LQd, 16)
                    sy.wait_ge(LU, 16 * 4)
                    for J in range(2):
                        for g in range(NKV):
                            sy.dma_start_transpose(
                                s_kdiag[:, (J * NKV + g) * TB:(J * NKV + g + 1) * TB],
                                ku_s.ap()[J * 8 * TB:J * 8 * TB + TB, g * 128:(g + 1) * 128]
                            ).then_inc(LKd, 16)
                    sy.wait_ge(GU, 16 * 4)
                    for g in range(NKV):
                        if g >= 2:
                            sy.wait_ge(TT, tt_avdone[(g - 2, 3, NSLOT - 1)])
                        sy.dma_start_transpose(
                            s_Kg[g % 2][:, :],
                            ku_s.ap()[:, g * 128:(g + 1) * 128]).then_inc(LG[g % 2], 16)
                        for p_ in range(NSLOT):
                            srcp = vu_s.ap()[p_ * TB:(p_ + 1) * TB,
                                             g * 128:(g + 1) * 128].rearrange(
                                "(j p) q -> p j q", p=128)
                            dst = s_Vg[g % 2][:, p_ * TB:(p_ + 1) * TB].rearrange(
                                "p (j q) -> p j q", q=128)
                            sy.dma_start(dst, srcp).then_inc(LG[g % 2], 16)

                @blk.gpsimd
                def _(g_):
                    g_.wait_ge(GKm, GKm_T)
                    rvd = g_.alloc_register("rvd")
                    g_.reg_load(rvd, s_tbl[:1, TBL_KD + 0:TBL_KD + 1])
                    g_.dma_start(bass.AP(vu_s, 0, [[1, TB * 512]]),
                                 bass.AP(v_row_s, rvd, [[1, TB * 512]])).then_inc(GU, 16)
                    g_.dma_start(bass.AP(vu_s, TB * 512, [[1, 7 * TB * 512]]),
                                 bass.AP(v_row_s, 0, [[1, 7 * TB * 512]])).then_inc(GU, 16)
                    g_.reg_load(rvd, s_tbl[:1, TBL_KD + 1:TBL_KD + 2])
                    g_.dma_start(bass.AP(vu_s, 8 * TB * 512, [[1, TB * 512]]),
                                 bass.AP(v_row_s, rvd, [[1, TB * 512]])).then_inc(GU, 16)
                    g_.dma_start(bass.AP(vu_s, 9 * TB * 512, [[1, 15 * TB * 512]]),
                                 bass.AP(v_row_s, 0, [[1, 15 * TB * 512]])).then_inc(GU, 16)

                @blk.tensor
                def _(t):
                    t.wait_ge(GV, gv0)
                    t.wait_ge(LQd, 16 * 32)
                    t.wait_ge(LKd, 16 * 8)
                    t.wait_ge(VV, vv_kd)
                    n = 0
                    for J in range(2):
                        for h in range(NH):
                            g = h // 4
                            for sq in range(4):
                                par = n % 2
                                if n >= 2:
                                    t.wait_ge(VV, vv_dmax[diag_lin[n - 2]])
                                lhs = s_qTd[:, (h * 2 + J) * TB + sq * 128:
                                            (h * 2 + J) * TB + (sq + 1) * 128]
                                rhs = s_kdiag[:, (J * NKV + g) * TB:(J * NKV + g + 1) * TB]
                                t.matmul(ps_scores[:, par * 512:(par + 1) * 512], lhs, rhs,
                                         start=True, stop=True).then_inc(TT, 1)
                                n += 1
                    t.wait_ge(VV, vv_mcol[(1, NH - 1)])
                    t.transpose(ps_trb[:, :], s_mbpad[:, :],
                                s_identb[:, :]).then_inc(TT, 1)
                    t.wait_ge(VV, vv_bmax0)
                    for h in range(NH):
                        mms = t.matmul(ps_misc[:, h:h + 1],
                                       s_negsel[:, h * 128:(h + 1) * 128],
                                       s_bmax[:, :1], start=True, stop=True)
                    mms.then_inc(TT, 1)
                    t.wait_ge(VV, vv_bmax1)
                    for h in range(NH):
                        mms = t.matmul(ps_misc[:, 16 + h:17 + h],
                                       s_negsel[:, h * 128:(h + 1) * 128],
                                       s_bmax[:, :1], start=True, stop=True)
                    mms.then_inc(TT, 1)
                    t.wait_ge(VV, vv_nbevac)
                    t.transpose(ps_trb[:, :], s_nbpad[:, :],
                                s_identb[:, :]).then_inc(TT, 1)
                    t.wait_ge(VV, vv_nbT)
                    t.matmul(ps_misc[:, 32:32 + NH * NSLOT], s_nbT[0:33, :], s_usel[:, :],
                             start=True, stop=True).then_inc(TT, 1)
                    for gi, (g, hh) in enumerate(ghs):
                        h = 4 * g + hh
                        t.wait_ge(LG[g % 2], 16 * 25 * (g // 2 + 1))
                        t.wait_ge(VV, vv_kg[g])
                        if gi >= 1:
                            t.wait_ge(VV, vv_epi[ghs[gi - 1]])
                        for p_ in range(NSLOT):
                            seg = 0 if p_ < 8 else 1
                            half = seg * 512
                            for j in range(4):
                                tl = p_ * 4 + j
                                par = tl % 2
                                if tl >= 2:
                                    pu, pj = divmod(tl - 2, 4)
                                    t.wait_ge(AA, aa_exp[(g, hh, pu, pj)])
                                elif gi >= 1:
                                    pg, ph_ = ghs[gi - 1]
                                    t.wait_ge(AA, aa_exp[(pg, ph_, NSLOT - 1, 2 + j)])
                                else:
                                    t.wait_ge(VV, vv_dmax[diag_lin[126 + j]])
                                t.matmul(ps_scores[:, par * 512:(par + 1) * 512],
                                         s_Kg[g % 2][:, p_ * TB + j * 128:
                                                     p_ * TB + (j + 1) * 128],
                                         s_qTd[:, (h * 2 + seg) * TB:
                                               (h * 2 + seg + 1) * TB],
                                         start=True, stop=True).then_inc(TT, 1)
                            psrc = s_PTm if p_ in (0, 8) else s_PT
                            for j in range(4):
                                if p_ in (0, 8):
                                    t.wait_ge(VV, vv_mask[(g, hh, p_, j)])
                                else:
                                    t.wait_ge(AA, aa_exp[(g, hh, p_, j)])
                                t.matmul(ps_sums[:1, half:half + 512],
                                         s_onesb[:, :1], psrc[j][:, :],
                                         start=(p_ in (0, 8) and j == 0), stop=(j == 3),
                                         skip_group_check=True)
                            for j in range(4):
                                mma = t.matmul(ps_attn[:, half:half + 512],
                                               s_Vg[g % 2][:, p_ * TB + j * 128:
                                                           p_ * TB + (j + 1) * 128],
                                               psrc[j][:, :],
                                               start=(p_ in (0, 8) and j == 0),
                                               stop=(j == 3),
                                               skip_group_check=True)
                            mma.then_inc(TT, 1)
                        t.wait_ge(VV, vv_recip[(g, hh)])
                        t.wait_ge(AA, aa_exp[(g, hh, NSLOT - 1, 3)])
                        t.matmul(ps_scores[:, 0:512], s_onesrf[:1, :], s_recip[:1, 0:512],
                                 start=True, stop=True)
                        t.matmul(ps_scores[:, 512:1024], s_onesrf[:1, :],
                                 s_recip[:1, 512:1024],
                                 start=True, stop=True).then_inc(TT, 1)

                @blk.scalar
                def _(sc):
                    sc.wait_ge(VV, vv_nbu)
                    for gi, (g, hh) in enumerate(ghs):
                        h = 4 * g + hh
                        for p_ in range(NSLOT):
                            if p_ >= 1:
                                sc.wait_ge(TT, tt_avdone[(g, hh, p_ - 1)])
                            elif gi >= 1:
                                pg, ph_ = ghs[gi - 1]
                                sc.wait_ge(TT, tt_avdone[(pg, ph_, NSLOT - 1)])
                            for j in range(4):
                                par = (p_ * 4 + j) % 2
                                sc.wait_ge(TT, tt_score[(g, hh, p_, j)])
                                sc.activation(s_PT[j][:, :],
                                              ps_scores[:, par * 512:(par + 1) * 512],
                                              AF.Exp,
                                              bias=s_nbu[:, h * NSLOT + p_:
                                                         h * NSLOT + p_ + 1]).then_inc(AA, 1)

                @blk.vector
                def _(v):
                    v.wait_ge(LKd, 16 * 8)
                    v.memset(s_mbpad[:, :], 0.0)
                    v.memset(s_nbpad[:, :], 0.0)
                    v.memset(s_nbpad[:, 32:33], 1e30)
                    v.tensor_scalar_mul(s_kdiag[:, :], s_kdiag[:, :],
                                        s_G[:, :1]).then_inc(VV, 1)
                    for J in range(2):
                        for h in range(NH):
                            for sq in range(4):
                                v.wait_ge(TT, tt_diag[(J, h, sq)])
                                par = (((J * NH + h) * 4) + sq) % 2
                                if sq == 0:
                                    vping(v, v.reduce_max(
                                        s_mcol[:, :1],
                                        ps_scores[:, par * 512:(par + 1) * 512], AX.X))
                                    v.engine_nop().then_inc(VV, 1)
                                else:
                                    vping(v, v.reduce_max(
                                        s_rmax[:, :1],
                                        ps_scores[:, par * 512:(par + 1) * 512], AX.X))
                                    v.engine_nop().then_inc(VV, 1)
                                    vping(v, v.tensor_max(s_mcol[:, :1], s_mcol[:, :1],
                                                          s_rmax[:, :1]))
                            v.tensor_copy(s_mbpad[:, J * 32 + h:J * 32 + h + 1],
                                          s_mcol[:, :1]).then_inc(VV, 1)
                    v.wait_ge(TT, tt_tr1)
                    vping(v, v.reduce_max(s_bmax[:, :1], ps_trb[0:16, :], AX.X))
                    v.engine_nop().then_inc(VV, 1)
                    v.wait_ge(TT, tt_sel0)
                    vping(v, v.reduce_max(s_bmax[:, :1], ps_trb[32:48, :], AX.X))
                    v.engine_nop().then_inc(VV, 1)
                    v.wait_ge(TT, tt_sel1)
                    v.tensor_copy(s_nbpad[:, 0:32], ps_misc[:, 0:32]).then_inc(VV, 1)
                    v.wait_ge(TT, tt_tr2)
                    v.tensor_copy(s_nbT[0:33, :], ps_trb[0:33, :]).then_inc(VV, 1)
                    v.wait_ge(TT, tt_usel)
                    v.tensor_copy(s_nbu[:, :],
                                  ps_misc[:, 32:32 + NH * NSLOT]).then_inc(VV, 1)
                    v.wait_ge(LG[0], 16 * 25)
                    v.tensor_scalar_mul(s_Kg[0][:, :], s_Kg[0][:, :],
                                        s_G[:, :1]).then_inc(VV, 1)
                    v.wait_ge(LG[1], 16 * 25)
                    v.tensor_scalar_mul(s_Kg[1][:, :], s_Kg[1][:, :],
                                        s_G[:, :1]).then_inc(VV, 1)
                    for gi, (g, hh) in enumerate(ghs):
                        h = 4 * g + hh
                        for p_ in (0, 8):
                            for j in range(4):
                                v.wait_ge(AA, aa_exp[(g, hh, p_, j)])
                                if j == 0:
                                    if p_ == 8:
                                        v.wait_ge(TT, tt_avdone[(g, hh, 0)])
                                    elif gi >= 1:
                                        pg, ph_ = ghs[gi - 1]
                                        v.wait_ge(TT, tt_avdone[(pg, ph_, NSLOT - 1)])
                                v.tensor_mul(s_PTm[j][:, :], s_PT[j][:, :],
                                             s_masks[:, j * TB:(j + 1) * TB]).then_inc(VV, 1)
                        v.wait_ge(TT, tt_avdone[(g, hh, NSLOT - 1)])
                        if gi >= 1:
                            v.wait_ge(TT, tt_repl[ghs[gi - 1]])
                        vping(v, v.reciprocal(s_recip[:1, :], ps_sums[:1, :]))
                        v.engine_nop().then_inc(VV, 1)
                        v.wait_ge(TT, tt_repl[(g, hh)])
                        vping(v, v.tensor_copy(s_recipb[:, :], ps_scores[:, :]))
                        v.tensor_mul(s_attnT[:, h * 1024:(h + 1) * 1024], ps_attn[:, :],
                                     s_recipb[:, :])
                        vping(v, v.tensor_copy(s_rmax[:, :1],
                                               s_recipb[:, 0:1]))
                        v.engine_nop().then_inc(VV, 1)
                        if hh == 0 and 2 <= g + 1 <= 3:
                            v.wait_ge(LG[(g + 1) % 2], 16 * 25 * ((g + 1) // 2 + 1))
                            v.tensor_scalar_mul(s_Kg[(g + 1) % 2][:, :],
                                                s_Kg[(g + 1) % 2][:, :],
                                                s_G[:, :1]).then_inc(VV, 1)

        if _PH >= 5:
            af2.close()
            tt = tt_F_end
            tt_gu = {}
            tt_b = {}
            for ch in range(2):
                for hc in range(NHC):
                    tt += 1
                    tt_gu[(ch, hc)] = tt
                for dc in range(4):
                    for t_ in range(4):
                        tt += 1
                        tt_b[(ch, dc, t_)] = tt
            aa_silu = {}
            aa = aa_F_end
            for ch in range(2):
                for hc in range(NHC):
                    aa += 1
                    aa_silu[(ch, hc)] = aa
            vv = vv_F_end
            vv_gu = {}
            vv_add = {}
            for ch in range(2):
                for hc in range(NHC):
                    vv += 1
                    vv_gu[(ch, hc)] = vv
                for dc in range(4):
                    for t_ in range(4):
                        vv += 1
                        vv_add[(ch, dc, t_)] = vv
            ldcis = [(ch, dc) for ch in range(2) for dc in range(4)]

            with ExitStack() as gout:
                Eg = gout.enter_context
                s_gu = [Eg(nc.sbuf_tensor(f"s_gu{hc}", [128, TB], BF16))
                        for hc in range(NHC)]
                s_Wo = [[Eg(nc.sbuf_tensor(f"s_Wo{p_}_{a}", [128, 512], BF16))
                         for a in range(ND)] for p_ in range(2)]

                for ch in range(2):
                    with ExitStack() as asc, nc.Block() as blk:
                        Ea_ = asc.enter_context
                        s_xnTg = [Ea_(nc.sbuf_tensor(f"s_xnTg{ch}_{d}", [128, TB], BF16))
                                  for d in range(ND)]
                        s_Wgs = [[Ea_(nc.sbuf_tensor(f"s_Wgs{ch}_{p_}_{d}", [128, 512],
                                                     BF16))
                                  for d in range(ND)] for p_ in range(2)]
                        s_Wus = [[Ea_(nc.sbuf_tensor(f"s_Wus{ch}_{p_}_{d}", [128, 512],
                                                     BF16))
                                  for d in range(ND)] for p_ in range(2)]
                        s_silu = [Ea_(nc.sbuf_tensor(f"s_silu{ch}_{k}", [128, TB], BF16))
                                  for k in range(2)]
                        ps_g = [Ea_(nc.psum_tensor(f"ps_g{ch}_{k}", [128, TB], F32))
                                for k in range(2)]
                        ps_u = [Ea_(nc.psum_tensor(f"ps_u{ch}_{k}", [128, TB], F32))
                                for k in range(2)]

                        @blk.sync
                        def _(sy, ch=ch, s_xnTg=s_xnTg, s_Wgs=s_Wgs, s_Wus=s_Wus):
                            if ch == 0:
                                sy.wait_ge(VV, vv_F_end)
                                sy.wait_ge(SWe, SWe_T)
                                sy.wait_ge(SWo, SWo_T)
                            if ch == 1:
                                sy.wait_ge(TT, tt_b[(0, 3, 3)])
                                sy.wait_ge(VV, vv_add[(0, 3, 3)])
                                sy.wait_ge(GS[0], 16 * 8)
                                sy.wait_ge(GS[1], 16 * 8)
                            for d in range(ND):
                                sy.dma_start_transpose(
                                    s_xnTg[d][:, :],
                                    xn_own.ap()[ch * TB:(ch + 1) * TB,
                                                d * 128:(d + 1) * 128]
                                ).then_inc(LX2, 16)
                            for hq in range(NHQ):
                                par = hq % 2
                                if hq >= 2:
                                    sy.wait_ge(TT, tt_gu[(ch, (hq - 2) * 4 + 3)])
                                for d in range(ND):
                                    sy.dma_start_transpose(
                                        s_Wgs[par][d][:, :],
                                        wg_s.ap()[hq * 512:(hq + 1) * 512,
                                                  d * 128:(d + 1) * 128]
                                    ).then_inc(LS[par], 16)
                                for d in range(ND):
                                    sy.dma_start_transpose(
                                        s_Wus[par][d][:, :],
                                        wu_s.ap()[hq * 512:(hq + 1) * 512,
                                                  d * 128:(d + 1) * 128]
                                    ).then_inc(LS[par], 16)

                        if ch == 1:
                            @blk.gpsimd
                            def _(g, ch=ch):
                                for i in (4, 5):
                                    pch, pdc = ldcis[i - 2]
                                    g.wait_ge(TT, tt_b[(pch, pdc, 3)])
                                    nch, ndc = ldcis[i]
                                    for a in range(ND):
                                        g.dma_start(s_Wo[ndc % 2][a][:, :],
                                                    Wo_in[a * 128:(a + 1) * 128,
                                                          ndc * 512:(ndc + 1) * 512]
                                                    ).then_inc(GO[ndc % 2], 16)

                        @blk.tensor
                        def _(t, ch=ch, s_xnTg=s_xnTg, s_Wgs=s_Wgs, s_Wus=s_Wus,
                              ps_g=ps_g, ps_u=ps_u):
                            if ch == 0:
                                t.wait_ge(VV, vv_F_end)
                            else:
                                t.wait_ge(VV, vv_add[(0, 3, 3)])
                            t.wait_ge(LX2, 16 * ND * (ch + 1))
                            for hq in range(NHQ):
                                par = hq % 2
                                lots = ch * (6 if par == 0 else 5) + hq // 2 + 1
                                t.wait_ge(LS[par], 16 * 32 * lots)
                                for hc4 in range(4):
                                    hc = hq * 4 + hc4
                                    if hc >= 2 or ch == 1:
                                        key = (ch, hc - 2) if hc >= 2 else (0, NHC - 2 + hc)
                                        t.wait_ge(VV, vv_gu[key])
                                    for d in range(ND):
                                        t.matmul(ps_g[hc % 2][:, :],
                                                 s_Wgs[par][d][:, hc4 * 128:(hc4 + 1) * 128],
                                                 s_xnTg[d][:, :], start=(d == 0),
                                                 stop=(d == ND - 1))
                                    for d in range(ND):
                                        mmu = t.matmul(
                                            ps_u[hc % 2][:, :],
                                            s_Wus[par][d][:, hc4 * 128:(hc4 + 1) * 128],
                                            s_xnTg[d][:, :], start=(d == 0),
                                            stop=(d == ND - 1))
                                    mmu.then_inc(TT, 1)

                        @blk.scalar
                        def _(sc, ch=ch, ps_g=ps_g, s_silu=s_silu):
                            for hc in range(NHC):
                                sc.wait_ge(TT, tt_gu[(ch, hc)])
                                sc.activation(s_silu[hc % 2][:, :], ps_g[hc % 2][:, :],
                                              AF.Silu).then_inc(AA, 1)

                        @blk.vector
                        def _(v, ch=ch, ps_u=ps_u, s_silu=s_silu):
                            for hc in range(NHC):
                                v.wait_ge(AA, aa_silu[(ch, hc)])
                                if ch == 1 and hc == 0:
                                    v.wait_ge(TT, tt_b[(0, 3, 3)])
                                v.tensor_mul(s_gu[hc][:, :], ps_u[hc % 2][:, :],
                                             s_silu[hc % 2][:, :]).then_inc(VV, 1)

                    with ExitStack() as bsc, nc.Block() as blk:
                        Eb_ = bsc.enter_context
                        s_WdT = [[Eb_(nc.sbuf_tensor(f"s_WdT{ch}_{p_}_{hc}", [128, 512],
                                                     BF16))
                                  for hc in range(NHC)] for p_ in range(2)]
                        s_xo = [Eb_(nc.sbuf_tensor(f"s_xo{ch}_{k}", [128, 512], F32))
                                for k in range(2)]
                        s_out = [Eb_(nc.sbuf_tensor(f"s_out{ch}_{k}", [128, 512], F32))
                                 for k in range(2)]
                        ps_out = [Eb_(nc.psum_tensor(f"ps_o{ch}_{k}", [128, 512], F32))
                                  for k in range(2)]

                        @blk.sync
                        def _(sy, ch=ch, s_WdT=s_WdT, s_xo=s_xo):
                            sy.wait_ge(TT, tt_gu[(ch, NHC - 1)])
                            sy.wait_ge(VV, vv_gu[(ch, NHC - 1)])
                            for dc in range(4):
                                ldci = 4 * ch + dc
                                if dc >= 2:
                                    pch, pdc = ldcis[ldci - 2]
                                    sy.wait_ge(TT, tt_b[(pch, pdc, 3)])
                                for hc in range(NHC):
                                    sy.dma_start_transpose(
                                        s_WdT[dc % 2][hc][:, :],
                                        wd_s.ap()[dc * 512:(dc + 1) * 512,
                                                  hc * 128:(hc + 1) * 128]
                                    ).then_inc(LDt[dc % 2], 16)
                                for t_ in range(4):
                                    lin = ldci * 4 + t_
                                    if lin >= 2:
                                        pld, pt = divmod(lin - 2, 4)
                                        pch, pdc = ldcis[pld]
                                        sy.wait_ge(VV, vv_add[(pch, pdc, pt)])
                                    sy.dma_start(
                                        s_xo[t_ % 2][:, :],
                                        x_own[(ch * 4 + t_) * 128:(ch * 4 + t_ + 1) * 128,
                                              dc * 512:(dc + 1) * 512]
                                    ).then_inc(LXO[t_ % 2], 16)

                        @blk.gpsimd
                        def _(g, ch=ch, s_out=s_out):
                            if ch == 0:
                                g.wait_ge(VV, vv_F_end)
                                for pre in range(2):
                                    pdc = ldcis[pre][1]
                                    for a in range(ND):
                                        g.dma_start(s_Wo[pdc % 2][a][:, :],
                                                    Wo_in[a * 128:(a + 1) * 128,
                                                          pdc * 512:(pdc + 1) * 512]
                                                    ).then_inc(GO[pdc % 2], 16)
                            for dc in range(4):
                                ldci = 4 * ch + dc
                                for t_ in range(4):
                                    g.wait_ge(VV, vv_add[(ch, dc, t_)])
                                    g.dma_start(
                                        out_own[(ch * 4 + t_) * 128:
                                                (ch * 4 + t_ + 1) * 128,
                                                dc * 512:(dc + 1) * 512],
                                        s_out[t_ % 2][:, :]).then_inc(GS[t_ % 2], 16)
                                if dc < 2 and (ldci + 2) // 4 == ch:
                                    nch, ndc = ldcis[ldci + 2]
                                    g.wait_ge(TT, tt_b[(ch, dc, 3)])
                                    for a in range(ND):
                                        g.dma_start(s_Wo[ndc % 2][a][:, :],
                                                    Wo_in[a * 128:(a + 1) * 128,
                                                          ndc * 512:(ndc + 1) * 512]
                                                    ).then_inc(GO[ndc % 2], 16)
                            if ch == 1:
                                g.wait_ge(GS[0], 16 * 16)
                                g.wait_ge(GS[1], 16 * 16)

                        @blk.tensor
                        def _(t, ch=ch, s_WdT=s_WdT, ps_out=ps_out):
                            t.wait_ge(VV, vv_gu[(ch, NHC - 1)])
                            for dc in range(4):
                                ldci = 4 * ch + dc
                                t.wait_ge(LDt[dc % 2], 16 * NHC * (ch * 2 + dc // 2 + 1))
                                t.wait_ge(GO[dc % 2], 16 * ND * (ch * 2 + dc // 2 + 1))
                                for t_ in range(4):
                                    if ldci * 4 + t_ >= 2:
                                        lin2 = ldci * 4 + t_ - 2
                                        t.wait_ge(GS[lin2 % 2], 16 * (lin2 // 2 + 1))
                                    for a in range(ND):
                                        t.matmul(ps_out[t_ % 2][:, :],
                                                 s_attnT[:,
                                                         a * 1024 + ch * 512 + t_ * 128:
                                                         a * 1024 + ch * 512 +
                                                         (t_ + 1) * 128],
                                                 s_Wo[dc % 2][a][:, :],
                                                 start=(a == 0), stop=False,
                                                 skip_group_check=True)
                                    for hc in range(NHC):
                                        mmd = t.matmul(ps_out[t_ % 2][:, :],
                                                       s_gu[hc][:, t_ * 128:(t_ + 1) * 128],
                                                       s_WdT[dc % 2][hc][:, :],
                                                       start=False, stop=(hc == NHC - 1),
                                                       skip_group_check=True)
                                    mmd.then_inc(TT, 1)

                        @blk.vector
                        def _(v, ch=ch, s_xo=s_xo, s_out=s_out, ps_out=ps_out):
                            for dc in range(4):
                                for t_ in range(4):
                                    lin = (4 * ch + dc) * 4 + t_
                                    v.wait_ge(TT, tt_b[(ch, dc, t_)])
                                    v.wait_ge(LXO[t_ % 2], 16 * (lin // 2 + 1))
                                    if lin >= 2:
                                        v.wait_ge(GS[t_ % 2], 16 * ((lin - 2) // 2 + 1))
                                    v.tensor_add(s_out[t_ % 2][:, :], s_xo[t_ % 2][:, :],
                                                 ps_out[t_ % 2][:, :]).then_inc(VV, 1)

        if _PH < 5:
            af2.close()
            with nc.Block() as blk:
                @blk.gpsimd
                def _(g):
                    g.wait_ge(VV, vv_F_end)
                    g.wait_ge(TT, tt_F_end)
                    g.wait_ge(AA, aa_F_end)
                    g.wait_ge(GV, gv_B)
                    g.wait_ge(SWe, SWe_T)
                    g.wait_ge(SWo, SWo_T)
                    g.wait_ge(SX, SX_T)
                    if _PH >= 2:
                        g.wait_ge(SKs, SK_T)
                        g.wait_ge(SVs, SV_T)
                        g.wait_ge(GXo, GXo_T)
                    if _PH >= 3:
                        g.wait_ge(SQ[0], SQ_T[0])
                        g.wait_ge(SQ[1], SQ_T[1])
                    nd = 0
                    if _PH == 1 and os.environ.get("KD") == "wu":
                        g.dma_start(out_own[0:1024, :],
                                    wu_s.ap()[0:1024, 0:2048]).then_inc(GS[0], 16)
                        nd = 1
                    elif _PH == 1 and os.environ.get("KD") == "wg":
                        g.dma_start(out_own[0:1024, :],
                                    wg_s.ap()[0:1024, 0:2048]).then_inc(GS[0], 16)
                        nd = 1
                    elif _PH == 1:
                        g.dma_start(out_own[0:512, :],
                                    xn_s.ap()[0:512, :]).then_inc(GS[0], 16)
                        g.wait_ge(GS[0], 16)
                        g.dma_start(out_own[512:768, :],
                                    wg_s.ap()[0:256, :]).then_inc(GS[0], 16)
                        g.wait_ge(GS[0], 32)
                        g.dma_start(out_own[768:896, :],
                                    wd_s.ap()[0:128, 0:2048]).then_inc(GS[0], 16)
                        g.wait_ge(GS[0], 48)
                        g.dma_start(out_own[896:1024, :],
                                    wu_s.ap()[0:128, 0:2048]).then_inc(GS[0], 16)
                        nd = 4
                    elif _PH == 2:
                        g.dma_start(out_own[0:1024, 0:512],
                                    k_s.ap()[0:1024, :]).then_inc(GS[0], 16)
                        g.wait_ge(GS[0], 16)
                        g.dma_start(out_own[0:1024, 512:1024],
                                    v_row_s.ap()[0:1024, :]).then_inc(GS[0], 16)
                        g.wait_ge(GS[0], 32)
                        nd = 2
                        if os.environ.get("KXO", "1") != "0":
                            g.dma_start(out_own[0:1024, 1024:2048],
                                        xn_own.ap()[0:1024, 0:1024]).then_inc(GS[0], 16)
                            nd = 3
                    elif _PH == 3:
                        g.dma_start(out_own[:, :], q_s.ap()[:, :]).then_inc(GS[0], 16)
                        nd = 1
                    elif _PH == 4:
                        g.dma_start(bass.AP(out_own.tensor, 0, [[16384, 128], [1, 16384]]),
                                    s_attnT[:, :]).then_inc(GS[0], 16)
                        nd = 1
                    g.wait_ge(GS[0], 16 * nd)

    return nc



def kernel(L0_attn_x, L0_attn_Wq, L0_attn_Wk, L0_attn_Wv, L0_attn_Wo,
           L0_attn_gq, L0_attn_gk, L0_ffn_wq_gate, L0_ffn_s_gate,
           L0_ffn_wq_up, L0_ffn_s_up, L0_ffn_wq_down, L0_ffn_s_down):
    x = np.ascontiguousarray(np.asarray(L0_attn_x, np.float32).reshape(S, D))
    Wq = np.ascontiguousarray(np.asarray(L0_attn_Wq, np.float32))
    Wk = np.ascontiguousarray(np.asarray(L0_attn_Wk, np.float32))
    Wv = np.ascontiguousarray(np.asarray(L0_attn_Wv, np.float32))
    Wo = np.ascontiguousarray(np.asarray(L0_attn_Wo, np.float32))
    gq = np.ascontiguousarray(np.asarray(L0_attn_gq, np.float32).reshape(HD, 1))
    gk = np.ascontiguousarray(np.asarray(L0_attn_gk, np.float32).reshape(HD, 1))
    wg = np.ascontiguousarray(np.asarray(L0_ffn_wq_gate, np.int32).astype(np.uint8))
    wu = np.ascontiguousarray(np.asarray(L0_ffn_wq_up, np.int32).astype(np.uint8))
    wd = np.ascontiguousarray(np.asarray(L0_ffn_wq_down, np.int32).astype(np.uint8))
    sg = np.ascontiguousarray(np.asarray(L0_ffn_s_gate, np.float32).reshape(NHC, 128).T)
    su = np.ascontiguousarray(np.asarray(L0_ffn_s_up, np.float32).reshape(NHC, 128).T)
    sd = np.ascontiguousarray(np.asarray(L0_ffn_s_down, np.float32).reshape(ND, 128).T)

    masks = np.zeros((128, 4 * TB), np.float32)
    for j in range(4):
        for kk in range(128):
            masks[kk, j * TB:(j + 1) * TB] = (j * 128 + kk <= np.arange(TB))
    masks = masks.astype(ml_dtypes.bfloat16)
    negsel = np.zeros((16, 16 * 128), np.float32)
    for h in range(16):
        negsel[h, h * 128:(h + 1) * 128] = -1.0
    negsel = negsel.astype(ml_dtypes.bfloat16)
    identb = np.eye(128, dtype=np.float32).astype(ml_dtypes.bfloat16)

    nc = build_nc()
    in_maps = []
    for core in range(8):
        J_lo, J_hi = core, 15 - core
        tbl = np.zeros((1, 16), np.int32)
        tbl[0, TBL_KD + 0] = J_lo * TB * (NKV * HD)
        tbl[0, TBL_KD + 1] = J_hi * TB * (NKV * HD)
        tbl[0, TBL_XO + 0] = J_lo * TB * D
        tbl[0, TBL_XO + 1] = J_hi * TB * D
        usel = np.zeros((33, NH * NSLOT), np.float32)
        for h in range(NH):
            for p_ in range(NSLOT):
                col = h * NSLOT + p_
                if p_ < 8:
                    real = (p_ == 0) or (p_ - 1 < J_lo)
                    seg = 0
                else:
                    real = (p_ == 8) or (p_ - 9 < J_hi)
                    seg = 1
                if real:
                    usel[seg * 16 + h, col] = 1.0
                else:
                    usel[32, col] = -1.0
        usel = usel.astype(ml_dtypes.bfloat16)
        x_own_c = np.ascontiguousarray(
            np.concatenate([x[J_lo * TB:(J_lo + 1) * TB],
                            x[J_hi * TB:(J_hi + 1) * TB]], axis=0))
        in_maps.append(dict(
            x_full=x, x_own=x_own_c, Wq_in=Wq, Wk_in=Wk, Wv_in=Wv, Wo_in=Wo,
            gq_col=gq, gk_col=gk, wg_i=wg, wu_i=wu, wd_i=wd,
            sg_in=sg, su_in=su, sd_in=sd, masks_in=masks, negsel_in=negsel,
            identb_in=identb, usel_in=usel, tbl_in=tbl,
        ))

    res = run_bass_kernel_spmd(nc, in_maps, core_ids=list(range(8)))
    out = np.zeros((S, D), np.float32)
    for core in range(8):
        J_lo, J_hi = core, 15 - core
        oo = np.asarray(res.results[core]["out_own"], np.float32)
        out[J_lo * TB:(J_lo + 1) * TB] = oo[0:TB]
        out[J_hi * TB:(J_hi + 1) * TB] = oo[TB:2 * TB]
    return out.reshape(1, S, D)
